# version 05
# baseline (speedup 1.0000x reference)
import numpy as np
import ml_dtypes
import concourse.bass as bass
import concourse.mybir as mybir
from concourse.bass_utils import run_bass_kernel_spmd

F32 = mybir.dt.float32
BF16 = mybir.dt.bfloat16
AF = mybir.ActivationFunctionType
ALU = mybir.AluOpType
AX = mybir.AxisListType
NPBF = ml_dtypes.bfloat16

S = 4096
D = 1024
NSL = 8
SL = 512
NT = 32
C = 128
HD = 256
DNH = 4
AH = 16
ADH = 64
NE = 8

ENGINES = ("pe", "act", "dve", "gp", "sp")


class Plan:

    def __init__(self):
        self.steps = []
        self.lastw = {}
        self.readers = {}
        self.region_of = {}
        self.reg_pending = {}
        self.reg_inited = {}

    def set_region(self, res, region):
        self.region_of[res] = region

    @staticmethod
    def _base(res):
        return res if isinstance(res, str) else res[0]

    def reset_region(self, region):
        pend = set()
        for res in list(self.lastw.keys()) + list(self.readers.keys()):
            if self.region_of.get(self._base(res)) != region:
                continue
            if res in self.lastw:
                pend.add(self.lastw.pop(res))
            pend |= self.readers.pop(res, set())
        self.reg_pending[region] = pend
        self.reg_inited[region] = set()

    def step(self, engine, emit, reads=(), writes=(), dma=False):
        idx = len(self.steps)
        deps = set()
        for r in reads:
            if r in self.lastw:
                deps.add(self.lastw[r])
        for w in writes:
            if w in self.lastw:
                deps.add(self.lastw[w])
            deps |= self.readers.get(w, set())
            reg = self.region_of.get(self._base(w))
            if reg is not None and self.reg_pending.get(reg):
                if w not in self.reg_inited[reg]:
                    deps |= self.reg_pending[reg]
                    self.reg_inited[reg].add(w)
        for r in reads:
            self.readers.setdefault(r, set()).add(idx)
        for w in writes:
            self.lastw[w] = idx
            self.readers[w] = {idx}
        deps.discard(idx)
        self.steps.append(dict(engine=engine, emit=emit, deps=deps, dma=dma,
                               need_sem=False))
        return idx

    SELF_SYNC = ("dve", "act", "gp")

    def finalize(self):
        steps = self.steps
        for s in steps:
            for d in s["deps"]:
                p = steps[d]
                if (p["engine"] != s["engine"] or p["dma"]
                        or p["engine"] in self.SELF_SYNC):
                    p["need_sem"] = True
        for s in steps:
            if s["dma"]:
                s["need_sem"] = True
        counters = {}
        for s in steps:
            if s["need_sem"]:
                key = s["engine"] + ("_dma" if s["dma"] else "")
                counters[key] = counters.get(key, 0) + (16 if s["dma"] else 1)
                s["semkey"] = key
                s["semval"] = counters[key]
        wm = {}
        for s in steps:
            need = {}
            for d in s["deps"]:
                p = steps[d]
                if (p["engine"] == s["engine"] and not p["dma"]
                        and p["engine"] not in self.SELF_SYNC):
                    continue
                need[p["semkey"]] = max(need.get(p["semkey"], 0), p["semval"])
            waits = []
            for key, val in sorted(need.items()):
                if wm.get((s["engine"], key), 0) >= val:
                    continue
                wm[(s["engine"], key)] = val
                waits.append((key, val))
            s["waits"] = waits
        return counters


def host_consts():
    c = {}
    c["identb"] = np.eye(128, dtype=NPBF)
    c["identf"] = np.eye(128, dtype=np.float32)
    c["onesf"] = np.ones((1, 128), np.float32)
    c["onescolb"] = np.ones((128, 1), NPBF)
    c["uincf"] = np.triu(np.ones((128, 128), np.float32))
    maskadd = np.where(np.arange(128)[:, None] <= np.arange(128)[None, :],
                       0.0, -1e30).astype(np.float32)
    c["maskaddTf"] = np.ascontiguousarray(maskadd.T)
    c["offdiagf"] = (1.0 - np.eye(128)).astype(np.float32)
    e127 = np.zeros((128, 128), np.float32)
    e127[127, :] = 1.0
    c["e127f"] = e127
    es0 = np.zeros((2, 128), np.float32); es0[0, :] = 1.0
    es1 = np.zeros((2, 128), np.float32); es1[1, :] = 1.0
    c["esel0f"], c["esel1f"] = es0, es1
    hs = np.zeros((128, 2), NPBF); hs[:64, 0] = 1; hs[64:, 1] = 1
    c["headsel"] = hs
    psw = np.zeros((128, 128), NPBF)
    for a in range(2):
        for dd in range(32):
            psw[a * 64 + dd + 32, a * 64 + dd] = -1.0
            psw[a * 64 + dd, a * 64 + dd + 32] = 1.0
    c["pswapT"] = psw
    inv = 1.0 / 10000.0 ** (np.arange(0, ADH, 2, dtype=np.float64) / ADH)
    ang = np.outer(inv, np.arange(S, dtype=np.float64))
    ang2 = np.concatenate([ang, ang], axis=0)
    cos = np.cos(ang2).astype(np.float32)
    sin = np.sin(ang2).astype(np.float32)
    c["copt"] = np.ascontiguousarray(np.concatenate([cos, cos], axis=0))
    c["sipt"] = np.ascontiguousarray(np.concatenate([sin, sin], axis=0))
    return c


CONSTS = host_consts()


def build_program(debug=(), phases=("p0","p1a","p1b","p2p","p2a","p3")):
    nc = bass.Bass("TRN2", target_bir_lowering=False, debug=False)
    P = Plan()

    def inp(name, shape, dt=F32):
        return nc.dram_tensor(name, list(shape), dt, kind="ExternalInput").ap()

    x_h = inp("x", (S, D))
    win_h = inp("win", (D, 3 * D))
    wog_h = inp("wog", (D, D))
    wab_h = inp("wab", (128, 64))
    bog_h = inp("bog", (128, 8))
    bab_h = inp("bab", (1, 8))
    wconv_h = inp("wconv", (128, 96))
    wout_h = inp("wout", (D, D))
    bout_h = inp("bout", (128, 8))
    ramp_h = inp("ramp", (128, 8))
    qwT_h = inp("qwT", (D, D))
    kwT_h = inp("kwT", (D, D))
    vwT_h = inp("vwT", (D, D))
    qb_h = inp("qb", (128, 8))
    kb_h = inp("kb", (128, 8))
    vb_h = inp("vbrow", (1, D))
    gwT_h = inp("gwT", (128, 64))
    gb_h = inp("gbrow", (1, 8))
    ewT_h = inp("ewT", (NE, D, D))
    ebT_h = inp("ebT", (128, 64))
    fwT_h = inp("fwT", (D, D))
    fb_h = inp("fbrow", (1, D))
    con_h = {k: inp(k, CONSTS[k].shape,
                    BF16 if CONSTS[k].dtype == NPBF else F32)
             for k in CONSTS}
    out_h = nc.dram_tensor("out", [S, D], F32, kind="ExternalOutput").ap()

    def scr(name, shape, dt):
        return nc.dram_tensor(name, list(shape), dt, kind="Internal").ap()

    xn32_d = scr("xn32", (S, D), F32)
    base_d = scr("base", (S, D), F32)
    oglog_d = scr("oglog", (D, S), F32)
    qc_d = scr("qc", (D, S), BF16)
    kc_d = scr("kc", (D, S), BF16)
    vc_d = scr("vc", (D, S), BF16)
    qr_d = scr("qr", (D, S), BF16)
    kr_d = scr("kr", (D, S), BF16)
    vt_d = scr("vt", (S, D), BF16)
    ao_d = scr("ao", (D, S), BF16)

    dbg_outs = {}
    def dbgout(name, shape, dt):
        if name in debug:
            dbg_outs[name] = nc.dram_tensor("dbg_" + name, list(shape), dt,
                                            kind="ExternalOutput").ap()
    dbgout("xn32", (S, D), F32)
    dbgout("qc", (D, S), F32)
    dbgout("kc", (D, S), F32)
    dbgout("vc", (D, S), F32)
    dbgout("oglog", (D, S), F32)
    dbgout("absb", (128, NT * 8), F32)
    dbgout("base", (S, D), F32)
    dbgout("qr", (D, S), F32)
    dbgout("kr", (D, S), F32)
    dbgout("vt", (S, D), F32)
    dbgout("ao", (D, S), F32)

    sb = {}

    def sbuf(name, shape, dt):
        t = nc.alloc_sbuf_tensor("s_" + name, list(shape), dt)
        sb[name] = t
        return t

    identb = sbuf("identb", (128, 128), BF16)
    identf = sbuf("identf", (128, 128), F32)
    onesf = sbuf("onesf", (1, 128), F32)
    onescolb = sbuf("onescolb", (128, 1), BF16)
    uincf = sbuf("uincf", (128, 128), F32)
    maskaddT = sbuf("maskaddT", (128, 128), F32)
    offdiag = sbuf("offdiag", (128, 128), F32)
    e127 = sbuf("e127", (128, 128), F32)
    esel0 = sbuf("esel0", (2, 128), F32)
    esel1 = sbuf("esel1", (2, 128), F32)
    headsel = sbuf("headsel", (128, 2), BF16)
    pswapT = sbuf("pswapT", (128, 128), BF16)
    bogc = sbuf("bogc", (128, 8), F32)
    babr = sbuf("babr", (1, 8), F32)
    boutc = sbuf("boutc", (128, 8), F32)
    rampc = sbuf("rampc", (128, 8), F32)
    rampt = sbuf("rampt", (128, 8), F32)
    qbc = sbuf("qbc", (128, 8), F32)
    kbc = sbuf("kbc", (128, 8), F32)
    vbrow = sbuf("vbrow", (1, D), F32)
    vbb = sbuf("vbb", (128, D), F32)
    gbrow = sbuf("gbrow", (1, 8), F32)
    gbb = sbuf("gbb", (128, 8), F32)
    babb = sbuf("babb", (128, 8), F32)
    fbrow = sbuf("fbrow", (1, D), F32)
    fbb = sbuf("fbb", (128, D), F32)
    gwTs = sbuf("gwTs", (128, 64), BF16)
    ebTs = sbuf("ebTs", (128, 64), BF16)
    wabsb = sbuf("wabsb", (128, 64), BF16)
    wconv = sbuf("wconv", (128, 96), F32)
    ebf = sbuf("ebf", (8, D), BF16)
    tails = sbuf("tails", (128, 24 * 3), BF16)
    st32 = sbuf("st32", (128, DNH * 2 * HD), F32)
    stbf = sbuf("stbf", (128, DNH * 2 * HD), BF16)
    absb = sbuf("absb", (128, NT * 8), F32)
    lnst = sbuf("lnst", (128, 8), F32)
    maccq = sbuf("maccq", (2, 8), F32)
    cpe5 = sbuf("cpe5", (128, 1), F32)
    cpe6 = sbuf("cpe6", (128, 1), F32)
    nc.const_aps.register(F32, 1e-5, cpe5[:])
    nc.const_aps.register(F32, 1e-6, cpe6[:])
    macck = sbuf("macck", (2, 8), F32)
    mtmp = sbuf("mtmp", (2, 1), F32)

    pool_base = (nc.sbuf_base + 1023) // 1024 * 1024
    pool_avail = nc.sbuf_bytes_remaining - (pool_base - nc.sbuf_base)
    pool = {}

    avail_pp = pool_avail if pool_avail <= 256 * 1024 else pool_avail // 128

    def palloc(name, shape, dt, off_kb, region):
        nbytes = int(np.prod(shape[1:])) * (2 if dt == BF16 else 4)
        off = pool_base + off_kb * 1024
        assert off_kb * 1024 + nbytes <= avail_pp - 1024, \
            f"pool overflow {name}: off_kb={off_kb} nbytes={nbytes} avail={avail_pp}"
        t = nc.alloc_sbuf_tensor_at("s_" + name, list(shape), dt, offset=off)
        pool[name] = t
        P.set_region(name, region)
        return t

    xnfm = palloc("xnfm", (128, 8 * S), BF16, 0, "RA")
    xt = [palloc(f"xt{i}", (128, D), F32, 64 + 4 * i, "RB") for i in range(2)]
    xbf = [palloc(f"xbf{i}", (128, D), BF16, 72 + 2 * i, "RB") for i in range(2)]
    x2t = palloc("x2t", (128, D), F32, 76, "RB")
    xn32t = [palloc(f"xn32t{i}", (128, D), F32, 80 + 4 * i, "RB") for i in range(2)]
    OFF = dict(
        winsb=64,
        wogsb=112,
        praw0=128, praw1=130, praw2=132, praw3=134,
        convt=136, ogt0=138, ogt1=140,
        qnsl=64, knsl=72, vnsl=80,
        woutsb=88,
        sqt=157, rrow=158,
        ogsl=112,
        qwTs=64, kwTs=80, vwTs=96,
        copt=112, sipt=128,
        qpt0=144, qpt1=145,
        ropet1=146, ropet2=148,
        ropeo=150, vtokt=152, sqa=154,
        khd=64, qhd=72, vslab=80,
        ptl0=88, ptl1=90, zbs=92, aot=94, zrow=98,
        negmh=95, mqcol=96, mkcol=97,
        expd0=0, expd1=16,
        moeacc=32,
        moebf=48, moefm=50, baset=52,
        outt0=56, outt1=60,
        gatesf=112, gtmp=113, gmsk=114,
        gcol=115, gcol2=116, wtok4=117, wfm4=118,
        aofm=64, fwTs=72,
    )

    CH_OFF = [136, 148]

    def chunk_tiles(p):
        base = CH_OFF[p]
        d = {}
        ofs = base * 1024

        def ca(name, shape, dt):
            nonlocal ofs
            nbytes = int(np.prod(shape[1:])) * (2 if dt == BF16 else 4)
            t = nc.alloc_sbuf_tensor_at(f"s_{name}{p}", list(shape), dt,
                                        offset=pool_base + ofs)
            ofs += (nbytes + 31) // 32 * 32
            pool[f"{name}{p}"] = t
            P.set_region(f"{name}{p}", "RC")
            assert ofs <= (base + 12) * 1024, f"chunk tile overflow {name}"
            return t

        d["gsb"] = ca("gsb", (128, 4), F32)
        d["biasc"] = ca("biasc", (128, 4), F32)
        d["ctc"] = ca("ctc", (128, 4), F32)
        d["clast4"] = ca("clast4", (128, 4), F32)
        d["grow"] = ca("grow", (1, 128), F32)
        d["dincl"] = ca("dincl", (128, 128), F32)
        d["dstr"] = ca("dstr", (128, 128), F32)
        d["at"] = ca("at", (128, 128), BF16)
        d["wt"] = ca("wt", (128, 128), BF16)
        d["an"] = ca("an", (128, 128), BF16)
        d["mc"] = ca("mc", (128, 256), F32)
        d["rr"] = ca("rr", (128, 256), BF16)
        d["u0"] = ca("u0", (128, 256), BF16)
        d["u1"] = ca("u1", (128, 256), BF16)
        d["pp0"] = ca("pp0", (128, 128), BF16)
        d["pp1"] = ca("pp1", (128, 128), BF16)
        d["bp0"] = ca("bp0", (128, 128), BF16)
        d["bp1"] = ca("bp1", (128, 128), BF16)
        d["ktok"] = ca("ktok", (128, 256), BF16)
        d["readf"] = ca("readf", (128, 256), F32)
        d["sqr"] = ca("sqr", (128, 256), F32)
        d["tiny"] = ca("tiny", (128, 4), F32)
        return d

    def rd_tiles(p):
        base = (104 + 4 * p) * 1024
        t1 = nc.alloc_sbuf_tensor_at(f"s_readn{p}", [128, D], BF16,
                                     offset=pool_base + base)
        t2 = nc.alloc_sbuf_tensor_at(f"s_readfm{p}", [128, D], BF16,
                                     offset=pool_base + base + 2048)
        t3 = nc.alloc_sbuf_tensor_at(f"s_dnbase{p}", [128, D], F32,
                                     offset=pool_base + (128 + 4 * p) * 1024)
        for t, n, rg in ((t1, f"readn{p}", "RB"), (t2, f"readfm{p}", "RB"),
                         (t3, f"dnbase{p}", "RC")):
            pool[n] = t
            P.set_region(n, rg)
        return t1, t2, t3

    pbw = [nc.alloc_psum_tensor(f"pw{i}", [128, 1024], F32) for i in range(2)]

    class BankView:
        def __init__(self, t, coff):
            self.t, self.coff = t, coff

        def __getitem__(self, idx):
            r, c = idx
            cs = c.start or 0
            ce = c.stop if c.stop is not None else 512
            return self.t[r, self.coff + cs: self.coff + ce]

    pbank = [BankView(pbw[0], 0), BankView(pbw[0], 512),
             BankView(pbw[1], 0), BankView(pbw[1], 512),
             nc.alloc_psum_tensor("pb4", [128, 512], F32),
             nc.alloc_psum_tensor("pb5", [128, 512], F32)]
    pbtr = [nc.alloc_psum_tensor(f"pt{i}", [128, 1024], BF16) for i in range(2)]

    def st(engine, emit, reads=(), writes=(), dma=False):
        return P.step(engine, emit, reads=reads, writes=writes, dma=dma)

    def dmac(engine, dst, src, reads, writes):
        return P.step(engine, (lambda e, d_=dst, s_=src: e.dma_start(d_, s_)),
                      reads=reads, writes=writes, dma=True)

    def loads_init():
        for t, h, key in ((identb, con_h["identb"], "identb"),
                          (identf, con_h["identf"], "identf"),
                          (onesf, con_h["onesf"], "onesf"),
                          (onescolb, con_h["onescolb"], "onescolb"),
                          (uincf, con_h["uincf"], "uincf"),
                          (maskaddT, con_h["maskaddTf"], "maskaddT"),
                          (offdiag, con_h["offdiagf"], "offdiag"),
                          (e127, con_h["e127f"], "e127"),
                          (esel0, con_h["esel0f"], "esel0"),
                          (esel1, con_h["esel1f"], "esel1"),
                          (headsel, con_h["headsel"], "headsel"),
                          (pswapT, con_h["pswapT"], "pswapT"),
                          (bogc, bog_h, "bogc"), (babr, bab_h, "babr"),
                          (boutc, bout_h, "boutc"),
                          (qbc, qb_h, "qbc"), (kbc, kb_h, "kbc"),
                          (vbrow, vb_h, "vbrow"), (gbrow, gb_h, "gbrow"),
                          (fbrow, fb_h, "fbrow"),
                          (rampt, ramp_h, "rampl")):
            dmac("sp", t[:], h, reads=(), writes=(key,))
        dmac("sp", wconv[:], wconv_h, reads=(), writes=("wconv",))
        dmac("gp", gwTs[:], gwT_h, reads=(), writes=("gwTs",))
        dmac("gp", ebTs[:], ebT_h, reads=(), writes=("ebTs",))
        dmac("gp", wabsb[:], wab_h, reads=(), writes=("wabsb",))
        st("act", lambda e: e.activation(out=rampc[:], in_=rampt[:], func=AF.Tanh,
                                         scale=0.5),
           reads=("rampl",), writes=("rampc",))
        st("dve", lambda e: e.tensor_scalar(rampc[:], rampc[:], 0.5, 0.5,
                                            ALU.mult, ALU.add),
           reads=("rampc",), writes=("rampc",))
        st("dve", lambda e: e.memset(cpe5[:], 1e-5), reads=(), writes=("constpool",))
        st("dve", lambda e: e.memset(cpe6[:], 1e-6), reads=(), writes=("constpool",))
        st("dve", lambda e: e.memset(maccq[:], 0.0), reads=(), writes=("maccq",))
        st("dve", lambda e: e.memset(macck[:], 0.0), reads=(), writes=("macck",))
        st("dve", lambda e: e.memset(tails[:], 0.0), reads=(), writes=("tails",))

        def bcast_row(dst, row, rowkey, dstkey, n, bank):
            for j in range(0, n, 512):
                w = min(512, n - j)
                st("pe", (lambda e, jj=j, ww=w, rr=row, bb=bank:
                          e.matmul(pbank[bb][:, 0:ww], onesf[:],
                                   rr[0:1, jj:jj + ww], start=True, stop=True)),
                   reads=("onesf", rowkey), writes=(f"pb{bank}",))
                st("dve", (lambda e, dd=dst, jj=j, ww=w, bb=bank:
                           e.tensor_copy(dd[:, jj:jj + ww], pbank[bb][:, 0:ww])),
                   reads=(f"pb{bank}",), writes=(dstkey,))
        bcast_row(vbb, vbrow, "vbrow", "vbb", D, 0)
        bcast_row(fbb, fbrow, "fbrow", "fbb", D, 1)
        bcast_row(gbb, gbrow, "gbrow", "gbb", 8, 2)
        bcast_row(babb, babr, "babr", "babb", 8, 3)
        st("dve", lambda e: e.memset(st32[:], 0.0), reads=(),
           writes=tuple(("st32", h, kc) for h in range(4) for kc in range(2)))
        st("dve", lambda e: e.memset(stbf[:], 0.0), reads=(),
           writes=tuple(("stbf", h, kc) for h in range(4) for kc in range(2)))

    def load_wchunks(dst, dstkey, src_h, n_ic, width):
        for ic in range(n_ic):
            dmac("gp", dst[:, ic * width:(ic + 1) * width],
                 src_h[ic * 128:(ic + 1) * 128, :],
                 reads=(), writes=((dstkey, ic),))

    def phase0():
        for i in range(NT):
            p = i % 2
            xtile, xbtile, xntile = xt[p], xbf[p], xn32t[p]
            kx, kxb, kxn = f"xt{p}", f"xbf{p}", f"xn32t{p}"
            dmac("sp", xtile[:], x_h[i * 128:(i + 1) * 128, :],
                 reads=(), writes=(kx,))
            st("dve", (lambda e, xx=xtile: e.tensor_reduce(
                lnst[:, 0:1], xx[:], AX.X, ALU.add)),
               reads=(kx,), writes=("lnst0",))
            st("dve", lambda e: e.tensor_scalar(
                lnst[:, 1:2], lnst[:, 0:1], -1.0 / D, None, ALU.mult),
               reads=("lnst0",), writes=("lnst1",))
            st("dve", (lambda e, xx=xtile: e.tensor_mul(x2t[:], xx[:], xx[:])),
               reads=(kx,), writes=("x2t",))
            st("dve", lambda e: e.tensor_reduce(
                lnst[:, 2:3], x2t[:], AX.X, ALU.add),
               reads=("x2t",), writes=("lnst2",))
            st("dve", lambda e: e.tensor_scalar(
                lnst[:, 3:4], lnst[:, 2:3], 1.0 / D, None, ALU.mult),
               reads=("lnst2",), writes=("lnst3",))
            st("dve", lambda e: e.tensor_mul(
                lnst[:, 4:5], lnst[:, 1:2], lnst[:, 1:2]),
               reads=("lnst1",), writes=("lnst4",))
            st("dve", lambda e: e.tensor_sub(
                lnst[:, 5:6], lnst[:, 3:4], lnst[:, 4:5]),
               reads=("lnst3", "lnst4"), writes=("lnst5",))
            st("act", lambda e: e.activation(
                out=lnst[:, 6:7], in_=lnst[:, 5:6], func=AF.Ln, bias=1e-5),
               reads=("lnst5", "constpool"), writes=("lnst6",))
            st("act", lambda e: e.activation(
                out=lnst[:, 7:8], in_=lnst[:, 6:7], func=AF.Exp, scale=-0.5),
               reads=("lnst6",), writes=("lnst7",))
            st("dve", (lambda e, xx=xtile, xo=xntile: e.tensor_scalar(
                xo[:], xx[:], lnst[:, 1:2], lnst[:, 7:8], ALU.add, ALU.mult)),
               reads=(kx, "lnst1", "lnst7"), writes=(kxn,))
            dmac("gp", xn32_d[i * 128:(i + 1) * 128, :], xntile[:],
                 reads=(kxn,), writes=(("xn32d", i),))
            st("dve", (lambda e, xx=xtile, xo=xntile: e.tensor_add(
                x2t[:], xx[:], xo[:])),
               reads=(kx, kxn), writes=("x2t",))
            dmac("gp", base_d[i * 128:(i + 1) * 128, :], x2t[:],
                 reads=("x2t",), writes=(("based", i),))
            st("dve", (lambda e, xo=xntile, xb=xbtile: e.tensor_copy(
                xb[:], xo[:])),
               reads=(kxn,), writes=(kxb,))
            for ic in range(8):
                bank = ic % 2
                st("pe", (lambda e, xb=xbtile, icc=ic, bb=bank: e.transpose(
                    pbtr[bb][:, 0:128], xb[:, icc * 128:(icc + 1) * 128],
                    identb[:])),
                   reads=(kxb, "identb"), writes=(f"pt{bank}",))
                st("dve", (lambda e, icc=ic, ii=i, bb=bank: e.tensor_copy(
                    xnfm[:, icc * S + ii * 128: icc * S + (ii + 1) * 128],
                    pbtr[bb][:, 0:128])),
                   reads=(f"pt{bank}",), writes=(("xnfm", ic, i),))

    def phase1a(g):
        winsb, wogsb, praw = g["winsb"], g["wogsb"], g["praw"]
        load_wchunks(winsb, "winsb", win_h, 8, 3 * D)
        load_wchunks(wogsb, "wogsb", wog_h, 8, D)
        for s_ in range(NSL):
            t0 = s_ * SL
            xn_reads = tuple(("xnfm", ic, t) for ic in range(8)
                             for t in range(s_ * 4, s_ * 4 + 4))
            for oc in range(24):
                qkv, occ = oc // 8, oc % 8
                bank = oc % 2
                for ic in range(8):
                    st("pe", (lambda e, oo=oc, ii=ic, bb=bank, tt=t0:
                              e.matmul(pbank[bb][:, :],
                                       winsb[:, ii * 3 * D + oo * 128:
                                             ii * 3 * D + oo * 128 + 128],
                                       xnfm[:, ii * S + tt: ii * S + tt + SL],
                                       start=(ii == 0), stop=(ii == 7))),
                       reads=(("winsb", ic),) + xn_reads,
                       writes=(f"pb{bank}",))
                pr = praw[oc % 4]
                kpr = f"praw{oc % 4}"
                st("dve", (lambda e, pr_=pr, oo=oc: e.tensor_copy(
                    pr_[:, 0:3], tails[:, oo * 3:oo * 3 + 3])),
                   reads=(("tails", oc),), writes=((kpr, "head"),))
                st("act", (lambda e, pr_=pr, bb=bank: e.activation(
                    out=pr_[:, 3:SL + 3], in_=pbank[bb][:, :],
                    func=AF.Identity)),
                   reads=(f"pb{bank}",), writes=(kpr,))
                st("dve", (lambda e, pr_=pr, oo=oc: e.tensor_copy(
                    tails[:, oo * 3:oo * 3 + 3], pr_[:, SL:SL + 3])),
                   reads=(kpr,), writes=(("tails", oc),))
                wc0 = qkv * 32 + occ * 4
                st("dve", (lambda e, pr_=pr, w0=wc0: e.tensor_scalar(
                    convt[:], pr_[:, 0:SL], wconv[:, w0:w0 + 1], None,
                    ALU.mult)),
                   reads=(kpr, (kpr, "head"), "wconv"), writes=("convt",))
                for tap in range(1, 4):
                    st("dve", (lambda e, pr_=pr, wc=wc0 + tap, tp=tap:
                               e.scalar_tensor_tensor(
                                   convt[:], pr_[:, tp:SL + tp],
                                   wconv[:, wc:wc + 1], convt[:],
                                   ALU.mult, ALU.add)),
                       reads=(kpr, (kpr, "head"), "wconv", "convt"),
                       writes=("convt",))
                st("act", lambda e: e.activation(out=convt[:], in_=convt[:],
                                                 func=AF.Silu),
                   reads=("convt",), writes=("convt",))
                dst_d = (qc_d, kc_d, vc_d)[qkv]
                dmac("gp", dst_d[occ * 128:(occ + 1) * 128, t0:t0 + SL],
                     convt[:], reads=("convt",),
                     writes=((("qc", "kc", "vc")[qkv], occ, s_),))
            for oc in range(8):
                bank = 2 + (oc % 2)
                for ic in range(8):
                    st("pe", (lambda e, oo=oc, ii=ic, bb=bank, tt=t0:
                              e.matmul(pbank[bb][:, :],
                                       wogsb[:, ii * D + oo * 128:
                                             ii * D + oo * 128 + 128],
                                       xnfm[:, ii * S + tt: ii * S + tt + SL],
                                       start=(ii == 0), stop=(ii == 7))),
                       reads=(("wogsb", ic),) + xn_reads,
                       writes=(f"pb{bank}",))
                p = oc % 2
                ogt = g["ogt"][p]
                st("act", (lambda e, oo=oc, bb=bank, og=ogt: e.activation(
                    out=og[:], in_=pbank[bb][:, :], func=AF.Identity,
                    bias=bogc[:, oo:oo + 1])),
                   reads=(f"pb{bank}", "bogc"), writes=(f"ogt{p}",))
                dmac("gp", oglog_d[oc * 128:(oc + 1) * 128, t0:t0 + SL],
                     ogt[:], reads=(f"ogt{p}",), writes=(("oglog", oc, s_),))
            for tt in range(4):
                ti = s_ * 4 + tt
                bank = 4
                for ic in range(8):
                    st("pe", (lambda e, ii=ic, tt_=ti, bb=bank: e.matmul(
                        pbank[bb][:, 0:8],
                        xnfm[:, ii * S + tt_ * 128: ii * S + tt_ * 128 + 128],
                        wabsb[:, ii * 8:(ii + 1) * 8],
                        start=(ii == 0), stop=(ii == 7))),
                       reads=xn_reads + ("wabsb",), writes=(f"pb{bank}",))
                st("dve", (lambda e, tt_=ti, bb=bank: e.tensor_add(
                    absb[:, tt_ * 8:(tt_ + 1) * 8], pbank[bb][:, 0:8],
                    babb[:, 0:8])),
                   reads=(f"pb{bank}", "babb"), writes=(("absb", ti),))

    def phase1b(g):
        woutsb = g["woutsb"]
        qnsl, knsl, vnsl, ogsl = g["qnsl"], g["knsl"], g["vnsl"], g["ogsl"]
        sqtl, rrow = g["sqt"], g["rrow"]
        load_wchunks(woutsb, "woutsb", wout_h, 8, D)
        for s_ in range(NSL):
            t0 = s_ * SL
            for oc in range(8):
                dmac("sp", qnsl[:, oc * SL:(oc + 1) * SL],
                     qc_d[oc * 128:(oc + 1) * 128, t0:t0 + SL],
                     reads=(("qc", oc, s_),), writes=(("qnsl", oc),))
                dmac("sp", knsl[:, oc * SL:(oc + 1) * SL],
                     kc_d[oc * 128:(oc + 1) * 128, t0:t0 + SL],
                     reads=(("kc", oc, s_),), writes=(("knsl", oc),))
                dmac("sp", vnsl[:, oc * SL:(oc + 1) * SL],
                     vc_d[oc * 128:(oc + 1) * 128, t0:t0 + SL],
                     reads=(("vc", oc, s_),), writes=(("vnsl", oc),))
                dmac("sp", ogsl[:, oc * SL:(oc + 1) * SL],
                     oglog_d[oc * 128:(oc + 1) * 128, t0:t0 + SL],
                     reads=(("oglog", oc, s_),), writes=(("ogsl", oc),))
            for which, tile in (("q", qnsl), ("k", knsl)):
                kk = "qnsl" if which == "q" else "knsl"
                for h in range(4):
                    bank = h
                    for half in range(2):
                        oc = 2 * h + half
                        st("dve", (lambda e, tl=tile, oo=oc: e.tensor_mul(
                            sqtl[:], tl[:, oo * SL:(oo + 1) * SL],
                            tl[:, oo * SL:(oo + 1) * SL])),
                           reads=((kk, oc),), writes=("sqt",))
                        st("pe", (lambda e, hf=half, bb=bank: e.matmul(
                            pbank[bb][0:1, 0:SL], onescolb[:], sqtl[:],
                            start=(hf == 0), stop=(hf == 1))),
                           reads=("sqt", "onescolb"), writes=(f"pb{bank}",))
                    r0 = h * SL
                    st("dve", (lambda e, rr=r0, bb=bank: e.tensor_scalar(
                        rrow[0:1, rr:rr + SL], pbank[bb][0:1, 0:SL],
                        1e-12, None, ALU.max)),
                       reads=(f"pb{bank}",), writes=(("rrow", h),))
                    st("act", (lambda e, rr=r0: e.activation(
                        out=rrow[0:1, rr:rr + SL], in_=rrow[0:1, rr:rr + SL],
                        func=AF.Ln)),
                       reads=(("rrow", h),), writes=(("rrow", h),))
                    st("act", (lambda e, rr=r0: e.activation(
                        out=rrow[0:1, rr:rr + SL], in_=rrow[0:1, rr:rr + SL],
                        func=AF.Exp, scale=-0.5)),
                       reads=(("rrow", h),), writes=(("rrow", h),))
                    st("dve", (lambda e, rr=r0: e.tensor_scalar(
                        rrow[0:1, rr:rr + SL], rrow[0:1, rr:rr + SL],
                        1e6, None, ALU.min)),
                       reads=(("rrow", h),), writes=(("rrow", h),))
                    bank2 = 4
                    st("pe", (lambda e, rr=r0, bb=bank2: e.matmul(
                        pbank[bb][:, 0:SL], onesf[:], rrow[0:1, rr:rr + SL],
                        start=True, stop=True)),
                       reads=(("rrow", h), "onesf"), writes=(f"pb{bank2}",))
                    for half in range(2):
                        oc = 2 * h + half
                        st("dve", (lambda e, tl=tile, oo=oc, bb=bank2:
                                   e.tensor_mul(
                                       tl[:, oo * SL:(oo + 1) * SL],
                                       tl[:, oo * SL:(oo + 1) * SL],
                                       pbank[bb][:, 0:SL])),
                           reads=((kk, oc), f"pb{bank2}"), writes=((kk, oc),))
            for tt in range(4):
                ti = s_ * 4 + tt
                p = ti % 2
                ct_ = cht[p]
                kp = lambda n: f"{n}{p}"
                st("act", (lambda e, d=ct_, tt_=ti: e.activation(
                    out=d["tiny"][:, 0:4] if False else d["gsb"][:, 0:4],
                    in_=absb[:, tt_ * 8:tt_ * 8 + 4], func=AF.Tanh, scale=0.5)),
                   reads=(("absb", ti),), writes=((kp("gsb"),),))
                st("act", (lambda e, d=ct_, tt_=ti: e.activation(
                    out=d["biasc"][:, 0:4], in_=absb[:, tt_ * 8 + 4:tt_ * 8 + 8],
                    func=AF.Tanh, scale=0.5)),
                   reads=(("absb", ti),), writes=((kp("biasc"),),))
                st("dve", (lambda e, d=ct_: e.tensor_scalar(
                    d["gsb"][:, 0:4], d["gsb"][:, 0:4], 0.49, 0.5,
                    ALU.mult, ALU.add)),
                   reads=((kp("gsb"),),), writes=((kp("gsb"),),))
                st("dve", (lambda e, d=ct_: e.tensor_scalar(
                    d["biasc"][:, 0:4], d["biasc"][:, 0:4], 0.5, 0.5,
                    ALU.mult, ALU.add)),
                   reads=((kp("biasc"),),), writes=((kp("biasc"),),))
                st("act", (lambda e, d=ct_: e.activation(
                    out=d["gsb"][:, 0:4], in_=d["gsb"][:, 0:4], func=AF.Ln)),
                   reads=((kp("gsb"),),), writes=((kp("gsb"),),))
                st("act", (lambda e, d=ct_: e.activation(
                    out=d["biasc"][:, 0:4], in_=d["biasc"][:, 0:4], func=AF.Ln)),
                   reads=((kp("biasc"),),), writes=((kp("biasc"),),))
                bank = 3
                st("pe", (lambda e, d=ct_, bb=bank: e.matmul(
                    pbank[bb][:, 0:4], uincf[:], d["gsb"][:, 0:4],
                    start=True, stop=True)),
                   reads=((kp("gsb"),), "uincf"), writes=(f"pb{bank}",))
                st("dve", (lambda e, d=ct_, bb=bank: e.tensor_copy(
                    d["gsb"][:, 0:4], pbank[bb][:, 0:4])),
                   reads=(f"pb{bank}",), writes=((kp("gsb"),),))
                st("dve", (lambda e, d=ct_: e.tensor_sub(
                    d["biasc"][:, 0:4], d["biasc"][:, 0:4], d["gsb"][:, 0:4])),
                   reads=((kp("biasc"),), (kp("gsb"),)),
                   writes=((kp("biasc"),),))
                st("act", (lambda e, d=ct_: e.activation(
                    out=d["ctc"][:, 0:4], in_=d["gsb"][:, 0:4], func=AF.Exp)),
                   reads=((kp("gsb"),),), writes=((kp("ctc"),),))
                st("pe", (lambda e, d=ct_, bb=bank: e.matmul(
                    pbank[bb][:, 0:4], e127[:], d["ctc"][:, 0:4],
                    start=True, stop=True)),
                   reads=((kp("ctc"),), "e127"), writes=(f"pb{bank}",))
                st("dve", (lambda e, d=ct_, bb=bank: e.tensor_copy(
                    d["clast4"][:, 0:4], pbank[bb][:, 0:4])),
                   reads=(f"pb{bank}",), writes=((kp("clast4"),),))
                for h in range(4):
                    _chunk_head(g, s_, tt, ti, p, ct_, h, qnsl, knsl, vnsl)
                _chunk_tail(g, s_, tt, ti, p, woutsb, ogsl)

    readn_t = [None, None]
    readfm_t = [None, None]
    dnbase_t = [None, None]
    cht = [None, None]

    def _chunk_head(g, s_, tt, ti, p, d, h, qnsl, knsl, vnsl):
        kp = lambda n: f"{n}{p}"
        hc0, hc1 = 2 * h, 2 * h + 1
        toff = tt * 128
        kq0, kq1 = ("qnsl", hc0), ("qnsl", hc1)
        kk0, kk1 = ("knsl", hc0), ("knsl", hc1)
        kv0, kv1 = ("vnsl", hc0), ("vnsl", hc1)
        qs0 = qnsl[:, hc0 * SL + toff: hc0 * SL + toff + 128]
        qs1 = qnsl[:, hc1 * SL + toff: hc1 * SL + toff + 128]
        ks0 = knsl[:, hc0 * SL + toff: hc0 * SL + toff + 128]
        ks1 = knsl[:, hc1 * SL + toff: hc1 * SL + toff + 128]
        vs0 = vnsl[:, hc0 * SL + toff: hc0 * SL + toff + 128]
        vs1 = vnsl[:, hc1 * SL + toff: hc1 * SL + toff + 128]
        stb0 = stbf[:, (2 * h) * 256:(2 * h) * 256 + 256]
        stb1 = stbf[:, (2 * h + 1) * 256:(2 * h + 1) * 256 + 256]
        kst0, kst1 = ("stbf", h, 0), ("stbf", h, 1)
        st("pe", (lambda e, dd=d, hh=h: e.matmul(
            pbank[4][0:1, 0:128], dd["gsb"][:, hh:hh + 1], identf[:],
            start=True, stop=True)),
           reads=((kp("gsb"),), "identf"), writes=("pb4",))
        st("dve", (lambda e, dd=d: e.tensor_copy(dd["grow"][:], pbank[4][0:1, 0:128])),
           reads=("pb4",), writes=((kp("grow"),),))
        st("pe", (lambda e, dd=d: e.matmul(
            pbank[4][:, 0:128], onesf[:], dd["grow"][:],
            start=True, stop=False)),
           reads=((kp("grow"),), "onesf"), writes=("pb4",))
        st("pe", lambda e: e.matmul(
            pbank[4][:, 0:128], maskaddT[:], identf[:],
            start=False, stop=True),
           reads=("maskaddT", "identf"), writes=("pb4",))
        st("act", (lambda e, dd=d, hh=h: e.activation(
            out=dd["dincl"][:], in_=pbank[4][:, 0:128], func=AF.Exp,
            bias=dd["biasc"][:, hh:hh + 1])),
           reads=("pb4", (kp("biasc"),)), writes=((kp("dincl"),),))
        st("dve", (lambda e, dd=d: e.tensor_mul(
            dd["dstr"][:], dd["dincl"][:], offdiag[:])),
           reads=((kp("dincl"),), "offdiag"), writes=((kp("dstr"),),))
        st("pe", (lambda e, a=ks0, b=ks0: e.matmul(
            pbank[4][:, 0:128], a, b, start=True, stop=False)),
           reads=(kk0,), writes=("pb4",))
        st("pe", (lambda e, a=ks1, b=ks1: e.matmul(
            pbank[4][:, 0:128], a, b, start=False, stop=True)),
           reads=(kk1,), writes=("pb4",))
        st("dve", (lambda e, dd=d: e.tensor_mul(
            dd["at"][:], pbank[4][:, 0:128], dd["dstr"][:])),
           reads=("pb4", (kp("dstr"),)), writes=((kp("at"),),))
        st("pe", (lambda e, a=ks0, b=qs0: e.matmul(
            pbank[4][:, 0:128], a, b, start=True, stop=False)),
           reads=(kk0, kq0), writes=("pb4",))
        st("pe", (lambda e, a=ks1, b=qs1: e.matmul(
            pbank[4][:, 0:128], a, b, start=False, stop=True)),
           reads=(kk1, kq1), writes=("pb4",))
        st("dve", (lambda e, dd=d: e.tensor_mul(
            dd["wt"][:], pbank[4][:, 0:128], dd["dincl"][:])),
           reads=("pb4", (kp("dincl"),)), writes=((kp("wt"),),))
        st("pe", (lambda e, dd=d: e.transpose(
            pbtr[0][:, 0:128], dd["at"][:], identb[:])),
           reads=((kp("at"),), "identb"), writes=("pt0",))
        st("act", (lambda e, dd=d: e.activation(
            out=dd["an"][:], in_=pbtr[0][:, 0:128], func=AF.Identity)),
           reads=("pt0",), writes=((kp("an"),),))
        st("pe", (lambda e, a=ks0, b=stb0: e.matmul(
            pbank[5][:, 0:256], a, b, start=True, stop=False)),
           reads=(kk0, kst0), writes=("pb5",))
        st("pe", (lambda e, a=ks1, b=stb1: e.matmul(
            pbank[5][:, 0:256], a, b, start=False, stop=True)),
           reads=(kk1, kst1), writes=("pb5",))
        st("dve", (lambda e, dd=d, hh=h: e.tensor_scalar(
            dd["mc"][:], pbank[5][:, 0:256], dd["ctc"][:, hh:hh + 1], None,
            ALU.mult)),
           reads=("pb5", (kp("ctc"),)), writes=((kp("mc"),),))
        for half, vs in ((0, vs0), (1, vs1)):
            st("pe", (lambda e, v=vs, hf=half: e.transpose(
                pbtr[1][:, hf * 128:hf * 128 + 128], v, identb[:])),
               reads=((kv0 if half == 0 else kv1), "identb"), writes=("pt1",))
            st("dve", (lambda e, dd=d, hf=half: e.tensor_sub(
                dd["rr"][:, hf * 128:hf * 128 + 128],
                pbtr[1][:, hf * 128:hf * 128 + 128],
                dd["mc"][:, hf * 128:hf * 128 + 128])),
               reads=("pt1", (kp("mc"),)), writes=((kp("rr"),),))
        st("pe", (lambda e, dd=d: e.matmul(
            pbank[0][:, 0:256], dd["at"][:], dd["rr"][:],
            start=True, stop=True)),
           reads=((kp("at"),), (kp("rr"),)), writes=("pb0",))
        st("dve", (lambda e, dd=d: e.tensor_sub(
            dd["u0"][:], dd["rr"][:], pbank[0][:, 0:256])),
           reads=((kp("rr"),), "pb0"), writes=((kp("u0"),),))
        cur_p, cur_b = "at", "an"
        cur_u = "u0"
        for lev, pwr in enumerate((2, 4, 8, 16, 32, 64)):
            new_p = ("pp0", "pp1")[lev % 2]
            new_b = ("bp0", "bp1")[lev % 2]
            if pwr < 64:
                st("pe", (lambda e, dd=d, cp=cur_p, cb=cur_b: e.matmul(
                    pbank[1][:, 0:128], dd[cp][:], dd[cb][:],
                    start=True, stop=True)),
                   reads=((kp(cur_p),), (kp(cur_b),)), writes=("pb1",))
            st("pe", (lambda e, dd=d, cp=cur_p, cb=cur_b: e.matmul(
                pbank[1][:, 128:256], dd[cb][:], dd[cp][:],
                start=True, stop=True)),
               reads=((kp(cur_p),), (kp(cur_b),)), writes=("pb1",))
            if pwr < 64:
                st("act", (lambda e, dd=d, nb=new_b: e.activation(
                    out=dd[nb][:], in_=pbank[1][:, 0:128], func=AF.Identity)),
                   reads=("pb1",), writes=((kp(new_b),),))
            st("act", (lambda e, dd=d, np_=new_p: e.activation(
                out=dd[np_][:], in_=pbank[1][:, 128:256], func=AF.Identity)),
               reads=("pb1",), writes=((kp(new_p),),))
            st("pe", (lambda e, dd=d, np_=new_p, cu=cur_u: e.matmul(
                pbank[0][:, 0:256], dd[np_][:], dd[cu][:],
                start=True, stop=True)),
               reads=((kp(new_p),), (kp(cur_u),)), writes=("pb0",))
            nxt_u = "u1" if cur_u == "u0" else "u0"
            st("dve", (lambda e, dd=d, cu=cur_u, nu=nxt_u: e.tensor_add(
                dd[nu][:], dd[cu][:], pbank[0][:, 0:256])),
               reads=((kp(cur_u),), "pb0"), writes=((kp(nxt_u),),))
            cur_p, cur_b, cur_u = new_p, new_b, nxt_u
        st("pe", (lambda e, a=qs0, b=stb0: e.matmul(
            pbank[5][:, 0:256], a, b, start=True, stop=False)),
           reads=(kq0, kst0), writes=("pb5",))
        st("pe", (lambda e, a=qs1, b=stb1: e.matmul(
            pbank[5][:, 0:256], a, b, start=False, stop=True)),
           reads=(kq1, kst1), writes=("pb5",))
        st("dve", (lambda e, dd=d, hh=h: e.tensor_scalar(
            dd["readf"][:], pbank[5][:, 0:256], dd["ctc"][:, hh:hh + 1], None,
            ALU.mult)),
           reads=("pb5", (kp("ctc"),)), writes=((kp("readf"),),))
        st("pe", (lambda e, dd=d, cu=cur_u: e.matmul(
            pbank[2][:, 0:256], dd["wt"][:], dd[cu][:],
            start=True, stop=True)),
           reads=((kp("wt"),), (kp(cur_u),)), writes=("pb2",))
        st("dve", (lambda e, dd=d: e.tensor_add(
            dd["readf"][:], dd["readf"][:], pbank[2][:, 0:256])),
           reads=((kp("readf"),), "pb2"), writes=((kp("readf"),),))
        st("act", (lambda e, dd=d: e.activation(
            out=dd["sqr"][:], in_=dd["readf"][:], func=AF.Square,
            accum_out=dd["tiny"][:, 0:1])),
           reads=((kp("readf"),),), writes=((kp("tiny"),), (kp("sqr"),)))
        st("act", (lambda e, dd=d: e.activation(
            out=dd["tiny"][:, 1:2], in_=dd["tiny"][:, 0:1], func=AF.Ln,
            scale=1.0 / 256, bias=1e-6)),
           reads=((kp("tiny"),), "constpool"), writes=((kp("tiny"),),))
        st("act", (lambda e, dd=d: e.activation(
            out=dd["tiny"][:, 2:3], in_=dd["tiny"][:, 1:2], func=AF.Exp,
            scale=-0.5)),
           reads=((kp("tiny"),),), writes=((kp("tiny"),),))
        st("dve", (lambda e, dd=d, hh=h, rn=readn_t[p]: e.tensor_scalar(
            rn[:, hh * 256:(hh + 1) * 256], dd["readf"][:],
            dd["tiny"][:, 2:3], None, ALU.mult)),
           reads=((kp("readf"),), (kp("tiny"),)),
           writes=((f"readn{p}", h),))
        for half, ks in ((0, ks0), (1, ks1)):
            st("pe", (lambda e, k=ks, hf=half: e.transpose(
                pbtr[1][:, hf * 128:hf * 128 + 128], k, identb[:])),
               reads=((kk0 if half == 0 else kk1), "identb"), writes=("pt1",))
            st("dve", (lambda e, dd=d, hf=half: e.tensor_scalar(
                dd["ktok"][:, hf * 128:hf * 128 + 128],
                pbtr[1][:, hf * 128:hf * 128 + 128],
                dd["dincl"][:, 127:128], None, ALU.mult)),
               reads=("pt1", (kp("dincl"),)), writes=((kp("ktok"),),))
        for kc in range(2):
            st("pe", (lambda e, dd=d, cc=kc, cu=cur_u: e.matmul(
                pbank[5][:, 0:256], dd["ktok"][:, cc * 128:cc * 128 + 128],
                dd[cu][:], start=True, stop=True)),
               reads=((kp("ktok"),), (kp(cur_u),)), writes=("pb5",))
            sl32 = st32[:, (2 * h + kc) * 256:(2 * h + kc) * 256 + 256]
            slbf = stbf[:, (2 * h + kc) * 256:(2 * h + kc) * 256 + 256]
            st("dve", (lambda e, dd=d, hh=h, s3=sl32: e.scalar_tensor_tensor(
                s3, s3, dd["clast4"][:, hh:hh + 1], pbank[5][:, 0:256],
                ALU.mult, ALU.add)),
               reads=(("st32", h, kc), (kp("clast4"),), "pb5"),
               writes=(("st32", h, kc),))
            st("act", (lambda e, s3=sl32, sbf_=slbf: e.activation(
                out=sbf_, in_=s3, func=AF.Identity)),
               reads=(("st32", h, kc),), writes=(("stbf", h, kc),))

    def _chunk_tail(g, s_, tt, ti, p, woutsb, ogsl):
        rn, rfm, dnb = readn_t[p], readfm_t[p], dnbase_t[p]
        krn = tuple((f"readn{p}", h) for h in range(4))
        for vc in range(8):
            bank = vc % 2
            st("pe", (lambda e, v=vc, bb=bank, r_=rn: e.transpose(
                pbtr[bb][:, 0:128], r_[:, v * 128:(v + 1) * 128], identb[:])),
               reads=krn + ("identb",), writes=(f"pt{bank}",))
            st("dve", (lambda e, v=vc, bb=bank, r_=rfm: e.tensor_copy(
                r_[:, v * 128:(v + 1) * 128], pbtr[bb][:, 0:128])),
               reads=(f"pt{bank}",), writes=((f"readfm{p}", vc),))
        dmac("sp", dnb[:], base_d[ti * 128:(ti + 1) * 128, :],
             reads=(("based", ti), (f"dnbase{p}", "out")),
             writes=((f"dnbase{p}", "sum"),))
        for oc in range(8):
            bank = 4
            for vc in range(8):
                st("pe", (lambda e, oo=oc, v=vc, bb=bank, r_=rfm: e.matmul(
                    pbank[bb][:, 0:128],
                    woutsb[:, v * D + oo * 128: v * D + oo * 128 + 128],
                    r_[:, v * 128:(v + 1) * 128],
                    start=(v == 0), stop=(v == 7))),
                   reads=(("woutsb", vc), (f"readfm{p}", vc)),
                   writes=(f"pb{bank}",))
            ogslice = ogsl[:, oc * SL + tt * 128: oc * SL + tt * 128 + 128]
            ksqr = (f"sqr{p}",)
            krr = (f"rr{p}",)
            st("act", (lambda e, og=ogslice, d=cht[p]: e.activation(
                out=d["sqr"][:, 0:128], in_=og, func=AF.Tanh, scale=0.5)),
               reads=(("ogsl", oc),), writes=(ksqr,))
            st("dve", (lambda e, d=cht[p]: e.tensor_scalar(
                d["sqr"][:, 0:128], d["sqr"][:, 0:128], 0.5, 0.5,
                ALU.mult, ALU.add)),
               reads=(ksqr,), writes=(ksqr,))
            st("dve", (lambda e, oo=oc, bb=bank, d=cht[p]:
                       e.scalar_tensor_tensor(
                           d["sqr"][:, 128:256], pbank[bb][:, 0:128],
                           boutc[:, oo:oo + 1], d["sqr"][:, 0:128],
                           ALU.add, ALU.mult)),
               reads=(f"pb{bank}", "boutc", ksqr),
               writes=(ksqr,))
            st("dve", (lambda e, oo=oc, d=cht[p]: e.tensor_scalar(
                d["sqr"][:, 128:256], d["sqr"][:, 128:256],
                rampc[:, oo:oo + 1], None, ALU.mult)),
               reads=(ksqr, "rampc"), writes=(ksqr,))
            st("dve", (lambda e, d=cht[p]: e.tensor_copy(
                d["rr"][:, 0:128], d["sqr"][:, 128:256])),
               reads=(ksqr, krr), writes=(krr,))
            bank2 = oc % 2
            st("pe", (lambda e, d=cht[p], bb=bank2: e.transpose(
                pbtr[bb][:, 0:128], d["rr"][:, 0:128], identb[:])),
               reads=(krr, "identb"), writes=(f"pt{bank2}",))
            st("dve", (lambda e, oo=oc, bb=bank2, b_=dnb: e.tensor_add(
                b_[:, oo * 128:(oo + 1) * 128], b_[:, oo * 128:(oo + 1) * 128],
                pbtr[bb][:, 0:128])),
               reads=(f"pt{bank2}", (f"dnbase{p}", "sum")),
               writes=((f"dnbase{p}", "sum"),))
        dmac("gp", base_d[ti * 128:(ti + 1) * 128, :], dnb[:],
             reads=((f"dnbase{p}", "sum"),), writes=(("based", ti), (f"dnbase{p}", "out")))

    def phase2proj(g):
        qwTs, kwTs, vwTs = g["qwTs"], g["kwTs"], g["vwTs"]
        copt, sipt = g["copt"], g["sipt"]
        qpt, ropet1, ropet2, ropeo = g["qpt"], g["ropet1"], g["ropet2"], g["ropeo"]
        vtokt, sqa = g["vtokt"], g["sqa"]
        load_wchunks(qwTs, "qwTs", qwT_h, 8, D)
        load_wchunks(kwTs, "kwTs", kwT_h, 8, D)
        load_wchunks(vwTs, "vwTs", vwT_h, 8, D)
        dmac("sp", copt[:], con_h["copt"], reads=(), writes=("copt",))
        dmac("sp", sipt[:], con_h["sipt"], reads=(), writes=("sipt",))
        for s_ in range(NSL):
            t0 = s_ * SL
            xn_reads = tuple(("xnfm", ic, t) for ic in range(8)
                             for t in range(s_ * 4, s_ * 4 + 4))
            for which, wts, wkey, bias, bkey, dst_d, dkey, macc, mkey in (
                    ("q", qwTs, "qwTs", qbc, "qbc", qr_d, "qr", maccq, "maccq"),
                    ("k", kwTs, "kwTs", kbc, "kbc", kr_d, "kr", macck, "macck")):
                for oc in range(8):
                    bank = oc % 2
                    for ic in range(8):
                        st("pe", (lambda e, oo=oc, ii=ic, bb=bank, tt=t0, w=wts:
                                  e.matmul(pbank[bb][:, :],
                                           w[:, ii * D + oo * 128:
                                             ii * D + oo * 128 + 128],
                                           xnfm[:, ii * S + tt: ii * S + tt + SL],
                                           start=(ii == 0), stop=(ii == 7))),
                           reads=((wkey, ic),) + xn_reads, writes=(f"pb{bank}",))
                    pq = oc % 2
                    qt = qpt[pq]
                    st("act", (lambda e, oo=oc, bb=bank, q_=qt, b_=bias:
                               e.activation(out=q_[:], in_=pbank[bb][:, :],
                                            func=AF.Identity,
                                            bias=b_[:, oo:oo + 1])),
                       reads=(f"pb{bank}", bkey), writes=((f"qpt{pq}",),))
                    st("dve", (lambda e, q_=qt: e.tensor_mul(
                        sqa[:], q_[:], q_[:])),
                       reads=((f"qpt{pq}",),), writes=("sqa",))
                    st("pe", (lambda e: e.matmul(
                        pbank[2][0:2, 0:SL], headsel[:], sqa[:],
                        start=True, stop=True)),
                       reads=("sqa", "headsel"), writes=("pb2",))
                    st("dve", lambda e: e.tensor_reduce(
                        mtmp[:], pbank[2][0:2, 0:SL], AX.X, ALU.max),
                       reads=("pb2",), writes=("mtmp",))
                    st("dve", (lambda e, m_=macc, oo=oc: e.tensor_max(
                        m_[:, oo:oo + 1], m_[:, oo:oo + 1], mtmp[:])),
                       reads=("mtmp", mkey), writes=(mkey,))
                    st("pe", (lambda e, q_=qt: e.matmul(
                        pbank[2][:, 0:SL], pswapT[:], q_[:],
                        start=True, stop=True)),
                       reads=((f"qpt{pq}",), "pswapT"), writes=("pb2",))
                    st("dve", (lambda e, q_=qt, tt=t0: e.tensor_mul(
                        ropet1[:], q_[:], copt[:, tt:tt + SL])),
                       reads=((f"qpt{pq}",), "copt"), writes=("ropet1",))
                    st("dve", (lambda e, tt=t0: e.tensor_mul(
                        ropet2[:], pbank[2][:, 0:SL], sipt[:, tt:tt + SL])),
                       reads=("pb2", "sipt"), writes=("ropet2",))
                    st("dve", lambda e: e.tensor_add(
                        ropeo[:], ropet1[:], ropet2[:]),
                       reads=("ropet1", "ropet2"), writes=("ropeo",))
                    dmac("gp", dst_d[oc * 128:(oc + 1) * 128, t0:t0 + SL],
                         ropeo[:], reads=("ropeo",), writes=((dkey, oc, s_),))
            for u in range(4):
                ti = s_ * 4 + u
                for half in range(2):
                    bank = 2 + half
                    for ic in range(8):
                        st("pe", (lambda e, ii=ic, hf=half, bb=bank, tt_=ti:
                                  e.matmul(pbank[bb][:, :],
                                           xnfm[:, ii * S + tt_ * 128:
                                                ii * S + tt_ * 128 + 128],
                                           vwTs[:, ii * D + hf * 512:
                                                ii * D + hf * 512 + 512],
                                           start=(ii == 0), stop=(ii == 7))),
                           reads=(("vwTs", ic),) + xn_reads,
                           writes=(f"pb{bank}",))
                    st("dve", (lambda e, hf=half, bb=2 + half: e.tensor_add(
                        vtokt[:, hf * 512:hf * 512 + 512], pbank[bb][:, :],
                        vbb[:, hf * 512:hf * 512 + 512])),
                       reads=(f"pb{2 + half}", "vbb"), writes=(("vtokt", half),))
                dmac("gp", vt_d[ti * 128:(ti + 1) * 128, :], vtokt[:],
                     reads=(("vtokt", 0), ("vtokt", 1)), writes=(("vt", ti),))

    def phase2attn(g):
        khd, qhd, vslab = g["khd"], g["qhd"], g["vslab"]
        ptl = g["ptl"]
        zbs, aot = g["zbs"], g["aot"]
        zrow = g["zrow"]
        negm, mqcol, mkcol = g["negmh"], g["mqcol"], g["mkcol"]
        st("dve", lambda e: e.memset(khd[64:128, :], 0.0),
           reads=(), writes=("khd",))
        st("dve", lambda e: e.memset(qhd[64:128, :], 0.0),
           reads=(), writes=("qhd",))
        st("dve", lambda e: e.memset(vslab[:], 0.0),
           reads=(), writes=(("vslab", "ones"),))
        st("dve", lambda e: e.memset(
            vslab[:].rearrange("p (kt c) -> p kt c", c=128)[:, :, 64:65], 1.0),
           reads=(("vslab", "ones"),), writes=(("vslab", "ones"),))
        vt_view = vt_d.rearrange("(kt p) v -> p kt v", p=128)
        for h in range(AH):
            dmac("sp", khd[0:64, :], kr_d[h * 64:(h + 1) * 64, :],
                 reads=tuple(("kr", 2 * (h // 2) + hh, s2) for hh in range(2)
                             for s2 in range(NSL)) + ("khd",),
                 writes=("khd",))
            dmac("sp", qhd[0:64, :], qr_d[h * 64:(h + 1) * 64, :],
                 reads=tuple(("qr", 2 * (h // 2) + hh, s2) for hh in range(2)
                             for s2 in range(NSL)) + ("qhd",),
                 writes=("qhd",))
            for g4 in range(4):
                dmac("sp",
                     vslab[:].rearrange("p (kt c) -> p kt c", c=128)
                     [:, g4 * 8:(g4 + 1) * 8, 0:64],
                     vt_view[:, g4 * 8:(g4 + 1) * 8, h * 64:(h + 1) * 64],
                     reads=tuple(("vt", tti) for tti in range(NT)) +
                           (("vslab", "war"),),
                     writes=(("vslab", g4),))
            hm, hs_ = (esel0, "esel0") if h % 2 == 0 else (esel1, "esel1")
            st("pe", (lambda e, sel=hm, hh=h: e.matmul(
                pbank[5][:, 0:1], sel[:], maccq[:, hh // 2:hh // 2 + 1],
                start=True, stop=True)),
               reads=(hs_, "maccq"), writes=("pb5",))
            st("act", lambda e: e.activation(out=mqcol[:], in_=pbank[5][:, 0:1],
                                             func=AF.Ln),
               reads=("pb5",), writes=("mqcol",))
            st("act", lambda e: e.activation(out=mqcol[:], in_=mqcol[:],
                                             func=AF.Exp, scale=0.5),
               reads=("mqcol",), writes=("mqcol",))
            st("pe", (lambda e, sel=hm, hh=h: e.matmul(
                pbank[5][:, 0:1], sel[:], macck[:, hh // 2:hh // 2 + 1],
                start=True, stop=True)),
               reads=(hs_, "macck"), writes=("pb5",))
            st("act", lambda e: e.activation(out=mkcol[:], in_=pbank[5][:, 0:1],
                                             func=AF.Ln),
               reads=("pb5",), writes=("mkcol",))
            st("act", lambda e: e.activation(out=mkcol[:], in_=mkcol[:],
                                             func=AF.Exp, scale=0.5),
               reads=("mkcol",), writes=("mkcol",))
            st("dve", lambda e: e.tensor_mul(negm[:], mqcol[:], mkcol[:]),
               reads=("mqcol", "mkcol"), writes=("negm",))
            st("dve", lambda e: e.tensor_scalar(
                negm[:], negm[:], -0.125, None, ALU.mult),
               reads=("negm",), writes=("negm",))
            for qb in range(8):
                q0 = qb * SL
                NG = NT // 2

                def emit_st(g):
                    for half in range(2):
                        kt = 2 * g + half
                        bs = 2 * (g % 2) + half
                        st("pe", (lambda e, kk=kt, q_=q0, bb=bs: e.matmul(
                            pbank[bb][:, :], khd[:, kk * 128:(kk + 1) * 128],
                            qhd[:, q_:q_ + SL], start=True, stop=True)),
                           reads=("khd", "qhd"), writes=(f"pb{bs}",))

                emit_st(0)
                emit_st(1)
                for g in range(NG):
                    jp = g % 2
                    b0 = 2 * (g % 2)
                    st("act", (lambda e, j=jp: e.activation(
                        out=ptl[j][:, 0:1024], in_=pbw[j][:, 0:1024],
                        func=AF.Exp, scale=0.125, bias=negm[:])),
                       reads=(f"pb{b0}", f"pb{b0 + 1}", "negm"),
                       writes=((f"ptl{jp}",),))
                    if g + 2 < NG:
                        emit_st(g + 2)
                    for half in range(2):
                        kt = 2 * g + half
                        st("pe", (lambda e, kk=kt, j=jp, hf=half: e.matmul(
                            pbank[4][:, 0:SL],
                            vslab[:, kk * 128:(kk + 1) * 128],
                            ptl[j][:, hf * SL:hf * SL + SL],
                            start=(kk == 0), stop=(kk == 31))),
                           reads=((f"ptl{jp}",), ("vslab", kt // 8),
                                  ("vslab", "ones")),
                           writes=("pb4",))
                st("dve", lambda e: e.reciprocal(zrow[0:1, 0:SL],
                                                 pbank[4][64:65, 0:SL]),
                   reads=("pb4",), writes=("zrow",))
                st("pe", lambda e: e.matmul(
                    pbank[5][0:64, 0:SL], onesf[0:1, 0:64], zrow[0:1, 0:SL],
                    start=True, stop=True),
                   reads=("zrow", "onesf"), writes=("pb5",))
                st("dve", lambda e: e.tensor_copy(zbs[0:64, 0:SL],
                                                  pbank[5][0:64, 0:SL]),
                   reads=("pb5",), writes=("zbs",))
                st("dve", lambda e: e.tensor_mul(
                    aot[0:64, 0:SL], pbank[4][0:64, 0:SL], zbs[0:64, 0:SL]),
                   reads=("pb4", "zbs"), writes=("aot",))
                dmac("gp", ao_d[h * 64:(h + 1) * 64, q0:q0 + SL], aot[0:64, 0:SL],
                     reads=("aot",), writes=(("ao", h, qb),))

    def phase3(g):
        aofm, fwTs = g["aofm"], g["fwTs"]
        expd = g["expd"]
        moeacc, moebf, moefm = g["moeacc"], g["moebf"], g["moefm"]
        baset, outt = g["baset"], g["outt"]
        gatesf, gtmp, gmsk = g["gatesf"], g["gtmp"], g["gmsk"]
        gcol, gcol2 = g["gcol"], g["gcol2"]
        wtok4, wfm4 = g["wtok4"], g["wfm4"]
        load_wchunks(fwTs, "fwTs", fwT_h, 8, D)
        for ic in range(8):
            for half in range(2):
                st("pe", (lambda e, ii=ic, hf=half: e.matmul(
                    (pbank[2] if hf == 0 else pbank[3])[0:8, 0:512],
                    ebTs[:, ii * 8:(ii + 1) * 8],
                    fwTs[:, ii * D + hf * 512: ii * D + hf * 512 + 512],
                    start=(ii == 0), stop=(ii == 7))),
                   reads=(("fwTs", ic), "ebTs"),
                   writes=(("pb2" if half == 0 else "pb3"),))
        for half in range(2):
            st("dve", (lambda e, hf=half: e.tensor_copy(
                ebf[0:8, hf * 512:hf * 512 + 512],
                (pbank[2] if hf == 0 else pbank[3])[0:8, 0:512])),
               reads=(("pb2" if half == 0 else "pb3"),), writes=("ebf",))
        for tl in range(8):
            t0 = tl * SL
            for ic in range(8):
                dmac("sp", aofm[:, ic * SL:(ic + 1) * SL],
                     ao_d[ic * 128:(ic + 1) * 128, t0:t0 + SL],
                     reads=tuple(("ao", h, tl) for h in range(AH)) + ("aofm",),
                     writes=("aofm",))
            for u in range(4):
                uo = u * 128
                for ic in range(8):
                    st("pe", (lambda e, ii=ic, oo=uo: e.matmul(
                        pbank[4][:, 0:8],
                        aofm[:, ii * SL + oo: ii * SL + oo + 128],
                        gwTs[:, ii * 8:(ii + 1) * 8],
                        start=(ii == 0), stop=(ii == 7))),
                       reads=("aofm", "gwTs"), writes=("pb4",))
                st("dve", lambda e: e.tensor_add(gatesf[:], pbank[4][:, 0:8],
                                                 gbb[:]),
                   reads=("pb4", "gbb"), writes=("gatesf",))
                st("dve", lambda e: e.tensor_reduce(
                    gcol[:], gatesf[:], AX.X, ALU.max, negate=True),
                   reads=("gatesf",), writes=("gcol",))
                st("act", lambda e: e.activation(
                    out=gtmp[:], in_=gatesf[:], func=AF.Exp, bias=gcol[:]),
                   reads=("gatesf", "gcol"), writes=("gtmp",))
                st("dve", lambda e: e.tensor_reduce(
                    gcol2[:], gtmp[:], AX.X, ALU.add),
                   reads=("gtmp",), writes=("gcol2",))
                st("dve", lambda e: e.reciprocal(gcol2[:], gcol2[:]),
                   reads=("gcol2",), writes=("gcol2",))
                st("dve", lambda e: e.tensor_scalar(
                    gatesf[:], gtmp[:], gcol2[:], None, ALU.mult),
                   reads=("gtmp", "gcol2"), writes=("gatesf",))
                st("dve", lambda e: e.tensor_reduce(
                    gcol[:], gatesf[:], AX.X, ALU.max),
                   reads=("gatesf",), writes=("gcol",))
                st("dve", lambda e: e.tensor_scalar(
                    gmsk[:], gatesf[:], gcol[:], None, ALU.is_ge),
                   reads=("gatesf", "gcol"), writes=("gmsk",))
                st("dve", lambda e: e.scalar_tensor_tensor(
                    gtmp[:], gmsk[:], -1e30, gatesf[:], ALU.mult, ALU.add),
                   reads=("gmsk", "gatesf"), writes=("gtmp",))
                st("dve", lambda e: e.tensor_reduce(
                    gcol2[:], gtmp[:], AX.X, ALU.max),
                   reads=("gtmp",), writes=("gcol2",))
                st("dve", lambda e: e.tensor_scalar(
                    gmsk[:], gatesf[:], gcol2[:], None, ALU.is_ge),
                   reads=("gatesf", "gcol2"), writes=("gmsk",))
                st("dve", (lambda e, uu=u: e.tensor_mul(
                    wtok4[:, uu * 8:(uu + 1) * 8], gmsk[:], gatesf[:])),
                   reads=("gmsk", "gatesf"), writes=("wtok4",))
                st("pe", (lambda e, uu=u: e.transpose(
                    pbank[4][0:8, 0:128], wtok4[:, uu * 8:(uu + 1) * 8],
                    identf[:])),
                   reads=("wtok4", "identf"), writes=("pb4",))
                st("dve", (lambda e, uu=u: e.tensor_copy(
                    wfm4[0:8, uu * 128:(uu + 1) * 128], pbank[4][0:8, 0:128])),
                   reads=("pb4",), writes=("wfm4",))
            for n in range(NE):
                j = n % 2
                for ic in range(8):
                    dmac("gp", expd[j][:, ic * D:(ic + 1) * D],
                         ewT_h[n, ic * 128:(ic + 1) * 128, :],
                         reads=(f"expd{j}",), writes=(f"expd{j}",))
                for u in range(4):
                    uo = u * 128
                    for half in range(2):
                        bank = half
                        for ic in range(8):
                            st("pe", (lambda e, ii=ic, jj=j, hf=half,
                                      bb=bank, oo=uo: e.matmul(
                                          pbank[bb][:, :],
                                          aofm[:, ii * SL + oo: ii * SL + oo + 128],
                                          expd[jj][:, ii * D + hf * 512:
                                                   ii * D + hf * 512 + 512],
                                          start=(ii == 0), stop=(ii == 7))),
                               reads=("aofm", f"expd{j}"),
                               writes=(f"pb{bank}",))
                    for half in range(2):
                        if n == 0:
                            st("dve", (lambda e, hf=half, uu=u: e.tensor_scalar(
                                moeacc[uu][:, hf * 512:hf * 512 + 512],
                                pbank[hf][:, :], wtok4[:, uu * 8:uu * 8 + 1],
                                None, ALU.mult)),
                               reads=(f"pb{half}", "wtok4"),
                               writes=((f"moeacc{u}",),))
                        else:
                            st("dve", (lambda e, nn=n, hf=half, uu=u:
                                       e.scalar_tensor_tensor(
                                           moeacc[uu][:, hf * 512:hf * 512 + 512],
                                           pbank[hf][:, :],
                                           wtok4[:, uu * 8 + nn:uu * 8 + nn + 1],
                                           moeacc[uu][:, hf * 512:hf * 512 + 512],
                                           ALU.mult, ALU.add)),
                               reads=(f"pb{half}", "wtok4", (f"moeacc{u}",)),
                               writes=((f"moeacc{u}",),))
            for u in range(4):
                st("dve", (lambda e, uu=u: e.tensor_copy(moebf[:], moeacc[uu][:])),
                   reads=((f"moeacc{u}",), "moebf"), writes=("moebf",))
                for oc in range(8):
                    bank = oc % 2
                    st("pe", (lambda e, oo=oc, bb=bank: e.transpose(
                        pbtr[bb][:, 0:128], moebf[:, oo * 128:(oo + 1) * 128],
                        identb[:])),
                       reads=("moebf", "identb"), writes=(f"pt{bank}",))
                    st("dve", (lambda e, oo=oc, bb=bank: e.tensor_copy(
                        moefm[:, oo * 128:(oo + 1) * 128], pbtr[bb][:, 0:128])),
                       reads=(f"pt{bank}",), writes=("moefm",))
                for half in range(2):
                    bank = 2 + half
                    for oc in range(8):
                        st("pe", (lambda e, oo=oc, hf=half, bb=bank: e.matmul(
                            pbank[bb][:, :],
                            moefm[:, oo * 128:(oo + 1) * 128],
                            fwTs[:, oo * D + hf * 512: oo * D + hf * 512 + 512],
                            start=(oo == 0), stop=False)),
                           reads=("moefm", ("fwTs", oc)),
                           writes=(f"pb{bank}",))
                    st("pe", (lambda e, hf=half, bb=bank, uu=u: e.matmul(
                        pbank[bb][:, :], wfm4[0:8, uu * 128:(uu + 1) * 128],
                        ebf[0:8, hf * 512:hf * 512 + 512],
                        start=False, stop=True)),
                       reads=("wfm4", "ebf"), writes=(f"pb{bank}",))
                ti = tl * 4 + u
                dmac("sp", baset[:], base_d[ti * 128:(ti + 1) * 128, :],
                     reads=(("based", ti), "baset"),
                     writes=("baset",))
                po = u % 2
                ot = outt[po]
                for half in range(2):
                    st("dve", (lambda e, hf=half, o_=ot: e.tensor_add(
                        o_[:, hf * 512:hf * 512 + 512], pbank[2 + hf][:, :],
                        fbb[:, hf * 512:hf * 512 + 512])),
                       reads=(f"pb{2 + half}", "fbb", (f"outt{po}",)),
                       writes=((f"outt{po}",),))
                    st("dve", (lambda e, hf=half, o_=ot: e.tensor_add(
                        o_[:, hf * 512:hf * 512 + 512],
                        o_[:, hf * 512:hf * 512 + 512],
                        baset[:, hf * 512:hf * 512 + 512])),
                       reads=((f"outt{po}",), "baset"),
                       writes=((f"outt{po}",),))
                dmac("gp", out_h[ti * 128:(ti + 1) * 128, :], ot[:],
                     reads=((f"outt{po}",),),
                     writes=((f"outt{po}",), ("outw", ti)))

    loads_init()
    if "p0" in phases:
        phase0()
    P.reset_region("RB")
    P.reset_region("RC")
    g1a = {}
    g1a["winsb"] = palloc("winsb", (128, 8 * 3 * D), BF16, OFF["winsb"], "RB")
    g1a["wogsb"] = palloc("wogsb", (128, 8 * D), BF16, OFF["wogsb"], "RC")
    g1a["praw"] = [palloc(f"praw{i}", (128, SL + 4), BF16, OFF[f"praw{i}"], "RC")
                   for i in range(4)]
    convt = palloc("convt", (128, SL), BF16, OFF["convt"], "RC")
    g1a["ogt"] = [palloc(f"ogt{i}", (128, SL), F32, OFF[f"ogt{i}"], "RC")
                  for i in range(2)]
    if "p1a" in phases:
        phase1a(g1a)
    P.reset_region("RB")
    P.reset_region("RC")
    g1b = {}
    g1b["qnsl"] = palloc("qnsl", (128, 8 * SL), BF16, OFF["qnsl"], "RB")
    g1b["knsl"] = palloc("knsl", (128, 8 * SL), BF16, OFF["knsl"], "RB")
    g1b["vnsl"] = palloc("vnsl", (128, 8 * SL), BF16, OFF["vnsl"], "RB")
    g1b["woutsb"] = palloc("woutsb", (128, 8 * D), BF16, OFF["woutsb"], "RB")
    g1b["ogsl"] = palloc("ogsl", (128, 8 * SL), F32, OFF["ogsl"], "RC")
    g1b["sqt"] = palloc("sqt", (128, SL), BF16, OFF["sqt"], "RC")
    g1b["rrow"] = palloc("rrow", (1, 4 * SL), F32, OFF["rrow"], "RC")
    cht[0] = chunk_tiles(0)
    cht[1] = chunk_tiles(1)
    readn_t[0], readfm_t[0], dnbase_t[0] = rd_tiles(0)
    readn_t[1], readfm_t[1], dnbase_t[1] = rd_tiles(1)
    if "p1b" in phases:
        phase1b(g1b)
    P.reset_region("RB")
    P.reset_region("RC")
    g2p = {}
    g2p["qwTs"] = palloc("qwTs", (128, 8 * D), BF16, OFF["qwTs"], "RB")
    g2p["kwTs"] = palloc("kwTs", (128, 8 * D), BF16, OFF["kwTs"], "RB")
    g2p["vwTs"] = palloc("vwTs", (128, 8 * D), BF16, OFF["vwTs"], "RB")
    g2p["copt"] = palloc("copt", (128, S), F32, OFF["copt"], "RC")
    g2p["sipt"] = palloc("sipt", (128, S), F32, OFF["sipt"], "RC")
    g2p["qpt"] = [palloc(f"qpt{i}", (128, SL), BF16, OFF[f"qpt{i}"], "RB")
                  for i in range(2)]
    g2p["ropet1"] = palloc("ropet1", (128, SL), F32, OFF["ropet1"], "RB")
    g2p["ropet2"] = palloc("ropet2", (128, SL), F32, OFF["ropet2"], "RB")
    g2p["ropeo"] = palloc("ropeo", (128, SL), BF16, OFF["ropeo"], "RB")
    g2p["vtokt"] = palloc("vtokt", (128, D), BF16, OFF["vtokt"], "RB")
    g2p["sqa"] = palloc("sqa", (128, SL), BF16, OFF["sqa"], "RB")
    if "p2p" in phases:
        phase2proj(g2p)
    P.reset_region("RB")
    P.reset_region("RC")
    g2a = {}
    g2a["khd"] = palloc("khd", (128, S), BF16, OFF["khd"], "RB")
    g2a["qhd"] = palloc("qhd", (128, S), BF16, OFF["qhd"], "RB")
    g2a["vslab"] = palloc("vslab", (128, 32 * 128), BF16, OFF["vslab"], "RB")
    g2a["ptl"] = [palloc(f"ptl{i}", (128, 1024), BF16, OFF[f"ptl{i}"], "RB")
                  for i in range(2)]
    g2a["zbs"] = palloc("zbs", (64, SL), F32, OFF["zbs"], "RB")
    g2a["zrow"] = palloc("zrow", (1, SL), F32, OFF["zrow"], "RB")
    g2a["aot"] = palloc("aot", (64, SL), BF16, OFF["aot"], "RB")
    g2a["negmh"] = palloc("negmh", (128, 1), F32, OFF["negmh"], "RB")
    g2a["mqcol"] = palloc("mqcol", (128, 1), F32, OFF["mqcol"], "RB")
    g2a["mkcol"] = palloc("mkcol", (128, 1), F32, OFF["mkcol"], "RB")
    if "p2a" in phases:
        phase2attn(g2a)
    P.reset_region("RA")
    P.reset_region("RB")
    P.reset_region("RC")
    g3 = {}
    g3["expd"] = [palloc(f"expd{i}", (128, 8 * D), BF16, OFF[f"expd{i}"], "RA")
                  for i in range(2)]
    g3["moeacc"] = [palloc(f"moeacc{i}", (128, D), F32, OFF["moeacc"] + 4 * i,
                           "RA") for i in range(4)]
    g3["moebf"] = palloc("moebf", (128, D), BF16, OFF["moebf"], "RA")
    g3["moefm"] = palloc("moefm", (128, D), BF16, OFF["moefm"], "RA")
    g3["baset"] = palloc("baset", (128, D), F32, OFF["baset"], "RA")
    g3["outt"] = [palloc(f"outt{i}", (128, D), F32, OFF[f"outt{i}"], "RA")
                  for i in range(2)]
    g3["gatesf"] = palloc("gatesf", (128, 8), F32, OFF["gatesf"], "RC")
    g3["gtmp"] = palloc("gtmp", (128, 8), F32, OFF["gtmp"], "RC")
    g3["gmsk"] = palloc("gmsk", (128, 8), F32, OFF["gmsk"], "RC")
    g3["gcol"] = palloc("gcol", (128, 1), F32, OFF["gcol"], "RC")
    g3["gcol2"] = palloc("gcol2", (128, 1), F32, OFF["gcol2"], "RC")
    g3["wtok4"] = palloc("wtok4", (128, 32), F32, OFF["wtok4"], "RC")
    g3["wfm4"] = palloc("wfm4", (8, 512), BF16, OFF["wfm4"], "RC")
    g3["aofm"] = palloc("aofm", (128, 8 * SL), BF16, OFF["aofm"], "RB")
    g3["fwTs"] = palloc("fwTs", (128, 8 * D), BF16, OFF["fwTs"], "RB")
    if "p3" in phases:
        phase3(g3)

    if "p3" not in phases:
        for i in range(NT):
            p = i % 2
            dmac("gp", out_h[i * 128:(i + 1) * 128, :], xn32t[p][:],
                 reads=(f"xn32t{p}",), writes=(("outw", i),))

    for name, h in dbg_outs.items():
        src = {"xn32": xn32_d, "qc": qc_d, "kc": kc_d, "vc": vc_d,
               "oglog": oglog_d, "base": base_d, "qr": qr_d, "kr": kr_d,
               "vt": vt_d, "ao": ao_d}.get(name)
        if src is not None:
            reads = {"xn32": tuple(("xn32d", i) for i in range(NT)),
                     "base": tuple(("based", i) for i in range(NT)),
                     "qc": tuple(("qc", oc, s2) for oc in range(8) for s2 in range(NSL)),
                     "kc": tuple(("kc", oc, s2) for oc in range(8) for s2 in range(NSL)),
                     "vc": tuple(("vc", oc, s2) for oc in range(8) for s2 in range(NSL)),
                     "oglog": tuple(("oglog", oc, s2) for oc in range(8) for s2 in range(NSL)),
                     "qr": tuple(("qr", oc, s2) for oc in range(8) for s2 in range(NSL)),
                     "kr": tuple(("kr", oc, s2) for oc in range(8) for s2 in range(NSL)),
                     "vt": tuple(("vt", i) for i in range(NT)),
                     "ao": tuple(("ao", hh, qb) for hh in range(AH) for qb in range(8)),
                     }[name]
            dmac("gp", h, src, reads=reads, writes=((name, "dbgw"),))
        elif name == "absb":
            dmac("gp", h, absb[:], reads=tuple(("absb", i) for i in range(NT)),
                 writes=(("absb", "dbgw"),))

    counters = P.finalize()
    per_engine = {e: [] for e in ENGINES}
    for s_ in P.steps:
        per_engine[s_["engine"]].append(s_)

    with (
        nc.semaphore("sem_pe") as sem_pe,
        nc.semaphore("sem_act") as sem_act,
        nc.semaphore("sem_dve") as sem_dve,
        nc.semaphore("sem_gp") as sem_gp,
        nc.semaphore("sem_sp") as sem_sp,
        nc.semaphore("sem_gp_dma") as sem_gp_dma,
        nc.semaphore("sem_sp_dma") as sem_sp_dma,
        nc.Block() as block,
    ):
        sems = {"pe": sem_pe, "act": sem_act, "dve": sem_dve, "gp": sem_gp,
                "sp": sem_sp, "gp_dma": sem_gp_dma, "sp_dma": sem_sp_dma}

        def run_engine(eng_obj, steps):
            for s_ in steps:
                for key, val in s_["waits"]:
                    eng_obj.wait_ge(sems[key], val)
                inst = s_["emit"](eng_obj)
                if s_["need_sem"]:
                    inst.then_inc(sems[s_["semkey"]], 16 if s_["dma"] else 1)

        @block.tensor
        def _(t):
            run_engine(t, per_engine["pe"])

        @block.scalar
        def _(a):
            run_engine(a, per_engine["act"])

        @block.vector
        def _(v):
            run_engine(v, per_engine["dve"])

        @block.gpsimd
        def _(gg):
            run_engine(gg, per_engine["gp"])
            for key in ("gp_dma", "sp_dma"):
                if counters.get(key):
                    gg.wait_ge(sems[key], counters[key])

        @block.sync
        def _(sy):
            run_engine(sy, per_engine["sp"])

    return nc, list(dbg_outs.keys())


def _prep_inputs(ins):
    f32 = np.float32

    def colchunk(v):
        return np.ascontiguousarray(np.asarray(v, f32).reshape(8, 128).T)

    base = {
        "win": np.ascontiguousarray(np.asarray(ins["L0_attn_W_in"], f32)),
        "wog": np.ascontiguousarray(np.asarray(ins["L0_attn_W_ctrl"], f32)[:, 8:]),
        "wab": np.ascontiguousarray(
            np.asarray(ins["L0_attn_W_ctrl"], f32)[:, :8]
            .reshape(8, 128, 8).transpose(1, 0, 2).reshape(128, 64)),
        "bog": colchunk(np.asarray(ins["L0_attn_b_ctrl"])[8:]),
        "bab": np.asarray(ins["L0_attn_b_ctrl"], f32)[:8].reshape(1, 8),
        "wconv": np.ascontiguousarray(np.stack([
            np.asarray(ins["L0_attn_w_q"], f32).reshape(8, 128, 4).transpose(1, 0, 2),
            np.asarray(ins["L0_attn_w_k"], f32).reshape(8, 128, 4).transpose(1, 0, 2),
            np.asarray(ins["L0_attn_w_v"], f32).reshape(8, 128, 4).transpose(1, 0, 2),
        ], axis=1).reshape(128, 96)),
        "wout": np.ascontiguousarray(np.asarray(ins["L0_attn_W_out"], f32)),
        "bout": colchunk(ins["L0_attn_b_out"]),
        "ramp": colchunk(ins["L0_attn_ramp_logit"]),
        "qwT": np.ascontiguousarray(np.asarray(ins["L0_ffn_q_w"], f32).T),
        "kwT": np.ascontiguousarray(np.asarray(ins["L0_ffn_k_w"], f32).T),
        "vwT": np.ascontiguousarray(np.asarray(ins["L0_ffn_v_w"], f32).T),
        "qb": colchunk(ins["L0_ffn_q_b"]),
        "kb": colchunk(ins["L0_ffn_k_b"]),
        "vbrow": np.asarray(ins["L0_ffn_v_b"], f32).reshape(1, D),
        "gwT": np.ascontiguousarray(
            np.asarray(ins["L0_ffn_gate_w"], f32).T
            .reshape(8, 128, 8).transpose(1, 0, 2).reshape(128, 64)),
        "gbrow": np.asarray(ins["L0_ffn_gate_b"], f32).reshape(1, 8),
        "ewT": np.ascontiguousarray(
            np.asarray(ins["L0_ffn_expert_w"], f32).transpose(0, 2, 1)),
        "ebT": np.ascontiguousarray(
            np.asarray(ins["L0_ffn_expert_b"], f32).T
            .reshape(8, 128, 8).transpose(1, 0, 2).reshape(128, 64)),
        "fwT": np.ascontiguousarray(np.asarray(ins["L0_ffn_ffn_w"], f32).T),
        "fbrow": np.asarray(ins["L0_ffn_ffn_b"], f32).reshape(1, D),
    }
    for k, v in CONSTS.items():
        base[k] = v
    x = np.asarray(ins["L0_attn_x"], np.float32)
    return [dict(base, x=np.ascontiguousarray(x[b])) for b in range(8)]


def kernel(**inputs):
    in_maps = _prep_inputs(inputs)
    nc, _dbg = build_program()
    res = run_bass_kernel_spmd(nc, in_maps, core_ids=list(range(8)))
    return np.stack([r["out"] for r in res.results], axis=0).astype(np.float32)


# version 11
# speedup vs baseline: 1.1576x; 1.1576x over previous
import numpy as np
import ml_dtypes
import concourse.bass as bass
import concourse.mybir as mybir
from concourse.bass_utils import run_bass_kernel_spmd

F32 = mybir.dt.float32
BF16 = mybir.dt.bfloat16
AF = mybir.ActivationFunctionType
ALU = mybir.AluOpType
AX = mybir.AxisListType
NPBF = ml_dtypes.bfloat16

S = 4096
D = 1024
NSL = 8
SL = 512
NT = 32
C = 128
HD = 256
DNH = 4
AH = 16
ADH = 64
NE = 8

ENGINES = ("pe", "act", "dve", "gp", "sp")


class Plan:

    def __init__(self):
        self.steps = []
        self.lastw = {}
        self.readers = {}
        self.region_of = {}
        self.reg_pending = {}
        self.reg_inited = {}

    def set_region(self, res, region):
        self.region_of[res] = region

    @staticmethod
    def _base(res):
        return res if isinstance(res, str) else res[0]

    def reset_region(self, region):
        pend = set()
        for res in list(self.lastw.keys()) + list(self.readers.keys()):
            if self.region_of.get(self._base(res)) != region:
                continue
            if res in self.lastw:
                pend.add(self.lastw.pop(res))
            pend |= self.readers.pop(res, set())
        self.reg_pending[region] = pend
        self.reg_inited[region] = set()

    def step(self, engine, emit, reads=(), writes=(), dma=False):
        idx = len(self.steps)
        deps = set()
        for r in reads:
            if r in self.lastw:
                deps.add(self.lastw[r])
        for w in writes:
            if w in self.lastw:
                deps.add(self.lastw[w])
            deps |= self.readers.get(w, set())
            reg = self.region_of.get(self._base(w))
            if reg is not None and self.reg_pending.get(reg):
                if w not in self.reg_inited[reg]:
                    deps |= self.reg_pending[reg]
                    self.reg_inited[reg].add(w)
        for r in reads:
            self.readers.setdefault(r, set()).add(idx)
        for w in writes:
            self.lastw[w] = idx
            self.readers[w] = {idx}
        deps.discard(idx)
        self.steps.append(dict(engine=engine, emit=emit, deps=deps, dma=dma,
                               need_sem=False))
        return idx

    SELF_SYNC = ("dve", "act", "gp")

    def finalize(self):
        steps = self.steps
        for s in steps:
            for d in s["deps"]:
                p = steps[d]
                if (p["engine"] != s["engine"] or p["dma"]
                        or p["engine"] in self.SELF_SYNC):
                    p["need_sem"] = True
        for s in steps:
            if s["dma"]:
                s["need_sem"] = True
        counters = {}
        for s in steps:
            if s["need_sem"]:
                key = s["engine"] + ("_dma" if s["dma"] else "")
                counters[key] = counters.get(key, 0) + (16 if s["dma"] else 1)
                s["semkey"] = key
                s["semval"] = counters[key]
        wm = {}
        for s in steps:
            need = {}
            for d in s["deps"]:
                p = steps[d]
                if (p["engine"] == s["engine"] and not p["dma"]
                        and p["engine"] not in self.SELF_SYNC):
                    continue
                need[p["semkey"]] = max(need.get(p["semkey"], 0), p["semval"])
            waits = []
            for key, val in sorted(need.items()):
                if wm.get((s["engine"], key), 0) >= val:
                    continue
                wm[(s["engine"], key)] = val
                waits.append((key, val))
            s["waits"] = waits
        return counters


def host_consts():
    c = {}
    c["identb"] = np.eye(128, dtype=NPBF)
    c["identf"] = np.eye(128, dtype=np.float32)
    c["onesf"] = np.ones((1, 128), np.float32)
    c["onescolb"] = np.ones((128, 1), NPBF)
    c["uincf"] = np.triu(np.ones((128, 128), np.float32))
    maskadd = np.where(np.arange(128)[:, None] <= np.arange(128)[None, :],
                       0.0, -1e30).astype(np.float32)
    c["maskaddTf"] = np.ascontiguousarray(maskadd.T)
    c["offdiagf"] = (1.0 - np.eye(128)).astype(np.float32)
    c["offdiag4f"] = np.tile((1.0 - np.eye(128)).astype(np.float32), (1, 4))
    e127 = np.zeros((128, 128), np.float32)
    e127[127, :] = 1.0
    c["e127f"] = e127
    es0 = np.zeros((2, 128), np.float32); es0[0, :] = 1.0
    es1 = np.zeros((2, 128), np.float32); es1[1, :] = 1.0
    c["esel0f"], c["esel1f"] = es0, es1
    c["esel0b"], c["esel1b"] = es0.astype(NPBF), es1.astype(NPBF)
    hs = np.zeros((128, 2), NPBF); hs[:64, 0] = 1; hs[64:, 1] = 1
    c["headsel"] = hs
    bsel = np.zeros((2, 128), np.float32); bsel[0, :64] = 1; bsel[1, 64:] = 1
    c["blockself"] = bsel
    hm = np.zeros((16, 8 * 128), np.float32)
    for ic in range(8):
        hm[2 * ic, ic * 128: ic * 128 + 64] = 1
        hm[2 * ic + 1, ic * 128 + 64: ic * 128 + 128] = 1
    c["headmapf"] = hm
    psw = np.zeros((128, 128), NPBF)
    for a in range(2):
        for dd in range(32):
            psw[a * 64 + dd + 32, a * 64 + dd] = -1.0
            psw[a * 64 + dd, a * 64 + dd + 32] = 1.0
    c["pswapT"] = psw
    inv = 1.0 / 10000.0 ** (np.arange(0, ADH, 2, dtype=np.float64) / ADH)
    ang = np.outer(inv, np.arange(S, dtype=np.float64))
    ang2 = np.concatenate([ang, ang], axis=0)
    cos = np.cos(ang2).astype(np.float32)
    sin = np.sin(ang2).astype(np.float32)
    c["copt"] = np.ascontiguousarray(np.concatenate([cos, cos], axis=0))
    c["sipt"] = np.ascontiguousarray(np.concatenate([sin, sin], axis=0))
    return c


CONSTS = host_consts()


def build_program(debug=(), phases=("p0","p1a","p1b","p2p","p2a","p3")):
    nc = bass.Bass("TRN2", target_bir_lowering=False, debug=False)
    P = Plan()

    def inp(name, shape, dt=F32):
        return nc.dram_tensor(name, list(shape), dt, kind="ExternalInput").ap()

    x_h = inp("x", (S, D))
    win_h = inp("win", (D, 3 * D))
    wog_h = inp("wog", (D, D))
    wab_h = inp("wab", (128, 64))
    bog_h = inp("bog", (128, 8))
    bab_h = inp("bab", (1, 8))
    wconv_h = inp("wconv", (128, 96))
    wout_h = inp("wout", (D, D))
    bout_h = inp("bout", (128, 8))
    ramp_h = inp("ramp", (128, 8))
    qwT_h = inp("qwT", (D, D))
    kwT_h = inp("kwT", (D, D))
    vwT_h = inp("vwT", (D, D))
    qb_h = inp("qb", (128, 8))
    kb_h = inp("kb", (128, 8))
    vb_h = inp("vbrow", (1, D))
    gwT_h = inp("gwT", (128, 64))
    gb_h = inp("gbrow", (1, 8))
    ewT_h = inp("ewT", (NE, D, D))
    ebT_h = inp("ebT", (128, 64))
    fwT_h = inp("fwT", (D, D))
    fb_h = inp("fbrow", (1, D))
    con_h = {k: inp(k, CONSTS[k].shape,
                    BF16 if CONSTS[k].dtype == NPBF else F32)
             for k in CONSTS}
    out_h = nc.dram_tensor("out", [S, D], F32, kind="ExternalOutput").ap()

    def scr(name, shape, dt):
        return nc.dram_tensor(name, list(shape), dt, kind="Internal").ap()

    xn32_d = scr("xn32", (S, D), F32)
    base_d = scr("base", (S, D), F32)
    oglog_d = scr("oglog", (D, S), F32)
    qc_d = scr("qc", (D, S), BF16)
    kc_d = scr("kc", (D, S), BF16)
    vc_d = scr("vc", (D, S), BF16)
    qr_d = scr("qr", (D, S), BF16)
    kr_d = scr("kr", (D, S), BF16)
    vt_d = scr("vt", (S, D), BF16)
    ao_d = scr("ao", (D, S), BF16)
    zs_d = scr("zs", (AH, S), F32)
    readn_d = scr("readn_d", (S, D), BF16)

    dbg_outs = {}
    def dbgout(name, shape, dt):
        if name in debug:
            dbg_outs[name] = nc.dram_tensor("dbg_" + name, list(shape), dt,
                                            kind="ExternalOutput").ap()
    dbgout("xn32", (S, D), F32)
    dbgout("qc", (D, S), F32)
    dbgout("kc", (D, S), F32)
    dbgout("vc", (D, S), F32)
    dbgout("oglog", (D, S), F32)
    dbgout("absb", (128, NT * 8), F32)
    dbgout("base", (S, D), F32)
    dbgout("qr", (D, S), F32)
    dbgout("kr", (D, S), F32)
    dbgout("vt", (S, D), F32)
    dbgout("ao", (D, S), F32)
    dbgout("zs", (AH, S), F32)
    dbgout("readn", (S, D), F32)

    sb = {}

    def sbuf(name, shape, dt):
        t = nc.alloc_sbuf_tensor("s_" + name, list(shape), dt)
        sb[name] = t
        return t

    identb = sbuf("identb", (128, 128), BF16)
    identf = sbuf("identf", (128, 128), F32)
    onesf = sbuf("onesf", (1, 128), F32)
    onescolb = sbuf("onescolb", (128, 1), BF16)
    uincf = sbuf("uincf", (128, 128), F32)
    maskaddT = sbuf("maskaddT", (128, 128), F32)
    offdiag = sbuf("offdiag", (128, 128), F32)
    offdiag4 = sbuf("offdiag4", (128, 512), F32)
    e127 = sbuf("e127", (128, 128), F32)
    esel0 = sbuf("esel0", (2, 128), F32)
    esel1 = sbuf("esel1", (2, 128), F32)
    esel0b = sbuf("esel0b", (2, 128), BF16)
    esel1b = sbuf("esel1b", (2, 128), BF16)
    maccqb = sbuf("maccqb", (2, 8), BF16)
    macckb = sbuf("macckb", (2, 8), BF16)
    headsel = sbuf("headsel", (128, 2), BF16)
    blocksel = sbuf("blocksel", (2, 128), F32)
    headmap = sbuf("headmap", (16, 8 * 128), F32)
    pswapT = sbuf("pswapT", (128, 128), BF16)
    bogc = sbuf("bogc", (128, 8), F32)
    bogc2 = sbuf("bogc2", (128, 8), F32)
    babr = sbuf("babr", (1, 8), F32)
    boutc = sbuf("boutc", (128, 8), F32)
    rampc = sbuf("rampc", (128, 8), F32)
    rampt = sbuf("rampt", (128, 8), F32)
    qbc = sbuf("qbc", (128, 8), F32)
    kbc = sbuf("kbc", (128, 8), F32)
    vbrow = sbuf("vbrow", (1, D), F32)
    vbb = sbuf("vbb", (128, D), F32)
    gbrow = sbuf("gbrow", (1, 8), F32)
    gbb = sbuf("gbb", (128, 8), F32)
    babb = sbuf("babb", (128, 8), F32)
    fbrow = sbuf("fbrow", (1, D), F32)
    fbb = sbuf("fbb", (128, D), F32)
    gwTs = sbuf("gwTs", (128, 64), BF16)
    ebTs = sbuf("ebTs", (128, 64), BF16)
    wabsb = sbuf("wabsb", (128, 64), BF16)
    wconv = sbuf("wconv", (128, 96), F32)
    ebf = sbuf("ebf", (8, D), BF16)
    tails = sbuf("tails", (128, 24 * 3), BF16)
    st32 = sbuf("st32", (128, DNH * 2 * HD), F32)
    stbf = sbuf("stbf", (128, DNH * 2 * HD), BF16)
    absb = sbuf("absb", (128, NT * 8), F32)
    lnst = sbuf("lnst", (128, 8), F32)
    maccq = sbuf("maccq", (2, 8), F32)
    sqt_p = sbuf("sqt_p", (128, SL), BF16)
    rrow_p = sbuf("rrow_p", (1, SL), F32)
    cpe5 = sbuf("cpe5", (128, 1), F32)
    cpe6 = sbuf("cpe6", (128, 1), F32)
    nc.const_aps.register(F32, 1e-5, cpe5[:])
    nc.const_aps.register(F32, 1e-6, cpe6[:])
    macck = sbuf("macck", (2, 8), F32)
    mtmp = sbuf("mtmp", (2, 1), F32)

    pool_base = (nc.sbuf_base + 1023) // 1024 * 1024
    pool_avail = nc.sbuf_bytes_remaining - (pool_base - nc.sbuf_base)
    pool = {}

    avail_pp = pool_avail if pool_avail <= 256 * 1024 else pool_avail // 128

    def palloc(name, shape, dt, off_kb, region):
        nbytes = int(np.prod(shape[1:])) * (2 if dt == BF16 else 4)
        off = pool_base + off_kb * 1024
        assert off_kb * 1024 + nbytes <= avail_pp - 1024, \
            f"pool overflow {name}: off_kb={off_kb} nbytes={nbytes} avail={avail_pp}"
        t = nc.alloc_sbuf_tensor_at("s_" + name, list(shape), dt, offset=off)
        pool[name] = t
        P.set_region(name, region)
        return t

    xnfm = palloc("xnfm", (128, 8 * S), BF16, 0, "RA")
    xt = [palloc(f"xt{i}", (128, D), F32, 64 + 4 * i, "RB") for i in range(2)]
    xbf = [palloc(f"xbf{i}", (128, D), BF16, 72 + 2 * i, "RB") for i in range(2)]
    x2t = palloc("x2t", (128, D), F32, 76, "RB")
    xn32t = [palloc(f"xn32t{i}", (128, D), F32, 80 + 4 * i, "RB") for i in range(2)]
    OFF = dict(
        winsb=64,
        wogsb=112,
        praw0=128, praw1=130, praw2=132, praw3=134,
        convt=136, ogt0=138, ogt1=140,
        qnsl=64, knsl=72, vnsl=80,
        woutsb=88,
        sqt=157, rrow=158,
        ogsl=112,
        qwTs=64, kwTs=80, vwTs=96,
        copt=112, sipt=128,
        qpt0=144, qpt1=145,
        ropet1=146, ropet2=148,
        ropeo=150, vtokt=152, sqa=154,
        khd=64, qhd=72, vslab=80,
        ptl0=88, ptl1=90, aot=92, zbs=96,
        negmh=93, mqcol=94, mkcol=95,
        expd0=0, expd1=16,
        moeacc=32,
        moebf=48, moefm=50, baset=52,
        outt0=56, outt1=60,
        zrec=88,
        gatesf=112, gtmp=113, gmsk=114,
        gcol=115, gcol2=116, wtok4=117, wfm4=118,
        aofm=64, fwTs=72,
    )

    def chunk_tiles4():
        ofs = 128 * 1024
        d = {}

        def ca(name, shape, dt):
            nonlocal ofs
            nbytes = int(np.prod(shape[1:])) * (2 if dt == BF16 else 4)
            t = nc.alloc_sbuf_tensor_at("s_" + name, list(shape), dt,
                                        offset=pool_base + ofs)
            ofs = (ofs + nbytes + 31) // 32 * 32
            pool[name] = t
            P.set_region(name, "RC")
            assert ofs <= 158 * 1024, f"chunk4 overflow {name} at {ofs}"
            return t

        d["gsb"] = ca("gsb", (128, 4), F32)
        d["biasc"] = ca("biasc", (128, 4), F32)
        d["ctc"] = ca("ctc", (128, 4), F32)
        d["clast4"] = ca("clast4", (128, 4), F32)
        d["tiny4"] = ca("tiny4", (128, 12), F32)
        d["growall"] = ca("growall", (1, 512), F32)
        d["dincl4"] = ca("dincl4", (128, 512), F32)
        d["dstr4"] = ca("dstr4", (128, 512), F32)
        d["at4"] = ca("at4", (128, 512), BF16)
        d["wt4"] = ca("wt4", (128, 512), BF16)
        d["an4"] = ca("an4", (128, 512), BF16)
        d["mc4"] = ca("mc4", (128, 1024), F32)
        d["rr4"] = ca("rr4", (128, 1024), BF16)
        d["u04"] = ca("u04", (128, 1024), BF16)
        d["u14"] = ca("u14", (128, 1024), BF16)
        d["ktok4"] = ca("ktok4", (128, 1024), BF16)
        d["pp4a"] = ca("pp4a", (128, 512), BF16)
        d["pp4b"] = ca("pp4b", (128, 512), BF16)
        d["bp4a"] = ca("bp4a", (128, 512), BF16)
        d["bp4b"] = ca("bp4b", (128, 512), BF16)
        d["readf4"] = ca("readf4", (128, 1024), F32)
        return d

    def rd_tiles4():
        t1 = nc.alloc_sbuf_tensor_at("s_readn4", [128, D], BF16,
                                     offset=pool_base + 104 * 1024)
        t2 = nc.alloc_sbuf_tensor_at("s_readfm4", [128, D], BF16,
                                     offset=pool_base + 106 * 1024)
        t3 = nc.alloc_sbuf_tensor_at("s_dnbase", [128, D], F32,
                                     offset=pool_base + 108 * 1024)
        for t, n in ((t1, "readn4"), (t2, "readfm4"), (t3, "dnbase")):
            pool[n] = t
            P.set_region(n, "RB")
        return t1, t2, t3

    pbw = [nc.alloc_psum_tensor(f"pw{i}", [128, 1024], F32) for i in range(2)]

    class BankView:
        def __init__(self, t, coff):
            self.t, self.coff = t, coff

        def __getitem__(self, idx):
            r, c = idx
            cs = c.start or 0
            ce = c.stop if c.stop is not None else 512
            return self.t[r, self.coff + cs: self.coff + ce]

    pbank = [BankView(pbw[0], 0), BankView(pbw[0], 512),
             BankView(pbw[1], 0), BankView(pbw[1], 512),
             nc.alloc_psum_tensor("pb4", [128, 512], F32),
             nc.alloc_psum_tensor("pb5", [128, 512], F32)]
    pbtr = [nc.alloc_psum_tensor(f"pt{i}", [128, 1024], BF16) for i in range(2)]

    def st(engine, emit, reads=(), writes=(), dma=False):
        return P.step(engine, emit, reads=reads, writes=writes, dma=dma)

    def dmac(engine, dst, src, reads, writes):
        return P.step(engine, (lambda e, d_=dst, s_=src: e.dma_start(d_, s_)),
                      reads=reads, writes=writes, dma=True)

    def loads_init():
        for t, h, key in ((identb, con_h["identb"], "identb"),
                          (identf, con_h["identf"], "identf"),
                          (onesf, con_h["onesf"], "onesf"),
                          (onescolb, con_h["onescolb"], "onescolb"),
                          (uincf, con_h["uincf"], "uincf"),
                          (maskaddT, con_h["maskaddTf"], "maskaddT"),
                          (offdiag, con_h["offdiagf"], "offdiag"),
                          (offdiag4, con_h["offdiag4f"], "offdiag4"),
                          (e127, con_h["e127f"], "e127"),
                          (esel0, con_h["esel0f"], "esel0"),
                          (esel1, con_h["esel1f"], "esel1"),
                          (esel0b, con_h["esel0b"], "esel0b"),
                          (esel1b, con_h["esel1b"], "esel1b"),
                          (headsel, con_h["headsel"], "headsel"),
                          (blocksel, con_h["blockself"], "blocksel"),
                          (headmap, con_h["headmapf"], "headmap"),
                          (pswapT, con_h["pswapT"], "pswapT"),
                          (bogc, bog_h, "bogc"), (babr, bab_h, "babr"),
                          (boutc, bout_h, "boutc"),
                          (qbc, qb_h, "qbc"), (kbc, kb_h, "kbc"),
                          (vbrow, vb_h, "vbrow"), (gbrow, gb_h, "gbrow"),
                          (fbrow, fb_h, "fbrow"),
                          (rampt, ramp_h, "rampl")):
            dmac("sp", t[:], h, reads=(), writes=(key,))
        dmac("sp", wconv[:], wconv_h, reads=(), writes=("wconv",))
        dmac("gp", gwTs[:], gwT_h, reads=(), writes=("gwTs",))
        dmac("gp", ebTs[:], ebT_h, reads=(), writes=("ebTs",))
        dmac("gp", wabsb[:], wab_h, reads=(), writes=("wabsb",))
        st("act", lambda e: e.activation(out=rampc[:], in_=rampt[:], func=AF.Tanh,
                                         scale=0.5),
           reads=("rampl",), writes=("rampc",))
        st("dve", lambda e: e.tensor_scalar(rampc[:], rampc[:], 0.5, 0.5,
                                            ALU.mult, ALU.add),
           reads=("rampc",), writes=("rampc",))
        st("dve", lambda e: e.tensor_scalar(bogc2[:], bogc[:], 0.5, None, ALU.mult),
           reads=("bogc",), writes=("bogc2",))
        st("dve", lambda e: e.memset(cpe5[:], 1e-5), reads=(), writes=("constpool",))
        st("dve", lambda e: e.memset(cpe6[:], 1e-6), reads=(), writes=("constpool",))
        st("dve", lambda e: e.memset(maccq[:], 0.0), reads=(), writes=("maccq",))
        st("dve", lambda e: e.memset(macck[:], 0.0), reads=(), writes=("macck",))
        st("dve", lambda e: e.memset(tails[:], 0.0), reads=(), writes=("tails",))

        def bcast_row(dst, row, rowkey, dstkey, n, bank):
            for j in range(0, n, 512):
                w = min(512, n - j)
                st("pe", (lambda e, jj=j, ww=w, rr=row, bb=bank:
                          e.matmul(pbank[bb][:, 0:ww], onesf[:],
                                   rr[0:1, jj:jj + ww], start=True, stop=True)),
                   reads=("onesf", rowkey), writes=(f"pb{bank}",))
                st("dve", (lambda e, dd=dst, jj=j, ww=w, bb=bank:
                           e.tensor_copy(dd[:, jj:jj + ww], pbank[bb][:, 0:ww])),
                   reads=(f"pb{bank}",), writes=(dstkey,))
        bcast_row(vbb, vbrow, "vbrow", "vbb", D, 0)
        bcast_row(fbb, fbrow, "fbrow", "fbb", D, 1)
        bcast_row(gbb, gbrow, "gbrow", "gbb", 8, 2)
        bcast_row(babb, babr, "babr", "babb", 8, 3)
        st("dve", lambda e: e.memset(st32[:], 0.0), reads=(),
           writes=tuple(("st32", h, kc) for h in range(4) for kc in range(2)))
        st("dve", lambda e: e.memset(stbf[:], 0.0), reads=(),
           writes=tuple(("stbf", h, kc) for h in range(4) for kc in range(2)))

    def load_wchunks(dst, dstkey, src_h, n_ic, width):
        for ic in range(n_ic):
            dmac("gp", dst[:, ic * width:(ic + 1) * width],
                 src_h[ic * 128:(ic + 1) * 128, :],
                 reads=(), writes=((dstkey, ic),))

    def phase0():
        for i in range(NT):
            p = i % 2
            xtile, xbtile, xntile = xt[p], xbf[p], xn32t[p]
            kx, kxb, kxn = f"xt{p}", f"xbf{p}", f"xn32t{p}"
            dmac("sp", xtile[:], x_h[i * 128:(i + 1) * 128, :],
                 reads=(), writes=(kx,))
            st("dve", (lambda e, xx=xtile: e.tensor_reduce(
                lnst[:, 0:1], xx[:], AX.X, ALU.add)),
               reads=(kx,), writes=("lnst0",))
            st("dve", lambda e: e.tensor_scalar(
                lnst[:, 1:2], lnst[:, 0:1], -1.0 / D, None, ALU.mult),
               reads=("lnst0",), writes=("lnst1",))
            st("dve", (lambda e, xx=xtile: e.tensor_mul(x2t[:], xx[:], xx[:])),
               reads=(kx,), writes=("x2t",))
            st("dve", lambda e: e.tensor_reduce(
                lnst[:, 2:3], x2t[:], AX.X, ALU.add),
               reads=("x2t",), writes=("lnst2",))
            st("dve", lambda e: e.tensor_scalar(
                lnst[:, 3:4], lnst[:, 2:3], 1.0 / D, None, ALU.mult),
               reads=("lnst2",), writes=("lnst3",))
            st("dve", lambda e: e.tensor_mul(
                lnst[:, 4:5], lnst[:, 1:2], lnst[:, 1:2]),
               reads=("lnst1",), writes=("lnst4",))
            st("dve", lambda e: e.tensor_sub(
                lnst[:, 5:6], lnst[:, 3:4], lnst[:, 4:5]),
               reads=("lnst3", "lnst4"), writes=("lnst5",))
            st("act", lambda e: e.activation(
                out=lnst[:, 6:7], in_=lnst[:, 5:6], func=AF.Ln, bias=1e-5),
               reads=("lnst5", "constpool"), writes=("lnst6",))
            st("act", lambda e: e.activation(
                out=lnst[:, 7:8], in_=lnst[:, 6:7], func=AF.Exp, scale=-0.5),
               reads=("lnst6",), writes=("lnst7",))
            st("dve", (lambda e, xx=xtile, xo=xntile: e.tensor_scalar(
                xo[:], xx[:], lnst[:, 1:2], lnst[:, 7:8], ALU.add, ALU.mult)),
               reads=(kx, "lnst1", "lnst7"), writes=(kxn,))
            dmac("gp", xn32_d[i * 128:(i + 1) * 128, :], xntile[:],
                 reads=(kxn,), writes=(("xn32d", i),))
            st("dve", (lambda e, xx=xtile, xo=xntile: e.tensor_add(
                x2t[:], xx[:], xo[:])),
               reads=(kx, kxn), writes=("x2t",))
            dmac("gp", base_d[i * 128:(i + 1) * 128, :], x2t[:],
                 reads=("x2t",), writes=(("based", i),))
            st("dve", (lambda e, xo=xntile, xb=xbtile: e.tensor_copy(
                xb[:], xo[:])),
               reads=(kxn,), writes=(kxb,))
            for ic in range(8):
                bank = ic % 2
                st("pe", (lambda e, xb=xbtile, icc=ic, bb=bank: e.transpose(
                    pbtr[bb][:, 0:128], xb[:, icc * 128:(icc + 1) * 128],
                    identb[:])),
                   reads=(kxb, "identb"), writes=(f"pt{bank}",))
                st("dve", (lambda e, icc=ic, ii=i, bb=bank: e.tensor_copy(
                    xnfm[:, icc * S + ii * 128: icc * S + (ii + 1) * 128],
                    pbtr[bb][:, 0:128])),
                   reads=(f"pt{bank}",), writes=(("xnfm", ic, i),))

    def phase1a(g):
        winsb, wogsb, praw = g["winsb"], g["wogsb"], g["praw"]
        load_wchunks(winsb, "winsb", win_h, 8, 3 * D)
        load_wchunks(wogsb, "wogsb", wog_h, 8, D)
        for s_ in range(NSL):
            t0 = s_ * SL
            xn_reads = tuple(("xnfm", ic, t) for ic in range(8)
                             for t in range(s_ * 4, s_ * 4 + 4))
            for oc in range(24):
                qkv, occ = oc // 8, oc % 8
                bank = oc % 2
                for ic in range(8):
                    st("pe", (lambda e, oo=oc, ii=ic, bb=bank, tt=t0:
                              e.matmul(pbank[bb][:, :],
                                       winsb[:, ii * 3 * D + oo * 128:
                                             ii * 3 * D + oo * 128 + 128],
                                       xnfm[:, ii * S + tt: ii * S + tt + SL],
                                       start=(ii == 0), stop=(ii == 7))),
                       reads=(("winsb", ic),) + xn_reads,
                       writes=(f"pb{bank}",))
                pr = praw[oc % 4]
                kpr = f"praw{oc % 4}"
                st("dve", (lambda e, pr_=pr, oo=oc: e.tensor_copy(
                    pr_[:, 0:3], tails[:, oo * 3:oo * 3 + 3])),
                   reads=(("tails", oc),), writes=((kpr, "head"),))
                st("act", (lambda e, pr_=pr, bb=bank: e.activation(
                    out=pr_[:, 3:SL + 3], in_=pbank[bb][:, :],
                    func=AF.Identity)),
                   reads=(f"pb{bank}",), writes=(kpr,))
                st("dve", (lambda e, pr_=pr, oo=oc: e.tensor_copy(
                    tails[:, oo * 3:oo * 3 + 3], pr_[:, SL:SL + 3])),
                   reads=(kpr,), writes=(("tails", oc),))
                wc0 = qkv * 32 + occ * 4
                st("dve", (lambda e, pr_=pr, w0=wc0: e.tensor_scalar(
                    convt[:], pr_[:, 0:SL], wconv[:, w0:w0 + 1], None,
                    ALU.mult)),
                   reads=(kpr, (kpr, "head"), "wconv"), writes=("convt",))
                for tap in range(1, 4):
                    st("dve", (lambda e, pr_=pr, wc=wc0 + tap, tp=tap:
                               e.scalar_tensor_tensor(
                                   convt[:], pr_[:, tp:SL + tp],
                                   wconv[:, wc:wc + 1], convt[:],
                                   ALU.mult, ALU.add)),
                       reads=(kpr, (kpr, "head"), "wconv", "convt"),
                       writes=("convt",))
                st("act", lambda e: e.activation(out=convt[:], in_=convt[:],
                                                 func=AF.Silu),
                   reads=("convt",), writes=("convt",))
                dst_d = (qc_d, kc_d, vc_d)[qkv]
                dmac("gp", dst_d[occ * 128:(occ + 1) * 128, t0:t0 + SL],
                     convt[:], reads=("convt",),
                     writes=((("qc", "kc", "vc")[qkv], occ, s_),))
            for oc in range(8):
                bank = 2 + (oc % 2)
                for ic in range(8):
                    st("pe", (lambda e, oo=oc, ii=ic, bb=bank, tt=t0:
                              e.matmul(pbank[bb][:, :],
                                       wogsb[:, ii * D + oo * 128:
                                             ii * D + oo * 128 + 128],
                                       xnfm[:, ii * S + tt: ii * S + tt + SL],
                                       start=(ii == 0), stop=(ii == 7))),
                       reads=(("wogsb", ic),) + xn_reads,
                       writes=(f"pb{bank}",))
                p = oc % 2
                ogt = g["ogt"][p]
                st("act", (lambda e, oo=oc, bb=bank, og=ogt: e.activation(
                    out=og[:], in_=pbank[bb][:, :], func=AF.Tanh,
                    scale=0.5, bias=bogc2[:, oo:oo + 1])),
                   reads=(f"pb{bank}", "bogc2"), writes=(f"ogt{p}",))
                st("dve", (lambda e, og=ogt: e.tensor_scalar(
                    og[:], og[:], 0.5, 0.5, ALU.mult, ALU.add)),
                   reads=(f"ogt{p}",), writes=(f"ogt{p}",))
                dmac("gp", oglog_d[oc * 128:(oc + 1) * 128, t0:t0 + SL],
                     ogt[:], reads=(f"ogt{p}",), writes=(("oglog", oc, s_),))
            for tt in range(4):
                ti = s_ * 4 + tt
                bank = 4
                for ic in range(8):
                    st("pe", (lambda e, ii=ic, tt_=ti, bb=bank: e.matmul(
                        pbank[bb][:, 0:8],
                        xnfm[:, ii * S + tt_ * 128: ii * S + tt_ * 128 + 128],
                        wabsb[:, ii * 8:(ii + 1) * 8],
                        start=(ii == 0), stop=(ii == 7))),
                       reads=xn_reads + ("wabsb",), writes=(f"pb{bank}",))
                st("dve", (lambda e, tt_=ti, bb=bank: e.tensor_add(
                    absb[:, tt_ * 8:(tt_ + 1) * 8], pbank[bb][:, 0:8],
                    babb[:, 0:8])),
                   reads=(f"pb{bank}", "babb"), writes=(("absb", ti),))
                st("act", (lambda e, tt_=ti: e.activation(
                    out=absb[:, tt_ * 8:(tt_ + 1) * 8],
                    in_=absb[:, tt_ * 8:(tt_ + 1) * 8], func=AF.Tanh,
                    scale=0.5)),
                   reads=(("absb", ti),), writes=(("absb", ti),))
                st("dve", (lambda e, tt_=ti: e.tensor_scalar(
                    absb[:, tt_ * 8:tt_ * 8 + 4],
                    absb[:, tt_ * 8:tt_ * 8 + 4], 0.49, 0.5,
                    ALU.mult, ALU.add)),
                   reads=(("absb", ti),), writes=(("absb", ti),))
                st("dve", (lambda e, tt_=ti: e.tensor_scalar(
                    absb[:, tt_ * 8 + 4:tt_ * 8 + 8],
                    absb[:, tt_ * 8 + 4:tt_ * 8 + 8], 0.5, 0.5,
                    ALU.mult, ALU.add)),
                   reads=(("absb", ti),), writes=(("absb", ti),))

    def phase1b(g):
        woutsb = g["woutsb"]
        qnsl, knsl, vnsl, ogsl = g["qnsl"], g["knsl"], g["vnsl"], g["ogsl"]
        sqtl, rrow = g["sqt"], g["rrow"]
        d = cht4
        rn, rfm, dnb = readn4_t, readfm4_t, dnbase_t
        load_wchunks(woutsb, "woutsb", wout_h, 8, D)
        PW0 = ("pb0", "pb1")
        PW1 = ("pb2", "pb3")
        for s_ in range(NSL):
            t0 = s_ * SL
            for oc in range(8):
                dmac("sp", qnsl[:, oc * SL:(oc + 1) * SL],
                     qc_d[oc * 128:(oc + 1) * 128, t0:t0 + SL],
                     reads=(("qc", oc, s_), ("qnsl", oc)), writes=(("qnsl", oc),))
                dmac("sp", knsl[:, oc * SL:(oc + 1) * SL],
                     kc_d[oc * 128:(oc + 1) * 128, t0:t0 + SL],
                     reads=(("kc", oc, s_), ("knsl", oc)), writes=(("knsl", oc),))
                dmac("sp", vnsl[:, oc * SL:(oc + 1) * SL],
                     vc_d[oc * 128:(oc + 1) * 128, t0:t0 + SL],
                     reads=(("vc", oc, s_), ("vnsl", oc)), writes=(("vnsl", oc),))
                dmac("sp", ogsl[:, oc * SL:(oc + 1) * SL],
                     oglog_d[oc * 128:(oc + 1) * 128, t0:t0 + SL],
                     reads=(("oglog", oc, s_), ("ogsl", oc)), writes=(("ogsl", oc),))
            for which, tile in (("q", qnsl), ("k", knsl)):
                kk = "qnsl" if which == "q" else "knsl"
                for h in range(4):
                    bank = h
                    for half in range(2):
                        oc = 2 * h + half
                        st("dve", (lambda e, tl=tile, oo=oc: e.tensor_mul(
                            sqtl[:], tl[:, oo * SL:(oo + 1) * SL],
                            tl[:, oo * SL:(oo + 1) * SL])),
                           reads=((kk, oc),), writes=("sqt",))
                        st("pe", (lambda e, hf=half, bb=bank: e.matmul(
                            pbank[bb][0:1, 0:SL], onescolb[:], sqtl[:],
                            start=(hf == 0), stop=(hf == 1))),
                           reads=("sqt", "onescolb"), writes=(f"pb{bank}",))
                    st("dve", (lambda e, bb=bank: e.tensor_scalar(
                        rrow[0:1, 0:SL], pbank[bb][0:1, 0:SL],
                        1e-12, None, ALU.max)),
                       reads=(f"pb{bank}",), writes=("rrow",))
                    st("act", lambda e: e.activation(
                        out=rrow[0:1, 0:SL], in_=rrow[0:1, 0:SL], func=AF.Ln),
                       reads=("rrow",), writes=("rrow",))
                    st("act", lambda e: e.activation(
                        out=rrow[0:1, 0:SL], in_=rrow[0:1, 0:SL],
                        func=AF.Exp, scale=-0.5),
                       reads=("rrow",), writes=("rrow",))
                    st("dve", lambda e: e.tensor_scalar(
                        rrow[0:1, 0:SL], rrow[0:1, 0:SL], 1e6, None, ALU.min),
                       reads=("rrow",), writes=("rrow",))
                    bank2 = 4
                    st("pe", (lambda e, bb=bank2: e.matmul(
                        pbank[bb][:, 0:SL], onesf[:], rrow[0:1, 0:SL],
                        start=True, stop=True)),
                       reads=("rrow", "onesf"), writes=(f"pb{bank2}",))
                    for half in range(2):
                        oc = 2 * h + half
                        st("dve", (lambda e, tl=tile, oo=oc, bb=bank2:
                                   e.tensor_mul(
                                       tl[:, oo * SL:(oo + 1) * SL],
                                       tl[:, oo * SL:(oo + 1) * SL],
                                       pbank[bb][:, 0:SL])),
                           reads=((kk, oc), f"pb{bank2}"), writes=((kk, oc),))
            for tt in range(4):
                ti = s_ * 4 + tt
                toff = tt * 128

                def qsl(h, half):
                    oc = 2 * h + half
                    return qnsl[:, oc * SL + toff: oc * SL + toff + 128]

                def ksl(h, half):
                    oc = 2 * h + half
                    return knsl[:, oc * SL + toff: oc * SL + toff + 128]

                def vsl(h, half):
                    oc = 2 * h + half
                    return vnsl[:, oc * SL + toff: oc * SL + toff + 128]

                kq_r = tuple(("qnsl", oc) for oc in range(8))
                kk_r = tuple(("knsl", oc) for oc in range(8))
                kv_r = tuple(("vnsl", oc) for oc in range(8))
                st("act", (lambda e, tt_=ti: e.activation(
                    out=d["gsb"][:, 0:4], in_=absb[:, tt_ * 8:tt_ * 8 + 4],
                    func=AF.Ln)),
                   reads=(("absb", ti),), writes=("gsb",))
                st("act", (lambda e, tt_=ti: e.activation(
                    out=d["biasc"][:, 0:4],
                    in_=absb[:, tt_ * 8 + 4:tt_ * 8 + 8], func=AF.Ln)),
                   reads=(("absb", ti),), writes=("biasc",))
                st("pe", lambda e: e.matmul(
                    pbank[5][:, 0:4], uincf[:], d["gsb"][:, 0:4],
                    start=True, stop=True),
                   reads=("gsb", "uincf"), writes=("pb5",))
                st("dve", lambda e: e.tensor_copy(d["gsb"][:, 0:4],
                                                  pbank[5][:, 0:4]),
                   reads=("pb5",), writes=("gsb",))
                st("dve", lambda e: e.tensor_sub(
                    d["biasc"][:, 0:4], d["biasc"][:, 0:4], d["gsb"][:, 0:4]),
                   reads=("biasc", "gsb"), writes=("biasc",))
                st("act", lambda e: e.activation(
                    out=d["ctc"][:, 0:4], in_=d["gsb"][:, 0:4], func=AF.Exp),
                   reads=("gsb",), writes=("ctc",))
                st("pe", lambda e: e.matmul(
                    pbank[5][:, 4:8], e127[:], d["ctc"][:, 0:4],
                    start=True, stop=True),
                   reads=("ctc", "e127"), writes=("pb5",))
                st("dve", lambda e: e.tensor_copy(d["clast4"][:, 0:4],
                                                  pbank[5][:, 4:8]),
                   reads=("pb5",), writes=("clast4",))
                for h in range(4):
                    st("pe", (lambda e, hh=h: e.matmul(
                        pbank[5][0:1, hh * 128:(hh + 1) * 128],
                        d["gsb"][:, hh:hh + 1], identf[:],
                        start=True, stop=True)),
                       reads=("gsb", "identf"), writes=("pb5",))
                st("dve", lambda e: e.tensor_copy(d["growall"][0:1, 0:512],
                                                  pbank[5][0:1, 0:512]),
                   reads=("pb5",), writes=("growall",))
                for h in range(4):
                    st("pe", (lambda e, hh=h: e.matmul(
                        pbank[4][:, hh * 128:(hh + 1) * 128], onesf[:],
                        d["growall"][0:1, hh * 128:(hh + 1) * 128],
                        start=True, stop=False)),
                       reads=("growall", "onesf"), writes=("pb4",))
                    st("pe", (lambda e, hh=h: e.matmul(
                        pbank[4][:, hh * 128:(hh + 1) * 128], maskaddT[:],
                        identf[:], start=False, stop=True)),
                       reads=("maskaddT", "identf"), writes=("pb4",))
                for h in range(4):
                    st("act", (lambda e, hh=h: e.activation(
                        out=d["dincl4"][:, hh * 128:(hh + 1) * 128],
                        in_=pbank[4][:, hh * 128:(hh + 1) * 128], func=AF.Exp,
                        bias=d["biasc"][:, hh:hh + 1])),
                       reads=("pb4", "biasc"), writes=("dincl4",))
                st("dve", lambda e: e.tensor_mul(d["dstr4"][:], d["dincl4"][:],
                                                 offdiag4[:]),
                   reads=("dincl4", "offdiag4"), writes=("dstr4",))
                for h in range(4):
                    for half in range(2):
                        st("pe", (lambda e, hh=h, hf=half, ka=ksl(h, half):
                                  e.matmul(
                                      pbw[0][:, hh * 128:(hh + 1) * 128],
                                      ka, ka,
                                      start=(hf == 0), stop=(hf == 1))),
                           reads=kk_r, writes=PW0)
                st("dve", lambda e: e.tensor_mul(
                    d["at4"][:], pbw[0][:, 0:512], d["dstr4"][:]),
                   reads=PW0 + ("dstr4",), writes=("at4",))
                for h in range(4):
                    for half in range(2):
                        st("pe", (lambda e, hh=h, hf=half, ka=ksl(h, half),
                                  qa=qsl(h, half): e.matmul(
                                      pbw[1][:, hh * 128:(hh + 1) * 128],
                                      ka, qa,
                                      start=(hf == 0), stop=(hf == 1))),
                           reads=kk_r + kq_r, writes=PW1)
                st("dve", lambda e: e.tensor_mul(
                    d["wt4"][:], pbw[1][:, 0:512], d["dincl4"][:]),
                   reads=PW1 + ("dincl4",), writes=("wt4",))
                for h in range(4):
                    st("pe", (lambda e, hh=h: e.transpose(
                        pbtr[0][:, hh * 128:(hh + 1) * 128],
                        d["at4"][:, hh * 128:(hh + 1) * 128], identb[:])),
                       reads=("at4", "identb"), writes=("pt0",))
                st("act", lambda e: e.activation(
                    out=d["an4"][:], in_=pbtr[0][:, 0:512], func=AF.Identity),
                   reads=("pt0",), writes=("an4",))
                for h in range(4):
                    for half in range(2):
                        st("pe", (lambda e, hh=h, hf=half, ka=ksl(h, half):
                                  e.matmul(
                                      pbw[1][:, hh * 256: hh * 256 + 256],
                                      ka,
                                      stbf[:, (2 * hh + hf) * 256:
                                           (2 * hh + hf) * 256 + 256],
                                      start=(hf == 0), stop=(hf == 1))),
                           reads=kk_r + (("stbf", h, half),), writes=PW1)
                for h in range(4):
                    st("dve", (lambda e, hh=h: e.tensor_scalar(
                        d["mc4"][:, hh * 256: hh * 256 + 256],
                        pbw[1][:, hh * 256: hh * 256 + 256],
                        d["ctc"][:, hh:hh + 1], None, ALU.mult)),
                       reads=PW1 + ("ctc",), writes=("mc4",))
                for h in range(4):
                    for half in range(2):
                        st("pe", (lambda e, hh=h, hf=half, va=vsl(h, half):
                                  e.transpose(
                                      pbtr[1][:, (2 * hh + hf) * 128:
                                              (2 * hh + hf) * 128 + 128],
                                      va, identb[:])),
                           reads=kv_r + ("identb",), writes=("pt1",))
                st("dve", lambda e: e.tensor_sub(
                    d["rr4"][:], pbtr[1][:, 0:1024], d["mc4"][:]),
                   reads=("pt1", "mc4"), writes=("rr4",))
                for h in range(4):
                    st("pe", (lambda e, hh=h: e.matmul(
                        pbw[0][:, hh * 256: hh * 256 + 256],
                        d["at4"][:, hh * 128:(hh + 1) * 128],
                        d["rr4"][:, hh * 256: hh * 256 + 256],
                        start=True, stop=True)),
                       reads=("at4", "rr4"), writes=PW0)
                st("dve", lambda e: e.tensor_sub(
                    d["u04"][:], d["rr4"][:], pbw[0][:, 0:1024]),
                   reads=("rr4",) + PW0, writes=("u04",))
                cur_p, cur_b, cur_u = "at4", "an4", "u04"
                for lev, pwr in enumerate((2, 4, 8, 16, 32, 64)):
                    new_p = ("pp4a", "pp4b")[lev % 2]
                    new_b = ("bp4a", "bp4b")[lev % 2]
                    for h in range(4):
                        if pwr < 64:
                            st("pe", (lambda e, hh=h, cp=cur_p, cb=cur_b:
                                      e.matmul(
                                          pbw[1][:, hh * 128:(hh + 1) * 128],
                                          d[cp][:, hh * 128:(hh + 1) * 128],
                                          d[cb][:, hh * 128:(hh + 1) * 128],
                                          start=True, stop=True)),
                               reads=(cur_p, cur_b), writes=PW1)
                        st("pe", (lambda e, hh=h, cp=cur_p, cb=cur_b: e.matmul(
                            pbw[1][:, 512 + hh * 128: 512 + (hh + 1) * 128],
                            d[cb][:, hh * 128:(hh + 1) * 128],
                            d[cp][:, hh * 128:(hh + 1) * 128],
                            start=True, stop=True)),
                           reads=(cur_p, cur_b), writes=PW1)
                    if pwr < 64:
                        st("act", (lambda e, nb=new_b: e.activation(
                            out=d[nb][:], in_=pbw[1][:, 0:512],
                            func=AF.Identity)),
                           reads=PW1, writes=(new_b,))
                    st("act", (lambda e, np_=new_p: e.activation(
                        out=d[np_][:], in_=pbw[1][:, 512:1024],
                        func=AF.Identity)),
                       reads=PW1, writes=(new_p,))
                    for h in range(4):
                        st("pe", (lambda e, hh=h, np_=new_p, cu=cur_u: e.matmul(
                            pbw[0][:, hh * 256: hh * 256 + 256],
                            d[np_][:, hh * 128:(hh + 1) * 128],
                            d[cu][:, hh * 256: hh * 256 + 256],
                            start=True, stop=True)),
                           reads=(new_p, cur_u), writes=PW0)
                    nxt_u = "u14" if cur_u == "u04" else "u04"
                    st("dve", (lambda e, cu=cur_u, nu=nxt_u: e.tensor_add(
                        d[nu][:], d[cu][:], pbw[0][:, 0:1024])),
                       reads=(cur_u,) + PW0, writes=(nxt_u,))
                    cur_p, cur_b, cur_u = new_p, new_b, nxt_u
                for h in range(4):
                    for half in range(2):
                        st("pe", (lambda e, hh=h, hf=half, qa=qsl(h, half):
                                  e.matmul(
                                      pbw[1][:, hh * 256: hh * 256 + 256],
                                      qa,
                                      stbf[:, (2 * hh + hf) * 256:
                                           (2 * hh + hf) * 256 + 256],
                                      start=(hf == 0), stop=(hf == 1))),
                           reads=kq_r + (("stbf", h, half),), writes=PW1)
                for h in range(4):
                    st("dve", (lambda e, hh=h: e.tensor_scalar(
                        d["readf4"][:, hh * 256: hh * 256 + 256],
                        pbw[1][:, hh * 256: hh * 256 + 256],
                        d["ctc"][:, hh:hh + 1], None, ALU.mult)),
                       reads=PW1 + ("ctc",), writes=("readf4",))
                for h in range(4):
                    st("pe", (lambda e, hh=h, cu=cur_u: e.matmul(
                        pbw[0][:, hh * 256: hh * 256 + 256],
                        d["wt4"][:, hh * 128:(hh + 1) * 128],
                        d[cu][:, hh * 256: hh * 256 + 256],
                        start=True, stop=True)),
                       reads=("wt4", cur_u), writes=PW0)
                st("dve", lambda e: e.tensor_add(
                    d["readf4"][:], d["readf4"][:], pbw[0][:, 0:1024]),
                   reads=("readf4",) + PW0, writes=("readf4",))
                for h in range(4):
                    st("act", (lambda e, hh=h: e.activation(
                        out=d["mc4"][:, hh * 256: hh * 256 + 256],
                        in_=d["readf4"][:, hh * 256: hh * 256 + 256],
                        func=AF.Square, accum_out=d["tiny4"][:, hh:hh + 1])),
                       reads=("readf4",), writes=("mc4", "tiny4"))
                st("act", lambda e: e.activation(
                    out=d["tiny4"][:, 4:8], in_=d["tiny4"][:, 0:4], func=AF.Ln,
                    scale=1.0 / 256, bias=1e-6),
                   reads=("tiny4", "constpool"), writes=("tiny4",))
                st("act", lambda e: e.activation(
                    out=d["tiny4"][:, 8:12], in_=d["tiny4"][:, 4:8],
                    func=AF.Exp, scale=-0.5),
                   reads=("tiny4",), writes=("tiny4",))
                for h in range(4):
                    st("dve", (lambda e, hh=h: e.tensor_scalar(
                        rn[:, hh * 256:(hh + 1) * 256],
                        d["readf4"][:, hh * 256: hh * 256 + 256],
                        d["tiny4"][:, 8 + hh:9 + hh], None, ALU.mult)),
                       reads=("readf4", "tiny4"), writes=(("readn4", h),))
                for h in range(4):
                    for half in range(2):
                        st("pe", (lambda e, hh=h, hf=half, ka=ksl(h, half):
                                  e.transpose(
                                      pbtr[1][:, (2 * hh + hf) * 128:
                                              (2 * hh + hf) * 128 + 128],
                                      ka, identb[:])),
                           reads=kk_r + ("identb",), writes=("pt1",))
                for h in range(4):
                    st("dve", (lambda e, hh=h: e.tensor_scalar(
                        d["ktok4"][:, hh * 256: hh * 256 + 256],
                        pbtr[1][:, hh * 256: hh * 256 + 256],
                        d["dincl4"][:, hh * 128 + 127: hh * 128 + 128],
                        None, ALU.mult)),
                       reads=("pt1", "dincl4"), writes=("ktok4",))
                for h in range(4):
                    for kc in range(2):
                        dstp = (pbw[0] if h < 2 else pbw[1])
                        ko = ((2 * h + kc) % 4) * 256
                        st("pe", (lambda e, hh=h, cc=kc, cu=cur_u, dd=dstp,
                                  oo=ko: e.matmul(
                                      dd[:, oo: oo + 256],
                                      d["ktok4"][:, (2 * hh + cc) * 128:
                                                 (2 * hh + cc) * 128 + 128],
                                      d[cu][:, hh * 256: hh * 256 + 256],
                                      start=True, stop=True)),
                           reads=("ktok4", cur_u),
                           writes=(PW0 if h < 2 else PW1))
                for h in range(4):
                    for kc in range(2):
                        srcp = (pbw[0] if h < 2 else pbw[1])
                        ko = ((2 * h + kc) % 4) * 256
                        sl32 = st32[:, (2 * h + kc) * 256:(2 * h + kc) * 256 + 256]
                        st("dve", (lambda e, hh=h, s3=sl32, sp=srcp, oo=ko:
                                   e.scalar_tensor_tensor(
                                       s3, s3, d["clast4"][:, hh:hh + 1],
                                       sp[:, oo: oo + 256], ALU.mult, ALU.add)),
                           reads=(("st32", h, kc), "clast4") +
                                 (PW0 if h < 2 else PW1),
                           writes=(("st32", h, kc),))
                for half in range(2):
                    st("act", (lambda e, hf=half: e.activation(
                        out=stbf[:, hf * 1024:(hf + 1) * 1024],
                        in_=st32[:, hf * 1024:(hf + 1) * 1024],
                        func=AF.Identity)),
                       reads=tuple(("st32", h2, kc2) for h2 in range(4)
                                   for kc2 in range(2)),
                       writes=tuple(("stbf", h2, kc2)
                                    for h2 in range(2 * half, 2 * half + 2)
                                    for kc2 in range(2)))
                if "readn" in dbg_outs:
                    dmac("gp", readn_d[ti * 128:(ti + 1) * 128, :], rn[:],
                         reads=tuple(("readn4", h2) for h2 in range(4)),
                         writes=(("readnd", ti),))
                _chunk_tail4(ti, tt, woutsb, ogsl, rn, rfm, dnb, d)

    def _chunk_tail4(ti, tt, woutsb, ogsl, rn, rfm, dnb, d):
        krn = tuple(("readn4", h) for h in range(4))
        for vc in range(8):
            bank = vc % 2
            st("pe", (lambda e, v=vc, bb=bank: e.transpose(
                pbtr[bb][:, (v // 2) * 128:(v // 2) * 128 + 128],
                rn[:, v * 128:(v + 1) * 128], identb[:])),
               reads=krn + ("identb",), writes=(f"pt{bank}",))
        for bank in range(2):
            st("dve", (lambda e, bb=bank: e.tensor_copy(
                rfm[:, bb * 512:(bb + 1) * 512], pbtr[bb][:, 0:512])),
               reads=(f"pt{bank}",), writes=(("readfm4", bank),))
        dmac("sp", dnb[:], base_d[ti * 128:(ti + 1) * 128, :],
             reads=(("based", ti), ("dnbase", "out")),
             writes=(("dnbase", "sum"),))
        for oc in range(8):
            bank = 4
            for vc in range(8):
                blk = (vc % 2) * 512 + (vc // 2) * 128
                st("pe", (lambda e, oo=oc, v=vc, bb=bank, bl=blk: e.matmul(
                    pbank[bb][:, 0:128],
                    woutsb[:, v * D + oo * 128: v * D + oo * 128 + 128],
                    rfm[:, bl: bl + 128],
                    start=(v == 0), stop=(v == 7))),
                   reads=(("woutsb", vc), ("readfm4", vc % 2)),
                   writes=(f"pb{bank}",))
            ogslice = ogsl[:, oc * SL + tt * 128: oc * SL + tt * 128 + 128]
            st("dve", (lambda e, oo=oc, bb=bank, og=ogslice:
                       e.scalar_tensor_tensor(
                           d["mc4"][:, 0:128], pbank[bb][:, 0:128],
                           boutc[:, oo:oo + 1], og, ALU.add, ALU.mult)),
               reads=(f"pb{bank}", "boutc", ("ogsl", oc), "mc4"),
               writes=("mc4",))
            st("dve", (lambda e, oo=oc: e.tensor_scalar(
                d["mc4"][:, 128:256], d["mc4"][:, 0:128],
                rampc[:, oo:oo + 1], None, ALU.mult)),
               reads=("mc4", "rampc"), writes=("mc4",))
            st("dve", lambda e: e.tensor_copy(d["rr4"][:, 0:128],
                                              d["mc4"][:, 128:256]),
               reads=("mc4", "rr4"), writes=("rr4",))
            bank2 = oc % 2
            st("pe", (lambda e, bb=bank2: e.transpose(
                pbtr[bb][:, 512:640], d["rr4"][:, 0:128], identb[:])),
               reads=("rr4", "identb"), writes=(f"pt{bank2}",))
            st("dve", (lambda e, oo=oc, bb=bank2, b_=dnb: e.tensor_add(
                b_[:, oo * 128:(oo + 1) * 128], b_[:, oo * 128:(oo + 1) * 128],
                pbtr[bb][:, 512:640])),
               reads=(f"pt{bank2}", ("dnbase", "sum")),
               writes=(("dnbase", "sum"),))
        dmac("gp", base_d[ti * 128:(ti + 1) * 128, :], dnb[:],
             reads=(("dnbase", "sum"),),
             writes=(("based", ti), ("dnbase", "out")))

    def phase2proj(g):
        qwTs, kwTs, vwTs = g["qwTs"], g["kwTs"], g["vwTs"]
        copt, sipt = g["copt"], g["sipt"]
        qpt, ropet1, ropet2, ropeo = g["qpt"], g["ropet1"], g["ropet2"], g["ropeo"]
        vtokt, sqa = g["vtokt"], g["sqa"]
        load_wchunks(qwTs, "qwTs", qwT_h, 8, D)
        load_wchunks(kwTs, "kwTs", kwT_h, 8, D)
        load_wchunks(vwTs, "vwTs", vwT_h, 8, D)
        dmac("sp", copt[:], con_h["copt"], reads=(), writes=("copt",))
        dmac("sp", sipt[:], con_h["sipt"], reads=(), writes=("sipt",))
        for s_ in range(NSL):
            t0 = s_ * SL
            xn_reads = tuple(("xnfm", ic, t) for ic in range(8)
                             for t in range(s_ * 4, s_ * 4 + 4))
            for which, wts, wkey, bias, bkey, dst_d, dkey, macc, mkey in (
                    ("q", qwTs, "qwTs", qbc, "qbc", qr_d, "qr", maccq, "maccq"),
                    ("k", kwTs, "kwTs", kbc, "kbc", kr_d, "kr", macck, "macck")):
                for oc in range(8):
                    bank = oc % 2
                    for ic in range(8):
                        st("pe", (lambda e, oo=oc, ii=ic, bb=bank, tt=t0, w=wts:
                                  e.matmul(pbank[bb][:, :],
                                           w[:, ii * D + oo * 128:
                                             ii * D + oo * 128 + 128],
                                           xnfm[:, ii * S + tt: ii * S + tt + SL],
                                           start=(ii == 0), stop=(ii == 7))),
                           reads=((wkey, ic),) + xn_reads, writes=(f"pb{bank}",))
                    pq = oc % 2
                    qt = qpt[pq]
                    st("act", (lambda e, oo=oc, bb=bank, q_=qt, b_=bias:
                               e.activation(out=q_[:], in_=pbank[bb][:, :],
                                            func=AF.Identity,
                                            bias=b_[:, oo:oo + 1])),
                       reads=(f"pb{bank}", bkey), writes=((f"qpt{pq}",),))
                    st("dve", (lambda e, q_=qt: e.tensor_mul(
                        sqa[:], q_[:], q_[:])),
                       reads=((f"qpt{pq}",),), writes=("sqa",))
                    st("pe", (lambda e: e.matmul(
                        pbank[2][0:2, 0:SL], headsel[:], sqa[:],
                        start=True, stop=True)),
                       reads=("sqa", "headsel"), writes=("pb2",))
                    st("dve", lambda e: e.tensor_reduce(
                        mtmp[:], pbank[2][0:2, 0:SL], AX.X, ALU.max),
                       reads=("pb2",), writes=("mtmp",))
                    st("dve", (lambda e, m_=macc, oo=oc: e.tensor_max(
                        m_[:, oo:oo + 1], m_[:, oo:oo + 1], mtmp[:])),
                       reads=("mtmp", mkey), writes=(mkey,))
                    st("pe", (lambda e, q_=qt: e.matmul(
                        pbank[2][:, 0:SL], pswapT[:], q_[:],
                        start=True, stop=True)),
                       reads=((f"qpt{pq}",), "pswapT"), writes=("pb2",))
                    st("dve", (lambda e, q_=qt, tt=t0: e.tensor_mul(
                        ropet1[:], q_[:], copt[:, tt:tt + SL])),
                       reads=((f"qpt{pq}",), "copt"), writes=("ropet1",))
                    st("dve", (lambda e, tt=t0: e.tensor_mul(
                        ropet2[:], pbank[2][:, 0:SL], sipt[:, tt:tt + SL])),
                       reads=("pb2", "sipt"), writes=("ropet2",))
                    st("dve", lambda e: e.tensor_add(
                        ropeo[:], ropet1[:], ropet2[:]),
                       reads=("ropet1", "ropet2"), writes=("ropeo",))
                    dmac("gp", dst_d[oc * 128:(oc + 1) * 128, t0:t0 + SL],
                         ropeo[:], reads=("ropeo",), writes=((dkey, oc, s_),))
            for u in range(4):
                ti = s_ * 4 + u
                for half in range(2):
                    bank = 2 + half
                    for ic in range(8):
                        st("pe", (lambda e, ii=ic, hf=half, bb=bank, tt_=ti:
                                  e.matmul(pbank[bb][:, :],
                                           xnfm[:, ii * S + tt_ * 128:
                                                ii * S + tt_ * 128 + 128],
                                           vwTs[:, ii * D + hf * 512:
                                                ii * D + hf * 512 + 512],
                                           start=(ii == 0), stop=(ii == 7))),
                           reads=(("vwTs", ic),) + xn_reads,
                           writes=(f"pb{bank}",))
                    st("dve", (lambda e, hf=half, bb=2 + half: e.tensor_add(
                        vtokt[:, hf * 512:hf * 512 + 512], pbank[bb][:, :],
                        vbb[:, hf * 512:hf * 512 + 512])),
                       reads=(f"pb{2 + half}", "vbb"), writes=(("vtokt", half),))
                dmac("gp", vt_d[ti * 128:(ti + 1) * 128, :], vtokt[:],
                     reads=(("vtokt", 0), ("vtokt", 1)), writes=(("vt", ti),))

    def phase2attn(g):
        khd, qhd, vslab = g["khd"], g["qhd"], g["vslab"]
        ptl = g["ptl"]
        aot = g["aot"]
        zsb = g["zsb"]
        negm, mqcol, mkcol = g["negmh"], g["mqcol"], g["mkcol"]
        st("dve", lambda e: e.memset(khd[64:128, :], 0.0),
           reads=(), writes=("khd",))
        st("dve", lambda e: e.memset(qhd[64:128, :], 0.0),
           reads=(), writes=("qhd",))
        st("dve", lambda e: e.memset(vslab[:], 0.0),
           reads=(), writes=(("vslab", "ones"),))
        st("dve", lambda e: e.memset(
            vslab[:].rearrange("p (kt c) -> p kt c", c=128)[:, :, 64:65], 1.0),
           reads=(("vslab", "ones"),), writes=(("vslab", "ones"),))
        st("dve", lambda e: e.tensor_copy(maccqb[:], maccq[:]),
           reads=("maccq",), writes=("maccqb",))
        st("dve", lambda e: e.tensor_copy(macckb[:], macck[:]),
           reads=("macck",), writes=("macckb",))
        vt_view = vt_d.rearrange("(kt p) v -> p kt v", p=128)
        for h in range(AH):
            dmac("sp", khd[0:64, :], kr_d[h * 64:(h + 1) * 64, :],
                 reads=tuple(("kr", 2 * (h // 2) + hh, s2) for hh in range(2)
                             for s2 in range(NSL)) + ("khd",),
                 writes=("khd",))
            dmac("sp", qhd[0:64, :], qr_d[h * 64:(h + 1) * 64, :],
                 reads=tuple(("qr", 2 * (h // 2) + hh, s2) for hh in range(2)
                             for s2 in range(NSL)) + ("qhd",),
                 writes=("qhd",))
            for g4 in range(4):
                dmac("sp",
                     vslab[:].rearrange("p (kt c) -> p kt c", c=128)
                     [:, g4 * 8:(g4 + 1) * 8, 0:64],
                     vt_view[:, g4 * 8:(g4 + 1) * 8, h * 64:(h + 1) * 64],
                     reads=tuple(("vt", tti) for tti in range(NT)) +
                           (("vslab", "war"),),
                     writes=(("vslab", g4),))
            hm, hs_ = (esel0b, "esel0b") if h % 2 == 0 else (esel1b, "esel1b")
            st("pe", (lambda e, sel=hm, hh=h: e.matmul(
                pbank[5][:, 0:1], sel[:], maccqb[:, hh // 2:hh // 2 + 1],
                start=True, stop=True)),
               reads=(hs_, "maccqb"), writes=("pb5",))
            st("act", lambda e: e.activation(out=mqcol[:], in_=pbank[5][:, 0:1],
                                             func=AF.Ln),
               reads=("pb5",), writes=("mqcol",))
            st("act", lambda e: e.activation(out=mqcol[:], in_=mqcol[:],
                                             func=AF.Exp, scale=0.5),
               reads=("mqcol",), writes=("mqcol",))
            st("pe", (lambda e, sel=hm, hh=h: e.matmul(
                pbank[5][:, 0:1], sel[:], macckb[:, hh // 2:hh // 2 + 1],
                start=True, stop=True)),
               reads=(hs_, "macckb"), writes=("pb5",))
            st("act", lambda e: e.activation(out=mkcol[:], in_=pbank[5][:, 0:1],
                                             func=AF.Ln),
               reads=("pb5",), writes=("mkcol",))
            st("act", lambda e: e.activation(out=mkcol[:], in_=mkcol[:],
                                             func=AF.Exp, scale=0.5),
               reads=("mkcol",), writes=("mkcol",))
            st("dve", lambda e: e.tensor_mul(negm[:], mqcol[:], mkcol[:]),
               reads=("mqcol", "mkcol"), writes=("negm",))
            st("dve", lambda e: e.tensor_scalar(
                negm[:], negm[:], -0.125, None, ALU.mult),
               reads=("negm",), writes=("negm",))
            for qb in range(8):
                q0 = qb * SL
                NG = NT // 2

                def emit_st(g):
                    for half in range(2):
                        kt = 2 * g + half
                        bs = 2 * (g % 2) + half
                        st("pe", (lambda e, kk=kt, q_=q0, bb=bs: e.matmul(
                            pbank[bb][:, :], khd[:, kk * 128:(kk + 1) * 128],
                            qhd[:, q_:q_ + SL], start=True, stop=True)),
                           reads=("khd", "qhd"), writes=(f"pb{bs}",))

                emit_st(0)
                emit_st(1)
                for g in range(NG):
                    jp = g % 2
                    b0 = 2 * (g % 2)
                    st("act", (lambda e, j=jp: e.activation(
                        out=ptl[j][:, 0:1024], in_=pbw[j][:, 0:1024],
                        func=AF.Exp, scale=0.125, bias=negm[:])),
                       reads=(f"pb{b0}", f"pb{b0 + 1}", "negm"),
                       writes=((f"ptl{jp}",),))
                    if g + 2 < NG:
                        emit_st(g + 2)
                    ob = 4 + qb % 2
                    for half in range(2):
                        kt = 2 * g + half
                        st("pe", (lambda e, kk=kt, j=jp, hf=half, bb=ob: e.matmul(
                            pbank[bb][:, 0:SL],
                            vslab[:, kk * 128:(kk + 1) * 128],
                            ptl[j][:, hf * SL:hf * SL + SL],
                            start=(kk == 0), stop=(kk == 31))),
                           reads=((f"ptl{jp}",), ("vslab", kt // 8),
                                  ("vslab", "ones")),
                           writes=(f"pb{ob}",))
                ob = 4 + qb % 2
                st("dve", (lambda e, bb=ob, q_=qb: e.tensor_copy(
                    aot[0:64, 0:SL], pbank[bb][0:64, 0:SL])),
                   reads=(f"pb{ob}",), writes=("aot",))
                st("dve", (lambda e, bb=ob, q_=qb: e.tensor_copy(
                    zsb[0:1, q_ * SL:(q_ + 1) * SL], pbank[bb][64:65, 0:SL])),
                   reads=(f"pb{ob}",), writes=("zsb",))
                dmac("gp", ao_d[h * 64:(h + 1) * 64, q0:q0 + SL], aot[0:64, 0:SL],
                     reads=("aot",), writes=(("ao", h, qb),))
                if qb == 7:
                    dmac("gp", zs_d[h:h + 1, :], zsb[0:1, :],
                         reads=("zsb",), writes=(("zs", h),))

    def phase3(g):
        aofm, fwTs = g["aofm"], g["fwTs"]
        zrec = g["zrec"]
        expd = g["expd"]
        moeacc, moebf, moefm = g["moeacc"], g["moebf"], g["moefm"]
        baset, outt = g["baset"], g["outt"]
        gatesf, gtmp, gmsk = g["gatesf"], g["gtmp"], g["gmsk"]
        gcol, gcol2 = g["gcol"], g["gcol2"]
        wtok4, wfm4 = g["wtok4"], g["wfm4"]
        load_wchunks(fwTs, "fwTs", fwT_h, 8, D)
        dmac("sp", zrec[0:AH, :], zs_d,
             reads=tuple(("zs", h) for h in range(AH)), writes=("zrec",))
        st("dve", lambda e: e.reciprocal(zrec[0:AH, :], zrec[0:AH, :]),
           reads=("zrec",), writes=("zrec",))
        for ic in range(8):
            for half in range(2):
                st("pe", (lambda e, ii=ic, hf=half: e.matmul(
                    (pbank[2] if hf == 0 else pbank[3])[0:8, 0:512],
                    ebTs[:, ii * 8:(ii + 1) * 8],
                    fwTs[:, ii * D + hf * 512: ii * D + hf * 512 + 512],
                    start=(ii == 0), stop=(ii == 7))),
                   reads=(("fwTs", ic), "ebTs"),
                   writes=(("pb2" if half == 0 else "pb3"),))
        for half in range(2):
            st("dve", (lambda e, hf=half: e.tensor_copy(
                ebf[0:8, hf * 512:hf * 512 + 512],
                (pbank[2] if hf == 0 else pbank[3])[0:8, 0:512])),
               reads=(("pb2" if half == 0 else "pb3"),), writes=("ebf",))
        for tl in range(8):
            t0 = tl * SL
            for ic in range(8):
                dmac("sp", aofm[:, ic * SL:(ic + 1) * SL],
                     ao_d[ic * 128:(ic + 1) * 128, t0:t0 + SL],
                     reads=tuple(("ao", h, tl) for h in range(AH)) + ("aofm",),
                     writes=("aofm",))
            for ic in range(8):
                st("pe", (lambda e, ii=ic, tt=t0: e.matmul(
                    pbank[5][:, 0:SL], headmap[:, ii * 128:(ii + 1) * 128],
                    zrec[0:16, tt:tt + SL],
                    start=True, stop=True)),
                   reads=("zrec", "headmap"), writes=("pb5",))
                st("dve", (lambda e, ii=ic: e.tensor_mul(
                    aofm[:, ii * SL:(ii + 1) * SL],
                    aofm[:, ii * SL:(ii + 1) * SL], pbank[5][:, 0:SL])),
                   reads=("aofm", "pb5"), writes=("aofm",))
            for u in range(4):
                uo = u * 128
                for ic in range(8):
                    st("pe", (lambda e, ii=ic, oo=uo: e.matmul(
                        pbank[4][:, 0:8],
                        aofm[:, ii * SL + oo: ii * SL + oo + 128],
                        gwTs[:, ii * 8:(ii + 1) * 8],
                        start=(ii == 0), stop=(ii == 7))),
                       reads=("aofm", "gwTs"), writes=("pb4",))
                st("dve", lambda e: e.tensor_add(gatesf[:], pbank[4][:, 0:8],
                                                 gbb[:]),
                   reads=("pb4", "gbb"), writes=("gatesf",))
                st("dve", lambda e: e.tensor_reduce(
                    gcol[:], gatesf[:], AX.X, ALU.max, negate=True),
                   reads=("gatesf",), writes=("gcol",))
                st("act", lambda e: e.activation(
                    out=gtmp[:], in_=gatesf[:], func=AF.Exp, bias=gcol[:]),
                   reads=("gatesf", "gcol"), writes=("gtmp",))
                st("dve", lambda e: e.tensor_reduce(
                    gcol2[:], gtmp[:], AX.X, ALU.add),
                   reads=("gtmp",), writes=("gcol2",))
                st("dve", lambda e: e.reciprocal(gcol2[:], gcol2[:]),
                   reads=("gcol2",), writes=("gcol2",))
                st("dve", lambda e: e.tensor_scalar(
                    gatesf[:], gtmp[:], gcol2[:], None, ALU.mult),
                   reads=("gtmp", "gcol2"), writes=("gatesf",))
                st("dve", lambda e: e.tensor_reduce(
                    gcol[:], gatesf[:], AX.X, ALU.max),
                   reads=("gatesf",), writes=("gcol",))
                st("dve", lambda e: e.tensor_scalar(
                    gmsk[:], gatesf[:], gcol[:], None, ALU.is_ge),
                   reads=("gatesf", "gcol"), writes=("gmsk",))
                st("dve", lambda e: e.scalar_tensor_tensor(
                    gtmp[:], gmsk[:], -1e30, gatesf[:], ALU.mult, ALU.add),
                   reads=("gmsk", "gatesf"), writes=("gtmp",))
                st("dve", lambda e: e.tensor_reduce(
                    gcol2[:], gtmp[:], AX.X, ALU.max),
                   reads=("gtmp",), writes=("gcol2",))
                st("dve", lambda e: e.tensor_scalar(
                    gmsk[:], gatesf[:], gcol2[:], None, ALU.is_ge),
                   reads=("gatesf", "gcol2"), writes=("gmsk",))
                st("dve", (lambda e, uu=u: e.tensor_mul(
                    wtok4[:, uu * 8:(uu + 1) * 8], gmsk[:], gatesf[:])),
                   reads=("gmsk", "gatesf"), writes=("wtok4",))
                st("pe", (lambda e, uu=u: e.transpose(
                    pbank[4][0:8, 0:128], wtok4[:, uu * 8:(uu + 1) * 8],
                    identf[:])),
                   reads=("wtok4", "identf"), writes=("pb4",))
                st("dve", (lambda e, uu=u: e.tensor_copy(
                    wfm4[0:8, uu * 128:(uu + 1) * 128], pbank[4][0:8, 0:128])),
                   reads=("pb4",), writes=("wfm4",))
            for n in range(NE):
                j = n % 2
                for ic in range(8):
                    dmac("gp", expd[j][:, ic * D:(ic + 1) * D],
                         ewT_h[n, ic * 128:(ic + 1) * 128, :],
                         reads=(f"expd{j}",), writes=(f"expd{j}",))
                for u in range(4):
                    uo = u * 128
                    for half in range(2):
                        bank = half
                        for ic in range(8):
                            st("pe", (lambda e, ii=ic, jj=j, hf=half,
                                      bb=bank, oo=uo: e.matmul(
                                          pbank[bb][:, :],
                                          aofm[:, ii * SL + oo: ii * SL + oo + 128],
                                          expd[jj][:, ii * D + hf * 512:
                                                   ii * D + hf * 512 + 512],
                                          start=(ii == 0), stop=(ii == 7))),
                               reads=("aofm", f"expd{j}"),
                               writes=(f"pb{bank}",))
                    for half in range(2):
                        if n == 0:
                            st("dve", (lambda e, hf=half, uu=u: e.tensor_scalar(
                                moeacc[uu][:, hf * 512:hf * 512 + 512],
                                pbank[hf][:, :], wtok4[:, uu * 8:uu * 8 + 1],
                                None, ALU.mult)),
                               reads=(f"pb{half}", "wtok4"),
                               writes=((f"moeacc{u}",),))
                        else:
                            st("dve", (lambda e, nn=n, hf=half, uu=u:
                                       e.scalar_tensor_tensor(
                                           moeacc[uu][:, hf * 512:hf * 512 + 512],
                                           pbank[hf][:, :],
                                           wtok4[:, uu * 8 + nn:uu * 8 + nn + 1],
                                           moeacc[uu][:, hf * 512:hf * 512 + 512],
                                           ALU.mult, ALU.add)),
                               reads=(f"pb{half}", "wtok4", (f"moeacc{u}",)),
                               writes=((f"moeacc{u}",),))
            for u in range(4):
                st("dve", (lambda e, uu=u: e.tensor_copy(moebf[:], moeacc[uu][:])),
                   reads=((f"moeacc{u}",), "moebf"), writes=("moebf",))
                for oc in range(8):
                    bank = oc % 2
                    st("pe", (lambda e, oo=oc, bb=bank: e.transpose(
                        pbtr[bb][:, 0:128], moebf[:, oo * 128:(oo + 1) * 128],
                        identb[:])),
                       reads=("moebf", "identb"), writes=(f"pt{bank}",))
                    st("dve", (lambda e, oo=oc, bb=bank: e.tensor_copy(
                        moefm[:, oo * 128:(oo + 1) * 128], pbtr[bb][:, 0:128])),
                       reads=(f"pt{bank}",), writes=("moefm",))
                for half in range(2):
                    bank = 2 + half
                    for oc in range(8):
                        st("pe", (lambda e, oo=oc, hf=half, bb=bank: e.matmul(
                            pbank[bb][:, :],
                            moefm[:, oo * 128:(oo + 1) * 128],
                            fwTs[:, oo * D + hf * 512: oo * D + hf * 512 + 512],
                            start=(oo == 0), stop=False)),
                           reads=("moefm", ("fwTs", oc)),
                           writes=(f"pb{bank}",))
                    st("pe", (lambda e, hf=half, bb=bank, uu=u: e.matmul(
                        pbank[bb][:, :], wfm4[0:8, uu * 128:(uu + 1) * 128],
                        ebf[0:8, hf * 512:hf * 512 + 512],
                        start=False, stop=True)),
                       reads=("wfm4", "ebf"), writes=(f"pb{bank}",))
                ti = tl * 4 + u
                dmac("sp", baset[:], base_d[ti * 128:(ti + 1) * 128, :],
                     reads=(("based", ti), "baset"),
                     writes=("baset",))
                po = u % 2
                ot = outt[po]
                for half in range(2):
                    st("dve", (lambda e, hf=half, o_=ot: e.tensor_add(
                        o_[:, hf * 512:hf * 512 + 512], pbank[2 + hf][:, :],
                        fbb[:, hf * 512:hf * 512 + 512])),
                       reads=(f"pb{2 + half}", "fbb", (f"outt{po}",)),
                       writes=((f"outt{po}",),))
                    st("dve", (lambda e, hf=half, o_=ot: e.tensor_add(
                        o_[:, hf * 512:hf * 512 + 512],
                        o_[:, hf * 512:hf * 512 + 512],
                        baset[:, hf * 512:hf * 512 + 512])),
                       reads=((f"outt{po}",), "baset"),
                       writes=((f"outt{po}",),))
                dmac("gp", out_h[ti * 128:(ti + 1) * 128, :], ot[:],
                     reads=((f"outt{po}",),),
                     writes=((f"outt{po}",), ("outw", ti)))

    loads_init()
    if "p0" in phases:
        phase0()
    P.reset_region("RB")
    P.reset_region("RC")
    g1a = {}
    g1a["winsb"] = palloc("winsb", (128, 8 * 3 * D), BF16, OFF["winsb"], "RB")
    g1a["wogsb"] = palloc("wogsb", (128, 8 * D), BF16, OFF["wogsb"], "RC")
    g1a["praw"] = [palloc(f"praw{i}", (128, SL + 4), BF16, OFF[f"praw{i}"], "RC")
                   for i in range(4)]
    convt = palloc("convt", (128, SL), BF16, OFF["convt"], "RC")
    g1a["ogt"] = [palloc(f"ogt{i}", (128, SL), F32, OFF[f"ogt{i}"], "RC")
                  for i in range(2)]
    if "p1a" in phases:
        phase1a(g1a)
    P.reset_region("RB")
    P.reset_region("RC")
    g1b = {}
    g1b["qnsl"] = palloc("qnsl", (128, 8 * SL), BF16, OFF["qnsl"], "RB")
    g1b["knsl"] = palloc("knsl", (128, 8 * SL), BF16, OFF["knsl"], "RB")
    g1b["vnsl"] = palloc("vnsl", (128, 8 * SL), BF16, OFF["vnsl"], "RB")
    g1b["woutsb"] = palloc("woutsb", (128, 8 * D), BF16, OFF["woutsb"], "RB")
    g1b["ogsl"] = palloc("ogsl", (128, 8 * SL), F32, OFF["ogsl"], "RC")
    g1b["sqt"] = sqt_p
    g1b["rrow"] = rrow_p
    cht4 = chunk_tiles4()
    readn4_t, readfm4_t, dnbase_t = rd_tiles4()
    if "p1b" in phases:
        phase1b(g1b)
    P.reset_region("RB")
    P.reset_region("RC")
    g2p = {}
    g2p["qwTs"] = palloc("qwTs", (128, 8 * D), BF16, OFF["qwTs"], "RB")
    g2p["kwTs"] = palloc("kwTs", (128, 8 * D), BF16, OFF["kwTs"], "RB")
    g2p["vwTs"] = palloc("vwTs", (128, 8 * D), BF16, OFF["vwTs"], "RB")
    g2p["copt"] = palloc("copt", (128, S), F32, OFF["copt"], "RC")
    g2p["sipt"] = palloc("sipt", (128, S), F32, OFF["sipt"], "RC")
    g2p["qpt"] = [palloc(f"qpt{i}", (128, SL), BF16, OFF[f"qpt{i}"], "RC")
                  for i in range(2)]
    g2p["ropet1"] = palloc("ropet1", (128, SL), F32, OFF["ropet1"], "RC")
    g2p["ropet2"] = palloc("ropet2", (128, SL), F32, OFF["ropet2"], "RC")
    g2p["ropeo"] = palloc("ropeo", (128, SL), BF16, OFF["ropeo"], "RC")
    g2p["vtokt"] = palloc("vtokt", (128, D), BF16, OFF["vtokt"], "RC")
    g2p["sqa"] = palloc("sqa", (128, SL), BF16, OFF["sqa"], "RC")
    if "p2p" in phases:
        phase2proj(g2p)
    P.reset_region("RB")
    P.reset_region("RC")
    g2a = {}
    g2a["khd"] = palloc("khd", (128, S), BF16, OFF["khd"], "RB")
    g2a["qhd"] = palloc("qhd", (128, S), BF16, OFF["qhd"], "RB")
    g2a["vslab"] = palloc("vslab", (128, 32 * 128), BF16, OFF["vslab"], "RB")
    g2a["ptl"] = [palloc(f"ptl{i}", (128, 1024), BF16, OFF[f"ptl{i}"], "RB")
                  for i in range(2)]
    g2a["zsb"] = palloc("zsb", (1, S), F32, OFF["zbs"], "RB")
    g2a["aot"] = palloc("aot", (64, SL), BF16, OFF["aot"], "RB")
    g2a["negmh"] = palloc("negmh", (128, 1), F32, OFF["negmh"], "RB")
    g2a["mqcol"] = palloc("mqcol", (128, 1), F32, OFF["mqcol"], "RB")
    g2a["mkcol"] = palloc("mkcol", (128, 1), F32, OFF["mkcol"], "RB")
    if "p2a" in phases:
        phase2attn(g2a)
    P.reset_region("RA")
    P.reset_region("RB")
    P.reset_region("RC")
    g3 = {}
    g3["expd"] = [palloc(f"expd{i}", (128, 8 * D), BF16, OFF[f"expd{i}"], "RA")
                  for i in range(2)]
    g3["moeacc"] = [palloc(f"moeacc{i}", (128, D), F32, OFF["moeacc"] + 4 * i,
                           "RA") for i in range(4)]
    g3["moebf"] = palloc("moebf", (128, D), BF16, OFF["moebf"], "RA")
    g3["moefm"] = palloc("moefm", (128, D), BF16, OFF["moefm"], "RA")
    g3["baset"] = palloc("baset", (128, D), F32, OFF["baset"], "RA")
    g3["outt"] = [palloc(f"outt{i}", (128, D), F32, OFF[f"outt{i}"], "RA")
                  for i in range(2)]
    g3["gatesf"] = palloc("gatesf", (128, 8), F32, OFF["gatesf"], "RC")
    g3["gtmp"] = palloc("gtmp", (128, 8), F32, OFF["gtmp"], "RC")
    g3["gmsk"] = palloc("gmsk", (128, 8), F32, OFF["gmsk"], "RC")
    g3["gcol"] = palloc("gcol", (128, 1), F32, OFF["gcol"], "RC")
    g3["gcol2"] = palloc("gcol2", (128, 1), F32, OFF["gcol2"], "RC")
    g3["wtok4"] = palloc("wtok4", (128, 32), F32, OFF["wtok4"], "RC")
    g3["wfm4"] = palloc("wfm4", (8, 512), BF16, OFF["wfm4"], "RC")
    g3["aofm"] = palloc("aofm", (128, 8 * SL), BF16, OFF["aofm"], "RB")
    g3["zrec"] = palloc("zrec", (16, S), F32, OFF["zrec"], "RB")
    g3["fwTs"] = palloc("fwTs", (128, 8 * D), BF16, OFF["fwTs"], "RB")
    if "p3" in phases:
        phase3(g3)

    if "p3" not in phases:
        for i in range(NT):
            p = i % 2
            dmac("gp", out_h[i * 128:(i + 1) * 128, :], xn32t[p][:],
                 reads=(f"xn32t{p}",), writes=(("outw", i),))

    for name, h in dbg_outs.items():
        src = {"xn32": xn32_d, "qc": qc_d, "kc": kc_d, "vc": vc_d,
               "oglog": oglog_d, "base": base_d, "qr": qr_d, "kr": kr_d,
               "vt": vt_d, "ao": ao_d, "zs": zs_d, "readn": readn_d}.get(name)
        if src is not None:
            reads = {"xn32": tuple(("xn32d", i) for i in range(NT)),
                     "base": tuple(("based", i) for i in range(NT)),
                     "qc": tuple(("qc", oc, s2) for oc in range(8) for s2 in range(NSL)),
                     "kc": tuple(("kc", oc, s2) for oc in range(8) for s2 in range(NSL)),
                     "vc": tuple(("vc", oc, s2) for oc in range(8) for s2 in range(NSL)),
                     "oglog": tuple(("oglog", oc, s2) for oc in range(8) for s2 in range(NSL)),
                     "qr": tuple(("qr", oc, s2) for oc in range(8) for s2 in range(NSL)),
                     "kr": tuple(("kr", oc, s2) for oc in range(8) for s2 in range(NSL)),
                     "vt": tuple(("vt", i) for i in range(NT)),
                     "ao": tuple(("ao", hh, qb) for hh in range(AH) for qb in range(8)),
                     "zs": tuple(("zs", hh) for hh in range(AH)),
                     "readn": tuple(("readnd", i) for i in range(NT)),
                     }[name]
            dmac("gp", h, src, reads=reads, writes=((name, "dbgw"),))
        elif name == "absb":
            dmac("gp", h, absb[:], reads=tuple(("absb", i) for i in range(NT)),
                 writes=(("absb", "dbgw"),))

    counters = P.finalize()
    per_engine = {e: [] for e in ENGINES}
    for s_ in P.steps:
        per_engine[s_["engine"]].append(s_)

    with (
        nc.semaphore("sem_pe") as sem_pe,
        nc.semaphore("sem_act") as sem_act,
        nc.semaphore("sem_dve") as sem_dve,
        nc.semaphore("sem_gp") as sem_gp,
        nc.semaphore("sem_sp") as sem_sp,
        nc.semaphore("sem_gp_dma") as sem_gp_dma,
        nc.semaphore("sem_sp_dma") as sem_sp_dma,
        nc.Block() as block,
    ):
        sems = {"pe": sem_pe, "act": sem_act, "dve": sem_dve, "gp": sem_gp,
                "sp": sem_sp, "gp_dma": sem_gp_dma, "sp_dma": sem_sp_dma}

        def run_engine(eng_obj, steps):
            for s_ in steps:
                for key, val in s_["waits"]:
                    eng_obj.wait_ge(sems[key], val)
                inst = s_["emit"](eng_obj)
                if s_["need_sem"]:
                    inst.then_inc(sems[s_["semkey"]], 16 if s_["dma"] else 1)

        @block.tensor
        def _(t):
            run_engine(t, per_engine["pe"])

        @block.scalar
        def _(a):
            run_engine(a, per_engine["act"])

        @block.vector
        def _(v):
            run_engine(v, per_engine["dve"])

        @block.gpsimd
        def _(gg):
            run_engine(gg, per_engine["gp"])
            for key in ("gp_dma", "sp_dma"):
                if counters.get(key):
                    gg.wait_ge(sems[key], counters[key])

        @block.sync
        def _(sy):
            run_engine(sy, per_engine["sp"])

    return nc, list(dbg_outs.keys())


def _prep_inputs(ins):
    f32 = np.float32

    def colchunk(v):
        return np.ascontiguousarray(np.asarray(v, f32).reshape(8, 128).T)

    base = {
        "win": np.ascontiguousarray(np.asarray(ins["L0_attn_W_in"], f32)),
        "wog": np.ascontiguousarray(np.asarray(ins["L0_attn_W_ctrl"], f32)[:, 8:]),
        "wab": np.ascontiguousarray(
            np.asarray(ins["L0_attn_W_ctrl"], f32)[:, :8]
            .reshape(8, 128, 8).transpose(1, 0, 2).reshape(128, 64)),
        "bog": colchunk(np.asarray(ins["L0_attn_b_ctrl"])[8:]),
        "bab": np.asarray(ins["L0_attn_b_ctrl"], f32)[:8].reshape(1, 8),
        "wconv": np.ascontiguousarray(np.stack([
            np.asarray(ins["L0_attn_w_q"], f32).reshape(8, 128, 4).transpose(1, 0, 2),
            np.asarray(ins["L0_attn_w_k"], f32).reshape(8, 128, 4).transpose(1, 0, 2),
            np.asarray(ins["L0_attn_w_v"], f32).reshape(8, 128, 4).transpose(1, 0, 2),
        ], axis=1).reshape(128, 96)),
        "wout": np.ascontiguousarray(np.asarray(ins["L0_attn_W_out"], f32)),
        "bout": colchunk(ins["L0_attn_b_out"]),
        "ramp": colchunk(ins["L0_attn_ramp_logit"]),
        "qwT": np.ascontiguousarray(np.asarray(ins["L0_ffn_q_w"], f32).T),
        "kwT": np.ascontiguousarray(np.asarray(ins["L0_ffn_k_w"], f32).T),
        "vwT": np.ascontiguousarray(np.asarray(ins["L0_ffn_v_w"], f32).T),
        "qb": colchunk(ins["L0_ffn_q_b"]),
        "kb": colchunk(ins["L0_ffn_k_b"]),
        "vbrow": np.asarray(ins["L0_ffn_v_b"], f32).reshape(1, D),
        "gwT": np.ascontiguousarray(
            np.asarray(ins["L0_ffn_gate_w"], f32).T
            .reshape(8, 128, 8).transpose(1, 0, 2).reshape(128, 64)),
        "gbrow": np.asarray(ins["L0_ffn_gate_b"], f32).reshape(1, 8),
        "ewT": np.ascontiguousarray(
            np.asarray(ins["L0_ffn_expert_w"], f32).transpose(0, 2, 1)),
        "ebT": np.ascontiguousarray(
            np.asarray(ins["L0_ffn_expert_b"], f32).T
            .reshape(8, 128, 8).transpose(1, 0, 2).reshape(128, 64)),
        "fwT": np.ascontiguousarray(np.asarray(ins["L0_ffn_ffn_w"], f32).T),
        "fbrow": np.asarray(ins["L0_ffn_ffn_b"], f32).reshape(1, D),
    }
    for k, v in CONSTS.items():
        base[k] = v
    x = np.asarray(ins["L0_attn_x"], np.float32)
    return [dict(base, x=np.ascontiguousarray(x[b])) for b in range(8)]


def kernel(**inputs):
    in_maps = _prep_inputs(inputs)
    nc, _dbg = build_program()
    res = run_bass_kernel_spmd(nc, in_maps, core_ids=list(range(8)))
    return np.stack([r["out"] for r in res.results], axis=0).astype(np.float32)


# version 14
# speedup vs baseline: 1.2604x; 1.0888x over previous
import numpy as np
import ml_dtypes
import concourse.bass as bass
import concourse.mybir as mybir
from concourse.bass_utils import run_bass_kernel_spmd

F32 = mybir.dt.float32
BF16 = mybir.dt.bfloat16
AF = mybir.ActivationFunctionType
ALU = mybir.AluOpType
AX = mybir.AxisListType
NPBF = ml_dtypes.bfloat16

S = 4096
D = 1024
NSL = 8
SL = 512
NT = 32
C = 128
HD = 256
DNH = 4
AH = 16
ADH = 64
NE = 8

ENGINES = ("pe", "act", "dve", "gp", "sp")


class Plan:

    def __init__(self):
        self.steps = []
        self.lastw = {}
        self.readers = {}
        self.region_of = {}
        self.reg_pending = {}
        self.reg_inited = {}

    def set_region(self, res, region):
        self.region_of[res] = region

    @staticmethod
    def _base(res):
        return res if isinstance(res, str) else res[0]

    def reset_region(self, region):
        pend = set()
        for res in list(self.lastw.keys()) + list(self.readers.keys()):
            if self.region_of.get(self._base(res)) != region:
                continue
            if res in self.lastw:
                pend.add(self.lastw.pop(res))
            pend |= self.readers.pop(res, set())
        self.reg_pending[region] = pend
        self.reg_inited[region] = set()

    def step(self, engine, emit, reads=(), writes=(), dma=False):
        idx = len(self.steps)
        deps = set()
        for r in reads:
            if r in self.lastw:
                deps.add(self.lastw[r])
        for w in writes:
            if w in self.lastw:
                deps.add(self.lastw[w])
            deps |= self.readers.get(w, set())
            reg = self.region_of.get(self._base(w))
            if reg is not None and self.reg_pending.get(reg):
                if w not in self.reg_inited[reg]:
                    deps |= self.reg_pending[reg]
                    self.reg_inited[reg].add(w)
        for r in reads:
            self.readers.setdefault(r, set()).add(idx)
        for w in writes:
            self.lastw[w] = idx
            self.readers[w] = {idx}
        deps.discard(idx)
        self.steps.append(dict(engine=engine, emit=emit, deps=deps, dma=dma,
                               need_sem=False))
        return idx

    SELF_SYNC = ("dve", "act", "gp")

    def finalize(self):
        steps = self.steps
        for s in steps:
            for d in s["deps"]:
                p = steps[d]
                if (p["engine"] != s["engine"] or p["dma"]
                        or p["engine"] in self.SELF_SYNC):
                    p["need_sem"] = True
        for s in steps:
            if s["dma"]:
                s["need_sem"] = True
        counters = {}
        for s in steps:
            if s["need_sem"]:
                key = s["engine"] + ("_dma" if s["dma"] else "")
                counters[key] = counters.get(key, 0) + (16 if s["dma"] else 1)
                s["semkey"] = key
                s["semval"] = counters[key]
        wm = {}
        for s in steps:
            need = {}
            for d in s["deps"]:
                p = steps[d]
                if (p["engine"] == s["engine"] and not p["dma"]
                        and p["engine"] not in self.SELF_SYNC):
                    continue
                need[p["semkey"]] = max(need.get(p["semkey"], 0), p["semval"])
            waits = []
            for key, val in sorted(need.items()):
                if wm.get((s["engine"], key), 0) >= val:
                    continue
                wm[(s["engine"], key)] = val
                waits.append((key, val))
            s["waits"] = waits
        return counters


def host_consts():
    c = {}
    c["identb"] = np.eye(128, dtype=NPBF)
    c["identf"] = np.eye(128, dtype=np.float32)
    c["onesf"] = np.ones((1, 128), np.float32)
    c["onescolb"] = np.ones((128, 1), NPBF)
    c["uincf"] = np.triu(np.ones((128, 128), np.float32))
    maskadd = np.where(np.arange(128)[:, None] <= np.arange(128)[None, :],
                       0.0, -1e30).astype(np.float32)
    c["maskaddTf"] = np.ascontiguousarray(maskadd.T)
    c["offdiagf"] = (1.0 - np.eye(128)).astype(np.float32)
    c["offdiag4f"] = np.tile(-(1.0 - np.eye(128)).astype(np.float32), (1, 4))
    e127 = np.zeros((128, 128), np.float32)
    e127[127, :] = 1.0
    c["e127f"] = e127
    es0 = np.zeros((2, 128), np.float32); es0[0, :] = 1.0
    es1 = np.zeros((2, 128), np.float32); es1[1, :] = 1.0
    c["esel0f"], c["esel1f"] = es0, es1
    c["esel0b"], c["esel1b"] = es0.astype(NPBF), es1.astype(NPBF)
    hs = np.zeros((128, 2), NPBF); hs[:64, 0] = 1; hs[64:, 1] = 1
    c["headsel"] = hs
    bsel = np.zeros((2, 128), np.float32); bsel[0, :64] = 1; bsel[1, 64:] = 1
    c["blockself"] = bsel
    hm = np.zeros((16, 8 * 128), np.float32)
    for ic in range(8):
        hm[2 * ic, ic * 128: ic * 128 + 64] = 1
        hm[2 * ic + 1, ic * 128 + 64: ic * 128 + 128] = 1
    c["headmapf"] = hm
    psw = np.zeros((128, 128), NPBF)
    for a in range(2):
        for dd in range(32):
            psw[a * 64 + dd + 32, a * 64 + dd] = -1.0
            psw[a * 64 + dd, a * 64 + dd + 32] = 1.0
    c["pswapT"] = psw
    inv = 1.0 / 10000.0 ** (np.arange(0, ADH, 2, dtype=np.float64) / ADH)
    ang = np.outer(inv, np.arange(S, dtype=np.float64))
    ang2 = np.concatenate([ang, ang], axis=0)
    cos = np.cos(ang2).astype(np.float32)
    sin = np.sin(ang2).astype(np.float32)
    c["copt"] = np.ascontiguousarray(np.concatenate([cos, cos], axis=0))
    c["sipt"] = np.ascontiguousarray(np.concatenate([sin, sin], axis=0))
    return c


CONSTS = host_consts()


def build_program(debug=(), phases=("p0","p1a","p1b","p2p","p2a","p3")):
    nc = bass.Bass("TRN2", target_bir_lowering=False, debug=False)
    P = Plan()

    def inp(name, shape, dt=F32):
        return nc.dram_tensor(name, list(shape), dt, kind="ExternalInput").ap()

    x_h = inp("x", (S, D))
    win_h = inp("win", (D, 3 * D))
    wog_h = inp("wog", (D, D))
    wab_h = inp("wab", (128, 64))
    bog_h = inp("bog", (128, 8))
    bab_h = inp("bab", (1, 8))
    wconv_h = inp("wconv", (128, 96))
    wout_h = inp("wout", (D, D))
    bout_h = inp("bout", (128, 8))
    ramp_h = inp("ramp", (128, 8))
    qwT_h = inp("qwT", (D, D))
    kwT_h = inp("kwT", (D, D))
    vwT_h = inp("vwT", (D, D))
    qb_h = inp("qb", (128, 8))
    kb_h = inp("kb", (128, 8))
    vb_h = inp("vbrow", (1, D))
    gwT_h = inp("gwT", (128, 64))
    gb_h = inp("gbrow", (1, 8))
    ewT_h = inp("ewT", (NE, D, D))
    ebT_h = inp("ebT", (128, 64))
    fwT_h = inp("fwT", (D, D))
    fb_h = inp("fbrow", (1, D))
    con_h = {k: inp(k, CONSTS[k].shape,
                    BF16 if CONSTS[k].dtype == NPBF else F32)
             for k in CONSTS}
    out_h = nc.dram_tensor("out", [S, D], F32, kind="ExternalOutput").ap()

    def scr(name, shape, dt):
        return nc.dram_tensor(name, list(shape), dt, kind="Internal").ap()

    xn32_d = scr("xn32", (S, D), F32)
    base_d = scr("base", (S, D), F32)
    oglog_d = scr("oglog", (D, S), F32)
    qc_d = scr("qc", (D, S), BF16)
    kc_d = scr("kc", (D, S), BF16)
    vc_d = scr("vc", (D, S), BF16)
    qr_d = scr("qr", (D, S), BF16)
    kr_d = scr("kr", (D, S), BF16)
    vt_d = scr("vt", (S, D), BF16)
    ao_d = scr("ao", (D, S), BF16)
    zs_d = scr("zs", (AH, S), F32)
    readn_d = scr("readn_d", (S, D), BF16)
    ewb_d = scr("ewb", (NE, D, D), BF16)

    dbg_outs = {}
    def dbgout(name, shape, dt):
        if name in debug:
            dbg_outs[name] = nc.dram_tensor("dbg_" + name, list(shape), dt,
                                            kind="ExternalOutput").ap()
    dbgout("xn32", (S, D), F32)
    dbgout("qc", (D, S), F32)
    dbgout("kc", (D, S), F32)
    dbgout("vc", (D, S), F32)
    dbgout("oglog", (D, S), F32)
    dbgout("absb", (128, NT * 8), F32)
    dbgout("base", (S, D), F32)
    dbgout("qr", (D, S), F32)
    dbgout("kr", (D, S), F32)
    dbgout("vt", (S, D), F32)
    dbgout("ao", (D, S), F32)
    dbgout("zs", (AH, S), F32)
    dbgout("readn", (S, D), F32)

    sb = {}

    def sbuf(name, shape, dt):
        t = nc.alloc_sbuf_tensor("s_" + name, list(shape), dt)
        sb[name] = t
        return t

    identb = sbuf("identb", (128, 128), BF16)
    identf = sbuf("identf", (128, 128), F32)
    onesf = sbuf("onesf", (1, 128), F32)
    onescolb = sbuf("onescolb", (128, 1), BF16)
    uincf = sbuf("uincf", (128, 128), F32)
    maskaddT = sbuf("maskaddT", (128, 128), F32)
    offdiag = sbuf("offdiag", (128, 128), F32)
    offdiag4 = sbuf("offdiag4", (128, 512), F32)
    e127 = sbuf("e127", (128, 128), F32)
    esel0 = sbuf("esel0", (2, 128), F32)
    esel1 = sbuf("esel1", (2, 128), F32)
    esel0b = sbuf("esel0b", (2, 128), BF16)
    esel1b = sbuf("esel1b", (2, 128), BF16)
    maccqb = sbuf("maccqb", (2, 8), BF16)
    macckb = sbuf("macckb", (2, 8), BF16)
    headsel = sbuf("headsel", (128, 2), BF16)
    blocksel = sbuf("blocksel", (2, 128), F32)
    headmap = sbuf("headmap", (16, 8 * 128), F32)
    pswapT = sbuf("pswapT", (128, 128), BF16)
    bogc = sbuf("bogc", (128, 8), F32)
    bogc2 = sbuf("bogc2", (128, 8), F32)
    babr = sbuf("babr", (1, 8), F32)
    boutc = sbuf("boutc", (128, 8), F32)
    rampc = sbuf("rampc", (128, 8), F32)
    rampt = sbuf("rampt", (128, 8), F32)
    qbc = sbuf("qbc", (128, 8), F32)
    kbc = sbuf("kbc", (128, 8), F32)
    vbrow = sbuf("vbrow", (1, D), F32)
    vbb = sbuf("vbb", (128, D), F32)
    gbrow = sbuf("gbrow", (1, 8), F32)
    gbb = sbuf("gbb", (128, 8), F32)
    babb = sbuf("babb", (128, 8), F32)
    fbrow = sbuf("fbrow", (1, D), F32)
    fbb = sbuf("fbb", (128, D), F32)
    gwTs = sbuf("gwTs", (128, 64), BF16)
    ebTs = sbuf("ebTs", (128, 64), BF16)
    wabsb = sbuf("wabsb", (128, 64), BF16)
    wconv = sbuf("wconv", (128, 96), F32)
    ebf = sbuf("ebf", (8, D), BF16)
    tails = sbuf("tails", (128, 24 * 3), BF16)
    st32 = sbuf("st32", (128, DNH * 2 * HD), F32)
    stbf = sbuf("stbf", (128, DNH * 2 * HD), BF16)
    absb = sbuf("absb", (128, NT * 8), F32)
    lnst = sbuf("lnst", (128, 8), F32)
    maccq = sbuf("maccq", (2, 8), F32)
    sqt_p = sbuf("sqt_p", (128, SL), BF16)
    rrow_p = sbuf("rrow_p", (1, SL), F32)
    cpe5 = sbuf("cpe5", (128, 1), F32)
    cpe6 = sbuf("cpe6", (128, 1), F32)
    nc.const_aps.register(F32, 1e-5, cpe5[:])
    nc.const_aps.register(F32, 1e-6, cpe6[:])
    macck = sbuf("macck", (2, 8), F32)
    mtmp = sbuf("mtmp", (2, 1), F32)

    pool_base = (nc.sbuf_base + 1023) // 1024 * 1024
    pool_avail = nc.sbuf_bytes_remaining - (pool_base - nc.sbuf_base)
    pool = {}

    avail_pp = pool_avail if pool_avail <= 256 * 1024 else pool_avail // 128

    def palloc(name, shape, dt, off_kb, region):
        nbytes = int(np.prod(shape[1:])) * (2 if dt == BF16 else 4)
        off = pool_base + off_kb * 1024
        assert off_kb * 1024 + nbytes <= avail_pp - 1024, \
            f"pool overflow {name}: off_kb={off_kb} nbytes={nbytes} avail={avail_pp}"
        t = nc.alloc_sbuf_tensor_at("s_" + name, list(shape), dt, offset=off)
        pool[name] = t
        P.set_region(name, region)
        return t

    xnfm = palloc("xnfm", (128, 8 * S), BF16, 0, "RA")
    xt = [palloc(f"xt{i}", (128, D), F32, 64 + 4 * i, "RB") for i in range(2)]
    xbf = [palloc(f"xbf{i}", (128, D), BF16, 72 + 2 * i, "RB") for i in range(2)]
    x2t = palloc("x2t", (128, D), F32, 76, "RB")
    xn32t = [palloc(f"xn32t{i}", (128, D), F32, 80 + 4 * i, "RB") for i in range(2)]
    OFF = dict(
        winsb=64,
        wogsb=112,
        praw0=128, praw1=130, praw2=132, praw3=134,
        convt=136, ogt0=138, ogt1=140,
        qnsl=64, knsl=72, vnsl=80,
        woutsb=88,
        sqt=157, rrow=158,
        ogsl=112,
        qwTs=64, kwTs=80, vwTs=96,
        copt=112, sipt=128,
        qpt0=144, qpt1=145,
        ropet1=146, ropet2=148,
        ropeo=150, vtokt=152, sqa=154,
        khd=64, qhd=72, vslab=80,
        ptl0=88, ptl1=90, aot=92, zbs=96,
        negmh=93, mqcol=94, mkcol=95,
        expd0=0, expd1=16,
        moeacc=32,
        moebf=48, moefm=52, baset=56,
        outt0=60, outt1=120,
        zrec=88,
        gatesf=112, gtmp=113, gmsk=114,
        gcol=115, gcol2=116, wtok4=117, wfm4=118,
        aofm=64, fwTs=72,
    )

    def chunk_tiles4():
        ofs = 128 * 1024
        d = {}

        def ca(name, shape, dt):
            nonlocal ofs
            nbytes = int(np.prod(shape[1:])) * (2 if dt == BF16 else 4)
            t = nc.alloc_sbuf_tensor_at("s_" + name, list(shape), dt,
                                        offset=pool_base + ofs)
            ofs = (ofs + nbytes + 31) // 32 * 32
            pool[name] = t
            P.set_region(name, "RC")
            assert ofs <= 158 * 1024, f"chunk4 overflow {name} at {ofs}"
            return t

        d["gsb"] = ca("gsb", (128, 4), F32)
        d["biasc"] = ca("biasc", (128, 4), F32)
        d["ctc"] = ca("ctc", (128, 4), F32)
        d["clast4"] = ca("clast4", (128, 4), F32)
        d["tiny4"] = ca("tiny4", (128, 12), F32)
        d["growall"] = ca("growall", (1, 512), F32)
        d["dincl4"] = ca("dincl4", (128, 512), F32)
        d["dstr4"] = ca("dstr4", (128, 512), F32)
        d["at4"] = ca("at4", (128, 512), BF16)
        d["wt4"] = ca("wt4", (128, 512), BF16)
        d["an4"] = ca("an4", (128, 512), BF16)
        d["mc4"] = ca("mc4", (128, 1024), F32)
        d["rr4"] = ca("rr4", (128, 1024), BF16)
        d["u04"] = ca("u04", (128, 1024), BF16)
        d["u14"] = ca("u14", (128, 1024), BF16)
        d["ktok4"] = ca("ktok4", (128, 1024), BF16)
        d["pp4a"] = ca("pp4a", (128, 512), BF16)
        d["pp4b"] = ca("pp4b", (128, 512), BF16)
        d["bp4a"] = ca("bp4a", (128, 512), BF16)
        d["bp4b"] = ca("bp4b", (128, 512), BF16)
        d["readf4"] = ca("readf4", (128, 1024), F32)
        return d

    def rd_tiles4():
        t1 = nc.alloc_sbuf_tensor_at("s_readn4", [128, D], BF16,
                                     offset=pool_base + 104 * 1024)
        t2 = nc.alloc_sbuf_tensor_at("s_readfm4", [128, D], BF16,
                                     offset=pool_base + 106 * 1024)
        t3 = nc.alloc_sbuf_tensor_at("s_dnbase", [128, D], F32,
                                     offset=pool_base + 108 * 1024)
        for t, n in ((t1, "readn4"), (t2, "readfm4"), (t3, "dnbase")):
            pool[n] = t
            P.set_region(n, "RB")
        return t1, t2, t3

    pbw = [nc.alloc_psum_tensor(f"pw{i}", [128, 1024], F32) for i in range(2)]

    class BankView:
        def __init__(self, t, coff):
            self.t, self.coff = t, coff

        def __getitem__(self, idx):
            r, c = idx
            cs = c.start or 0
            ce = c.stop if c.stop is not None else 512
            return self.t[r, self.coff + cs: self.coff + ce]

    pbank = [BankView(pbw[0], 0), BankView(pbw[0], 512),
             BankView(pbw[1], 0), BankView(pbw[1], 512),
             nc.alloc_psum_tensor("pb4", [128, 512], F32),
             nc.alloc_psum_tensor("pb5", [128, 512], F32)]
    pbtr = [nc.alloc_psum_tensor(f"pt{i}", [128, 1024], BF16) for i in range(2)]

    def st(engine, emit, reads=(), writes=(), dma=False):
        return P.step(engine, emit, reads=reads, writes=writes, dma=dma)

    def dmac(engine, dst, src, reads, writes):
        return P.step(engine, (lambda e, d_=dst, s_=src: e.dma_start(d_, s_)),
                      reads=reads, writes=writes, dma=True)

    def loads_init():
        for t, h, key in ((identb, con_h["identb"], "identb"),
                          (identf, con_h["identf"], "identf"),
                          (onesf, con_h["onesf"], "onesf"),
                          (onescolb, con_h["onescolb"], "onescolb"),
                          (uincf, con_h["uincf"], "uincf"),
                          (maskaddT, con_h["maskaddTf"], "maskaddT"),
                          (offdiag, con_h["offdiagf"], "offdiag"),
                          (offdiag4, con_h["offdiag4f"], "offdiag4"),
                          (e127, con_h["e127f"], "e127"),
                          (esel0, con_h["esel0f"], "esel0"),
                          (esel1, con_h["esel1f"], "esel1"),
                          (esel0b, con_h["esel0b"], "esel0b"),
                          (esel1b, con_h["esel1b"], "esel1b"),
                          (headsel, con_h["headsel"], "headsel"),
                          (blocksel, con_h["blockself"], "blocksel"),
                          (headmap, con_h["headmapf"], "headmap"),
                          (pswapT, con_h["pswapT"], "pswapT"),
                          (bogc, bog_h, "bogc"), (babr, bab_h, "babr"),
                          (boutc, bout_h, "boutc"),
                          (qbc, qb_h, "qbc"), (kbc, kb_h, "kbc"),
                          (vbrow, vb_h, "vbrow"), (gbrow, gb_h, "gbrow"),
                          (fbrow, fb_h, "fbrow"),
                          (rampt, ramp_h, "rampl")):
            dmac("sp", t[:], h, reads=(), writes=(key,))
        dmac("sp", wconv[:], wconv_h, reads=(), writes=("wconv",))
        dmac("gp", gwTs[:], gwT_h, reads=(), writes=("gwTs",))
        dmac("gp", ebTs[:], ebT_h, reads=(), writes=("ebTs",))
        dmac("gp", wabsb[:], wab_h, reads=(), writes=("wabsb",))
        st("act", lambda e: e.activation(out=rampc[:], in_=rampt[:], func=AF.Tanh,
                                         scale=0.5),
           reads=("rampl",), writes=("rampc",))
        st("dve", lambda e: e.tensor_scalar(rampc[:], rampc[:], 0.5, 0.5,
                                            ALU.mult, ALU.add),
           reads=("rampc",), writes=("rampc",))
        st("dve", lambda e: e.tensor_scalar(bogc2[:], bogc[:], 0.5, None, ALU.mult),
           reads=("bogc",), writes=("bogc2",))
        st("dve", lambda e: e.memset(cpe5[:], 1e-5), reads=(), writes=("constpool",))
        st("dve", lambda e: e.memset(cpe6[:], 1e-6), reads=(), writes=("constpool",))
        st("dve", lambda e: e.memset(maccq[:], 0.0), reads=(), writes=("maccq",))
        st("dve", lambda e: e.memset(macck[:], 0.0), reads=(), writes=("macck",))
        st("dve", lambda e: e.memset(tails[:], 0.0), reads=(), writes=("tails",))

        def bcast_row(dst, row, rowkey, dstkey, n, bank):
            for j in range(0, n, 512):
                w = min(512, n - j)
                st("pe", (lambda e, jj=j, ww=w, rr=row, bb=bank:
                          e.matmul(pbank[bb][:, 0:ww], onesf[:],
                                   rr[0:1, jj:jj + ww], start=True, stop=True)),
                   reads=("onesf", rowkey), writes=(f"pb{bank}",))
                st("dve", (lambda e, dd=dst, jj=j, ww=w, bb=bank:
                           e.tensor_copy(dd[:, jj:jj + ww], pbank[bb][:, 0:ww])),
                   reads=(f"pb{bank}",), writes=(dstkey,))
        bcast_row(vbb, vbrow, "vbrow", "vbb", D, 0)
        bcast_row(fbb, fbrow, "fbrow", "fbb", D, 1)
        bcast_row(gbb, gbrow, "gbrow", "gbb", 8, 2)
        bcast_row(babb, babr, "babr", "babb", 8, 3)
        st("dve", lambda e: e.memset(st32[:], 0.0), reads=(),
           writes=tuple(("st32", h, kc) for h in range(4) for kc in range(2)))
        st("dve", lambda e: e.memset(stbf[:], 0.0), reads=(),
           writes=tuple(("stbf", h, kc) for h in range(4) for kc in range(2)))

    def load_wchunks(dst, dstkey, src_h, n_ic, width):
        for ic in range(n_ic):
            dmac("gp", dst[:, ic * width:(ic + 1) * width],
                 src_h[ic * 128:(ic + 1) * 128, :],
                 reads=(), writes=((dstkey, ic),))

    def phase0():
        for i in range(NT):
            p = i % 2
            xtile, xbtile, xntile = xt[p], xbf[p], xn32t[p]
            kx, kxb, kxn = f"xt{p}", f"xbf{p}", f"xn32t{p}"
            dmac("sp", xtile[:], x_h[i * 128:(i + 1) * 128, :],
                 reads=(), writes=(kx,))
            st("dve", (lambda e, xx=xtile: e.tensor_reduce(
                lnst[:, 0:1], xx[:], AX.X, ALU.add)),
               reads=(kx,), writes=("lnst0",))
            st("dve", lambda e: e.tensor_scalar(
                lnst[:, 1:2], lnst[:, 0:1], -1.0 / D, None, ALU.mult),
               reads=("lnst0",), writes=("lnst1",))
            st("dve", (lambda e, xx=xtile: e.tensor_mul(x2t[:], xx[:], xx[:])),
               reads=(kx,), writes=("x2t",))
            st("dve", lambda e: e.tensor_reduce(
                lnst[:, 2:3], x2t[:], AX.X, ALU.add),
               reads=("x2t",), writes=("lnst2",))
            st("dve", lambda e: e.tensor_scalar(
                lnst[:, 3:4], lnst[:, 2:3], 1.0 / D, None, ALU.mult),
               reads=("lnst2",), writes=("lnst3",))
            st("dve", lambda e: e.tensor_mul(
                lnst[:, 4:5], lnst[:, 1:2], lnst[:, 1:2]),
               reads=("lnst1",), writes=("lnst4",))
            st("dve", lambda e: e.tensor_sub(
                lnst[:, 5:6], lnst[:, 3:4], lnst[:, 4:5]),
               reads=("lnst3", "lnst4"), writes=("lnst5",))
            st("act", lambda e: e.activation(
                out=lnst[:, 6:7], in_=lnst[:, 5:6], func=AF.Ln, bias=1e-5),
               reads=("lnst5", "constpool"), writes=("lnst6",))
            st("act", lambda e: e.activation(
                out=lnst[:, 7:8], in_=lnst[:, 6:7], func=AF.Exp, scale=-0.5),
               reads=("lnst6",), writes=("lnst7",))
            st("dve", (lambda e, xx=xtile, xo=xntile: e.tensor_scalar(
                xo[:], xx[:], lnst[:, 1:2], lnst[:, 7:8], ALU.add, ALU.mult)),
               reads=(kx, "lnst1", "lnst7"), writes=(kxn,))
            dmac("gp", xn32_d[i * 128:(i + 1) * 128, :], xntile[:],
                 reads=(kxn,), writes=(("xn32d", i),))
            st("dve", (lambda e, xx=xtile, xo=xntile: e.tensor_add(
                x2t[:], xx[:], xo[:])),
               reads=(kx, kxn), writes=("x2t",))
            dmac("gp", base_d[i * 128:(i + 1) * 128, :], x2t[:],
                 reads=("x2t",), writes=(("based", i),))
            st("dve", (lambda e, xo=xntile, xb=xbtile: e.tensor_copy(
                xb[:], xo[:])),
               reads=(kxn,), writes=(kxb,))
            for ic in range(8):
                bank = ic % 2
                st("pe", (lambda e, xb=xbtile, icc=ic, bb=bank: e.transpose(
                    pbtr[bb][:, 0:128], xb[:, icc * 128:(icc + 1) * 128],
                    identb[:])),
                   reads=(kxb, "identb"), writes=(f"pt{bank}",))
                st("dve", (lambda e, icc=ic, ii=i, bb=bank: e.tensor_copy(
                    xnfm[:, icc * S + ii * 128: icc * S + (ii + 1) * 128],
                    pbtr[bb][:, 0:128])),
                   reads=(f"pt{bank}",), writes=(("xnfm", ic, i),))

    def phase1a(g):
        winsb, wogsb, praw = g["winsb"], g["wogsb"], g["praw"]
        load_wchunks(winsb, "winsb", win_h, 8, 3 * D)
        load_wchunks(wogsb, "wogsb", wog_h, 8, D)
        for s_ in range(NSL):
            t0 = s_ * SL
            xn_reads = tuple(("xnfm", ic, t) for ic in range(8)
                             for t in range(s_ * 4, s_ * 4 + 4))
            for oc in range(24):
                qkv, occ = oc // 8, oc % 8
                bank = oc % 2
                for ic in range(8):
                    st("pe", (lambda e, oo=oc, ii=ic, bb=bank, tt=t0:
                              e.matmul(pbank[bb][:, :],
                                       winsb[:, ii * 3 * D + oo * 128:
                                             ii * 3 * D + oo * 128 + 128],
                                       xnfm[:, ii * S + tt: ii * S + tt + SL],
                                       start=(ii == 0), stop=(ii == 7))),
                       reads=(("winsb", ic),) + xn_reads,
                       writes=(f"pb{bank}",))
                pr = praw[oc % 4]
                kpr = f"praw{oc % 4}"
                st("dve", (lambda e, pr_=pr, oo=oc: e.tensor_copy(
                    pr_[:, 0:3], tails[:, oo * 3:oo * 3 + 3])),
                   reads=(("tails", oc),), writes=((kpr, "head"),))
                st("act", (lambda e, pr_=pr, bb=bank: e.activation(
                    out=pr_[:, 3:SL + 3], in_=pbank[bb][:, :],
                    func=AF.Identity)),
                   reads=(f"pb{bank}",), writes=(kpr,))
                st("dve", (lambda e, pr_=pr, oo=oc: e.tensor_copy(
                    tails[:, oo * 3:oo * 3 + 3], pr_[:, SL:SL + 3])),
                   reads=(kpr,), writes=(("tails", oc),))
                wc0 = qkv * 32 + occ * 4
                ceng = "dve"
                cvt = convt if oc % 2 == 0 else convt2
                kcv = "convt" if oc % 2 == 0 else "convt2"
                st(ceng, (lambda e, pr_=pr, w0=wc0, cv=cvt: e.tensor_scalar(
                    cv[:], pr_[:, 0:SL], wconv[:, w0:w0 + 1], None,
                    ALU.mult)),
                   reads=(kpr, (kpr, "head"), "wconv"), writes=(kcv,))
                for tap in range(1, 4):
                    st(ceng, (lambda e, pr_=pr, wc=wc0 + tap, tp=tap, cv=cvt:
                               e.scalar_tensor_tensor(
                                   cv[:], pr_[:, tp:SL + tp],
                                   wconv[:, wc:wc + 1], cv[:],
                                   ALU.mult, ALU.add)),
                       reads=(kpr, (kpr, "head"), "wconv", kcv),
                       writes=(kcv,))
                st("act", (lambda e, cv=cvt: e.activation(
                    out=cv[:], in_=cv[:], func=AF.Silu)),
                   reads=(kcv,), writes=(kcv,))
                dst_d = (qc_d, kc_d, vc_d)[qkv]
                dmac("gp", dst_d[occ * 128:(occ + 1) * 128, t0:t0 + SL],
                     cvt[:], reads=(kcv,),
                     writes=((("qc", "kc", "vc")[qkv], occ, s_),))
            for oc in range(8):
                bank = 2 + (oc % 2)
                for ic in range(8):
                    st("pe", (lambda e, oo=oc, ii=ic, bb=bank, tt=t0:
                              e.matmul(pbank[bb][:, :],
                                       wogsb[:, ii * D + oo * 128:
                                             ii * D + oo * 128 + 128],
                                       xnfm[:, ii * S + tt: ii * S + tt + SL],
                                       start=(ii == 0), stop=(ii == 7))),
                       reads=(("wogsb", ic),) + xn_reads,
                       writes=(f"pb{bank}",))
                p = oc % 2
                ogt = g["ogt"][p]
                st("act", (lambda e, oo=oc, bb=bank, og=ogt: e.activation(
                    out=og[:], in_=pbank[bb][:, :], func=AF.Tanh,
                    scale=0.5, bias=bogc2[:, oo:oo + 1])),
                   reads=(f"pb{bank}", "bogc2"), writes=(f"ogt{p}",))
                st("dve", (lambda e, og=ogt: e.tensor_scalar(
                    og[:], og[:], 0.5, 0.5, ALU.mult, ALU.add)),
                   reads=(f"ogt{p}",), writes=(f"ogt{p}",))
                dmac("gp", oglog_d[oc * 128:(oc + 1) * 128, t0:t0 + SL],
                     ogt[:], reads=(f"ogt{p}",), writes=(("oglog", oc, s_),))
            for tt in range(4):
                ti = s_ * 4 + tt
                bank = 4
                for ic in range(8):
                    st("pe", (lambda e, ii=ic, tt_=ti, bb=bank: e.matmul(
                        pbank[bb][:, 0:8],
                        xnfm[:, ii * S + tt_ * 128: ii * S + tt_ * 128 + 128],
                        wabsb[:, ii * 8:(ii + 1) * 8],
                        start=(ii == 0), stop=(ii == 7))),
                       reads=xn_reads + ("wabsb",), writes=(f"pb{bank}",))
                st("dve", (lambda e, tt_=ti, bb=bank: e.tensor_add(
                    absb[:, tt_ * 8:(tt_ + 1) * 8], pbank[bb][:, 0:8],
                    babb[:, 0:8])),
                   reads=(f"pb{bank}", "babb"), writes=(("absb", ti),))
                st("act", (lambda e, tt_=ti: e.activation(
                    out=absb[:, tt_ * 8:(tt_ + 1) * 8],
                    in_=absb[:, tt_ * 8:(tt_ + 1) * 8], func=AF.Tanh,
                    scale=0.5)),
                   reads=(("absb", ti),), writes=(("absb", ti),))
                st("dve", (lambda e, tt_=ti: e.tensor_scalar(
                    absb[:, tt_ * 8:tt_ * 8 + 4],
                    absb[:, tt_ * 8:tt_ * 8 + 4], 0.49, 0.5,
                    ALU.mult, ALU.add)),
                   reads=(("absb", ti),), writes=(("absb", ti),))
                st("dve", (lambda e, tt_=ti: e.tensor_scalar(
                    absb[:, tt_ * 8 + 4:tt_ * 8 + 8],
                    absb[:, tt_ * 8 + 4:tt_ * 8 + 8], 0.5, 0.5,
                    ALU.mult, ALU.add)),
                   reads=(("absb", ti),), writes=(("absb", ti),))

    def phase1b(g):
        woutsb = g["woutsb"]
        qnsl, knsl, vnsl, ogsl = g["qnsl"], g["knsl"], g["vnsl"], g["ogsl"]
        sqtl, rrow = g["sqt"], g["rrow"]
        d = cht4
        rn, rfm, dnb = readn4_t, readfm4_t, dnbase_t
        load_wchunks(woutsb, "woutsb", wout_h, 8, D)
        PW0 = ("pb0", "pb1")
        PW1 = ("pb2", "pb3")
        for s_ in range(NSL):
            t0 = s_ * SL
            for oc in range(8):
                dmac("sp", qnsl[:, oc * SL:(oc + 1) * SL],
                     qc_d[oc * 128:(oc + 1) * 128, t0:t0 + SL],
                     reads=(("qc", oc, s_), ("qnsl", oc)), writes=(("qnsl", oc),))
                dmac("sp", knsl[:, oc * SL:(oc + 1) * SL],
                     kc_d[oc * 128:(oc + 1) * 128, t0:t0 + SL],
                     reads=(("kc", oc, s_), ("knsl", oc)), writes=(("knsl", oc),))
                dmac("sp", vnsl[:, oc * SL:(oc + 1) * SL],
                     vc_d[oc * 128:(oc + 1) * 128, t0:t0 + SL],
                     reads=(("vc", oc, s_), ("vnsl", oc)), writes=(("vnsl", oc),))
                dmac("sp", ogsl[:, oc * SL:(oc + 1) * SL],
                     oglog_d[oc * 128:(oc + 1) * 128, t0:t0 + SL],
                     reads=(("oglog", oc, s_), ("ogsl", oc)), writes=(("ogsl", oc),))
            for which, tile in (("q", qnsl), ("k", knsl)):
                kk = "qnsl" if which == "q" else "knsl"
                for h in range(4):
                    bank = h
                    for half in range(2):
                        oc = 2 * h + half
                        st("dve", (lambda e, tl=tile, oo=oc: e.tensor_mul(
                            sqtl[:], tl[:, oo * SL:(oo + 1) * SL],
                            tl[:, oo * SL:(oo + 1) * SL])),
                           reads=((kk, oc),), writes=("sqt",))
                        st("pe", (lambda e, hf=half, bb=bank: e.matmul(
                            pbank[bb][0:1, 0:SL], onescolb[:], sqtl[:],
                            start=(hf == 0), stop=(hf == 1))),
                           reads=("sqt", "onescolb"), writes=(f"pb{bank}",))
                    st("dve", (lambda e, bb=bank: e.tensor_scalar(
                        rrow[0:1, 0:SL], pbank[bb][0:1, 0:SL],
                        1e-12, None, ALU.max)),
                       reads=(f"pb{bank}",), writes=("rrow",))
                    st("act", lambda e: e.activation(
                        out=rrow[0:1, 0:SL], in_=rrow[0:1, 0:SL], func=AF.Ln),
                       reads=("rrow",), writes=("rrow",))
                    st("act", lambda e: e.activation(
                        out=rrow[0:1, 0:SL], in_=rrow[0:1, 0:SL],
                        func=AF.Exp, scale=-0.5),
                       reads=("rrow",), writes=("rrow",))
                    st("dve", lambda e: e.tensor_scalar(
                        rrow[0:1, 0:SL], rrow[0:1, 0:SL], 1e6, None, ALU.min),
                       reads=("rrow",), writes=("rrow",))
                    bank2 = 4
                    st("pe", (lambda e, bb=bank2: e.matmul(
                        pbank[bb][:, 0:SL], onesf[:], rrow[0:1, 0:SL],
                        start=True, stop=True)),
                       reads=("rrow", "onesf"), writes=(f"pb{bank2}",))
                    for half in range(2):
                        oc = 2 * h + half
                        st("dve", (lambda e, tl=tile, oo=oc, bb=bank2:
                                   e.tensor_mul(
                                       tl[:, oo * SL:(oo + 1) * SL],
                                       tl[:, oo * SL:(oo + 1) * SL],
                                       pbank[bb][:, 0:SL])),
                           reads=((kk, oc), f"pb{bank2}"), writes=((kk, oc),))
            for tt in range(4):
                ti = s_ * 4 + tt
                toff = tt * 128

                def qsl(h, half):
                    oc = 2 * h + half
                    return qnsl[:, oc * SL + toff: oc * SL + toff + 128]

                def ksl(h, half):
                    oc = 2 * h + half
                    return knsl[:, oc * SL + toff: oc * SL + toff + 128]

                def vsl(h, half):
                    oc = 2 * h + half
                    return vnsl[:, oc * SL + toff: oc * SL + toff + 128]

                kq_r = tuple(("qnsl", oc) for oc in range(8))
                kk_r = tuple(("knsl", oc) for oc in range(8))
                kv_r = tuple(("vnsl", oc) for oc in range(8))
                st("act", (lambda e, tt_=ti: e.activation(
                    out=d["gsb"][:, 0:4], in_=absb[:, tt_ * 8:tt_ * 8 + 4],
                    func=AF.Ln)),
                   reads=(("absb", ti),), writes=("gsb",))
                st("act", (lambda e, tt_=ti: e.activation(
                    out=d["biasc"][:, 0:4],
                    in_=absb[:, tt_ * 8 + 4:tt_ * 8 + 8], func=AF.Ln)),
                   reads=(("absb", ti),), writes=("biasc",))
                st("pe", lambda e: e.matmul(
                    pbank[5][:, 0:4], uincf[:], d["gsb"][:, 0:4],
                    start=True, stop=True),
                   reads=("gsb", "uincf"), writes=("pb5",))
                st("dve", lambda e: e.tensor_copy(d["gsb"][:, 0:4],
                                                  pbank[5][:, 0:4]),
                   reads=("pb5",), writes=("gsb",))
                st("dve", lambda e: e.tensor_sub(
                    d["biasc"][:, 0:4], d["biasc"][:, 0:4], d["gsb"][:, 0:4]),
                   reads=("biasc", "gsb"), writes=("biasc",))
                st("act", lambda e: e.activation(
                    out=d["ctc"][:, 0:4], in_=d["gsb"][:, 0:4], func=AF.Exp),
                   reads=("gsb",), writes=("ctc",))
                st("pe", lambda e: e.matmul(
                    pbank[5][:, 4:8], e127[:], d["ctc"][:, 0:4],
                    start=True, stop=True),
                   reads=("ctc", "e127"), writes=("pb5",))
                st("dve", lambda e: e.tensor_copy(d["clast4"][:, 0:4],
                                                  pbank[5][:, 4:8]),
                   reads=("pb5",), writes=("clast4",))
                for h in range(4):
                    st("pe", (lambda e, hh=h: e.matmul(
                        pbank[5][0:1, hh * 128:(hh + 1) * 128],
                        d["gsb"][:, hh:hh + 1], identf[:],
                        start=True, stop=True)),
                       reads=("gsb", "identf"), writes=("pb5",))
                st("dve", lambda e: e.tensor_copy(d["growall"][0:1, 0:512],
                                                  pbank[5][0:1, 0:512]),
                   reads=("pb5",), writes=("growall",))
                for h in range(4):
                    st("pe", (lambda e, hh=h: e.matmul(
                        pbank[4][:, hh * 128:(hh + 1) * 128], onesf[:],
                        d["growall"][0:1, hh * 128:(hh + 1) * 128],
                        start=True, stop=False)),
                       reads=("growall", "onesf"), writes=("pb4",))
                    st("pe", (lambda e, hh=h: e.matmul(
                        pbank[4][:, hh * 128:(hh + 1) * 128], maskaddT[:],
                        identf[:], start=False, stop=True)),
                       reads=("maskaddT", "identf"), writes=("pb4",))
                for h in range(4):
                    st("act", (lambda e, hh=h: e.activation(
                        out=d["dincl4"][:, hh * 128:(hh + 1) * 128],
                        in_=pbank[4][:, hh * 128:(hh + 1) * 128], func=AF.Exp,
                        bias=d["biasc"][:, hh:hh + 1])),
                       reads=("pb4", "biasc"), writes=("dincl4",))
                st("dve", lambda e: e.tensor_mul(d["dstr4"][:], d["dincl4"][:],
                                                 offdiag4[:]),
                   reads=("dincl4", "offdiag4"), writes=("dstr4",))
                for h in range(4):
                    for half in range(2):
                        st("pe", (lambda e, hh=h, hf=half, ka=ksl(h, half):
                                  e.matmul(
                                      pbw[0][:, hh * 128:(hh + 1) * 128],
                                      ka, ka,
                                      start=(hf == 0), stop=(hf == 1))),
                           reads=kk_r, writes=PW0)
                st("dve", lambda e: e.tensor_mul(
                    d["at4"][:], pbw[0][:, 0:512], d["dstr4"][:]),
                   reads=PW0 + ("dstr4",), writes=("at4",))
                for h in range(4):
                    for half in range(2):
                        st("pe", (lambda e, hh=h, hf=half, ka=ksl(h, half),
                                  qa=qsl(h, half): e.matmul(
                                      pbw[1][:, hh * 128:(hh + 1) * 128],
                                      ka, qa,
                                      start=(hf == 0), stop=(hf == 1))),
                           reads=kk_r + kq_r, writes=PW1)
                st("dve", lambda e: e.tensor_mul(
                    d["wt4"][:], pbw[1][:, 0:512], d["dincl4"][:]),
                   reads=PW1 + ("dincl4",), writes=("wt4",))
                for h in range(4):
                    st("pe", (lambda e, hh=h: e.transpose(
                        pbtr[0][:, hh * 128:(hh + 1) * 128],
                        d["at4"][:, hh * 128:(hh + 1) * 128], identb[:])),
                       reads=("at4", "identb"), writes=("pt0",))
                st("act", lambda e: e.activation(
                    out=d["an4"][:], in_=pbtr[0][:, 0:512], func=AF.Identity),
                   reads=("pt0",), writes=("an4",))
                for h in range(4):
                    for half in range(2):
                        st("pe", (lambda e, hh=h, hf=half, ka=ksl(h, half):
                                  e.matmul(
                                      pbw[1][:, hh * 256: hh * 256 + 256],
                                      ka,
                                      stbf[:, (2 * hh + hf) * 256:
                                           (2 * hh + hf) * 256 + 256],
                                      start=(hf == 0), stop=(hf == 1))),
                           reads=kk_r + (("stbf", h, half),), writes=PW1)
                for h in range(4):
                    st("act", (lambda e, hh=h: e.activation(
                        out=d["mc4"][:, hh * 256: hh * 256 + 256],
                        in_=pbw[1][:, hh * 256: hh * 256 + 256],
                        func=AF.Identity, scale=d["ctc"][:, hh:hh + 1])),
                       reads=PW1 + ("ctc",), writes=("mc4",))
                for h in range(4):
                    for half in range(2):
                        st("pe", (lambda e, hh=h, hf=half, va=vsl(h, half):
                                  e.transpose(
                                      pbtr[1][:, (2 * hh + hf) * 128:
                                              (2 * hh + hf) * 128 + 128],
                                      va, identb[:])),
                           reads=kv_r + ("identb",), writes=("pt1",))
                st("dve", lambda e: e.tensor_sub(
                    d["rr4"][:], pbtr[1][:, 0:1024], d["mc4"][:]),
                   reads=("pt1", "mc4"), writes=("rr4",))
                for h in range(4):
                    st("pe", (lambda e, hh=h: e.matmul(
                        pbw[0][:, hh * 256: hh * 256 + 256],
                        d["at4"][:, hh * 128:(hh + 1) * 128],
                        d["rr4"][:, hh * 256: hh * 256 + 256],
                        start=True, stop=False)),
                       reads=("at4", "rr4"), writes=PW0)
                    st("pe", (lambda e, hh=h: e.matmul(
                        pbw[0][:, hh * 256: hh * 256 + 256],
                        identb[:],
                        d["rr4"][:, hh * 256: hh * 256 + 256],
                        start=False, stop=True)),
                       reads=("identb", "rr4"), writes=PW0)
                st("act", lambda e: e.activation(
                    out=d["u04"][:], in_=pbw[0][:, 0:1024], func=AF.Identity),
                   reads=PW0, writes=("u04",))
                cur_p, cur_b, cur_u = "at4", "an4", "u04"
                for lev, pwr in enumerate((2, 4, 8, 16, 32, 64)):
                    new_p = ("pp4a", "pp4b")[lev % 2]
                    new_b = ("bp4a", "bp4b")[lev % 2]
                    for h in range(4):
                        if pwr < 64:
                            st("pe", (lambda e, hh=h, cp=cur_p, cb=cur_b:
                                      e.matmul(
                                          pbw[1][:, hh * 128:(hh + 1) * 128],
                                          d[cp][:, hh * 128:(hh + 1) * 128],
                                          d[cb][:, hh * 128:(hh + 1) * 128],
                                          start=True, stop=True)),
                               reads=(cur_p, cur_b), writes=PW1)
                        st("pe", (lambda e, hh=h, cp=cur_p, cb=cur_b: e.matmul(
                            pbw[1][:, 512 + hh * 128: 512 + (hh + 1) * 128],
                            d[cb][:, hh * 128:(hh + 1) * 128],
                            d[cp][:, hh * 128:(hh + 1) * 128],
                            start=True, stop=True)),
                           reads=(cur_p, cur_b), writes=PW1)
                    if pwr < 64:
                        st("act", (lambda e, nb=new_b: e.activation(
                            out=d[nb][:], in_=pbw[1][:, 0:512],
                            func=AF.Identity)),
                           reads=PW1, writes=(new_b,))
                    st("act", (lambda e, np_=new_p: e.activation(
                        out=d[np_][:], in_=pbw[1][:, 512:1024],
                        func=AF.Identity)),
                       reads=PW1, writes=(new_p,))
                    for h in range(4):
                        st("pe", (lambda e, hh=h, np_=new_p, cu=cur_u: e.matmul(
                            pbw[0][:, hh * 256: hh * 256 + 256],
                            d[np_][:, hh * 128:(hh + 1) * 128],
                            d[cu][:, hh * 256: hh * 256 + 256],
                            start=True, stop=False)),
                           reads=(new_p, cur_u), writes=PW0)
                        st("pe", (lambda e, hh=h, cu=cur_u: e.matmul(
                            pbw[0][:, hh * 256: hh * 256 + 256],
                            identb[:],
                            d[cu][:, hh * 256: hh * 256 + 256],
                            start=False, stop=True)),
                           reads=("identb", cur_u), writes=PW0)
                    nxt_u = "u14" if cur_u == "u04" else "u04"
                    st("act", (lambda e, nu=nxt_u: e.activation(
                        out=d[nu][:], in_=pbw[0][:, 0:1024], func=AF.Identity)),
                       reads=PW0, writes=(nxt_u,))
                    cur_p, cur_b, cur_u = new_p, new_b, nxt_u
                for h in range(4):
                    for half in range(2):
                        st("pe", (lambda e, hh=h, hf=half, qa=qsl(h, half):
                                  e.matmul(
                                      pbw[1][:, hh * 256: hh * 256 + 256],
                                      qa,
                                      stbf[:, (2 * hh + hf) * 256:
                                           (2 * hh + hf) * 256 + 256],
                                      start=(hf == 0), stop=(hf == 1))),
                           reads=kq_r + (("stbf", h, half),), writes=PW1)
                for h in range(4):
                    st("act", (lambda e, hh=h: e.activation(
                        out=d["readf4"][:, hh * 256: hh * 256 + 256],
                        in_=pbw[1][:, hh * 256: hh * 256 + 256],
                        func=AF.Identity, scale=d["ctc"][:, hh:hh + 1])),
                       reads=PW1 + ("ctc",), writes=("readf4",))
                for h in range(4):
                    st("pe", (lambda e, hh=h, cu=cur_u: e.matmul(
                        pbw[0][:, hh * 256: hh * 256 + 256],
                        d["wt4"][:, hh * 128:(hh + 1) * 128],
                        d[cu][:, hh * 256: hh * 256 + 256],
                        start=True, stop=True)),
                       reads=("wt4", cur_u), writes=PW0)
                st("dve", lambda e: e.tensor_add(
                    d["readf4"][:], d["readf4"][:], pbw[0][:, 0:1024]),
                   reads=("readf4",) + PW0, writes=("readf4",))
                for h in range(4):
                    st("act", (lambda e, hh=h: e.activation(
                        out=d["mc4"][:, hh * 256: hh * 256 + 256],
                        in_=d["readf4"][:, hh * 256: hh * 256 + 256],
                        func=AF.Square, accum_out=d["tiny4"][:, hh:hh + 1])),
                       reads=("readf4",), writes=("mc4", "tiny4"))
                st("act", lambda e: e.activation(
                    out=d["tiny4"][:, 4:8], in_=d["tiny4"][:, 0:4], func=AF.Ln,
                    scale=1.0 / 256, bias=1e-6),
                   reads=("tiny4", "constpool"), writes=("tiny4",))
                st("act", lambda e: e.activation(
                    out=d["tiny4"][:, 8:12], in_=d["tiny4"][:, 4:8],
                    func=AF.Exp, scale=-0.5),
                   reads=("tiny4",), writes=("tiny4",))
                for h in range(4):
                    st("dve", (lambda e, hh=h: e.tensor_scalar(
                        rn[:, hh * 256:(hh + 1) * 256],
                        d["readf4"][:, hh * 256: hh * 256 + 256],
                        d["tiny4"][:, 8 + hh:9 + hh], None, ALU.mult)),
                       reads=("readf4", "tiny4"), writes=(("readn4", h),))
                for h in range(4):
                    for half in range(2):
                        st("pe", (lambda e, hh=h, hf=half, ka=ksl(h, half):
                                  e.transpose(
                                      pbtr[1][:, (2 * hh + hf) * 128:
                                              (2 * hh + hf) * 128 + 128],
                                      ka, identb[:])),
                           reads=kk_r + ("identb",), writes=("pt1",))
                for h in range(4):
                    st("dve", (lambda e, hh=h: e.tensor_scalar(
                        d["ktok4"][:, hh * 256: hh * 256 + 256],
                        pbtr[1][:, hh * 256: hh * 256 + 256],
                        d["dincl4"][:, hh * 128 + 127: hh * 128 + 128],
                        None, ALU.mult)),
                       reads=("pt1", "dincl4"), writes=("ktok4",))
                for h in range(4):
                    for kc in range(2):
                        dstp = (pbw[0] if h < 2 else pbw[1])
                        ko = ((2 * h + kc) % 4) * 256
                        st("pe", (lambda e, hh=h, cc=kc, cu=cur_u, dd=dstp,
                                  oo=ko: e.matmul(
                                      dd[:, oo: oo + 256],
                                      d["ktok4"][:, (2 * hh + cc) * 128:
                                                 (2 * hh + cc) * 128 + 128],
                                      d[cu][:, hh * 256: hh * 256 + 256],
                                      start=True, stop=True)),
                           reads=("ktok4", cur_u),
                           writes=(PW0 if h < 2 else PW1))
                for h in range(4):
                    for kc in range(2):
                        srcp = (pbw[0] if h < 2 else pbw[1])
                        ko = ((2 * h + kc) % 4) * 256
                        sl32 = st32[:, (2 * h + kc) * 256:(2 * h + kc) * 256 + 256]
                        st("dve", (lambda e, hh=h, s3=sl32, sp=srcp, oo=ko:
                                   e.scalar_tensor_tensor(
                                       s3, s3, d["clast4"][:, hh:hh + 1],
                                       sp[:, oo: oo + 256], ALU.mult, ALU.add)),
                           reads=(("st32", h, kc), "clast4") +
                                 (PW0 if h < 2 else PW1),
                           writes=(("st32", h, kc),))
                for half in range(2):
                    st("act", (lambda e, hf=half: e.activation(
                        out=stbf[:, hf * 1024:(hf + 1) * 1024],
                        in_=st32[:, hf * 1024:(hf + 1) * 1024],
                        func=AF.Identity)),
                       reads=tuple(("st32", h2, kc2) for h2 in range(4)
                                   for kc2 in range(2)),
                       writes=tuple(("stbf", h2, kc2)
                                    for h2 in range(2 * half, 2 * half + 2)
                                    for kc2 in range(2)))
                if "readn" in dbg_outs:
                    dmac("gp", readn_d[ti * 128:(ti + 1) * 128, :], rn[:],
                         reads=tuple(("readn4", h2) for h2 in range(4)),
                         writes=(("readnd", ti),))
                _chunk_tail4(ti, tt, woutsb, ogsl, rn, rfm, dnb, d)

    def _chunk_tail4(ti, tt, woutsb, ogsl, rn, rfm, dnb, d):
        krn = tuple(("readn4", h) for h in range(4))
        for vc in range(8):
            bank = vc % 2
            st("pe", (lambda e, v=vc, bb=bank: e.transpose(
                pbtr[bb][:, (v // 2) * 128:(v // 2) * 128 + 128],
                rn[:, v * 128:(v + 1) * 128], identb[:])),
               reads=krn + ("identb",), writes=(f"pt{bank}",))
        for bank in range(2):
            st("dve", (lambda e, bb=bank: e.tensor_copy(
                rfm[:, bb * 512:(bb + 1) * 512], pbtr[bb][:, 0:512])),
               reads=(f"pt{bank}",), writes=(("readfm4", bank),))
        dmac("sp", dnb[:], base_d[ti * 128:(ti + 1) * 128, :],
             reads=(("based", ti), ("dnbase", "out")),
             writes=(("dnbase", "sum"),))
        for oc in range(8):
            bank = 4
            for vc in range(8):
                blk = (vc % 2) * 512 + (vc // 2) * 128
                st("pe", (lambda e, oo=oc, v=vc, bb=bank, bl=blk: e.matmul(
                    pbank[bb][:, 0:128],
                    woutsb[:, v * D + oo * 128: v * D + oo * 128 + 128],
                    rfm[:, bl: bl + 128],
                    start=(v == 0), stop=(v == 7))),
                   reads=(("woutsb", vc), ("readfm4", vc % 2)),
                   writes=(f"pb{bank}",))
            ogslice = ogsl[:, oc * SL + tt * 128: oc * SL + tt * 128 + 128]
            st("dve", (lambda e, oo=oc, bb=bank, og=ogslice:
                       e.scalar_tensor_tensor(
                           d["mc4"][:, 0:128], pbank[bb][:, 0:128],
                           boutc[:, oo:oo + 1], og, ALU.add, ALU.mult)),
               reads=(f"pb{bank}", "boutc", ("ogsl", oc), "mc4"),
               writes=("mc4",))
            st("dve", (lambda e, oo=oc: e.tensor_scalar(
                d["mc4"][:, 128:256], d["mc4"][:, 0:128],
                rampc[:, oo:oo + 1], None, ALU.mult)),
               reads=("mc4", "rampc"), writes=("mc4",))
            st("dve", lambda e: e.tensor_copy(d["rr4"][:, 0:128],
                                              d["mc4"][:, 128:256]),
               reads=("mc4", "rr4"), writes=("rr4",))
            bank2 = oc % 2
            st("pe", (lambda e, bb=bank2: e.transpose(
                pbtr[bb][:, 512:640], d["rr4"][:, 0:128], identb[:])),
               reads=("rr4", "identb"), writes=(f"pt{bank2}",))
            st("dve", (lambda e, oo=oc, bb=bank2, b_=dnb: e.tensor_add(
                b_[:, oo * 128:(oo + 1) * 128], b_[:, oo * 128:(oo + 1) * 128],
                pbtr[bb][:, 512:640])),
               reads=(f"pt{bank2}", ("dnbase", "sum")),
               writes=(("dnbase", "sum"),))
        dmac("gp", base_d[ti * 128:(ti + 1) * 128, :], dnb[:],
             reads=(("dnbase", "sum"),),
             writes=(("based", ti), ("dnbase", "out")))

    def phase2proj(g):
        qwTs, kwTs, vwTs = g["qwTs"], g["kwTs"], g["vwTs"]
        copt, sipt = g["copt"], g["sipt"]
        qpt, ropet1, ropet2, ropeo = g["qpt"], g["ropet1"], g["ropet2"], g["ropeo"]
        vtokt, sqa = g["vtokt"], g["sqa"]
        load_wchunks(qwTs, "qwTs", qwT_h, 8, D)
        load_wchunks(kwTs, "kwTs", kwT_h, 8, D)
        load_wchunks(vwTs, "vwTs", vwT_h, 8, D)
        dmac("sp", copt[:], con_h["copt"], reads=(), writes=("copt",))
        dmac("sp", sipt[:], con_h["sipt"], reads=(), writes=("sipt",))
        for s_ in range(NSL):
            t0 = s_ * SL
            xn_reads = tuple(("xnfm", ic, t) for ic in range(8)
                             for t in range(s_ * 4, s_ * 4 + 4))
            for which, wts, wkey, bias, bkey, dst_d, dkey, macc, mkey in (
                    ("q", qwTs, "qwTs", qbc, "qbc", qr_d, "qr", maccq, "maccq"),
                    ("k", kwTs, "kwTs", kbc, "kbc", kr_d, "kr", macck, "macck")):
                for oc in range(8):
                    bank = oc % 2
                    for ic in range(8):
                        st("pe", (lambda e, oo=oc, ii=ic, bb=bank, tt=t0, w=wts:
                                  e.matmul(pbank[bb][:, :],
                                           w[:, ii * D + oo * 128:
                                             ii * D + oo * 128 + 128],
                                           xnfm[:, ii * S + tt: ii * S + tt + SL],
                                           start=(ii == 0), stop=(ii == 7))),
                           reads=((wkey, ic),) + xn_reads, writes=(f"pb{bank}",))
                    pq = oc % 2
                    qt = qpt[pq]
                    st("act", (lambda e, oo=oc, bb=bank, q_=qt, b_=bias:
                               e.activation(out=q_[:], in_=pbank[bb][:, :],
                                            func=AF.Identity,
                                            bias=b_[:, oo:oo + 1])),
                       reads=(f"pb{bank}", bkey), writes=((f"qpt{pq}",),))
                    st("dve", (lambda e, q_=qt: e.tensor_mul(
                        sqa[:], q_[:], q_[:])),
                       reads=((f"qpt{pq}",),), writes=("sqa",))
                    st("pe", (lambda e: e.matmul(
                        pbank[2][0:2, 0:SL], headsel[:], sqa[:],
                        start=True, stop=True)),
                       reads=("sqa", "headsel"), writes=("pb2",))
                    st("dve", lambda e: e.tensor_reduce(
                        mtmp[:], pbank[2][0:2, 0:SL], AX.X, ALU.max),
                       reads=("pb2",), writes=("mtmp",))
                    st("dve", (lambda e, m_=macc, oo=oc: e.tensor_max(
                        m_[:, oo:oo + 1], m_[:, oo:oo + 1], mtmp[:])),
                       reads=("mtmp", mkey), writes=(mkey,))
                    st("pe", (lambda e, q_=qt: e.matmul(
                        pbank[2][:, 0:SL], pswapT[:], q_[:],
                        start=True, stop=True)),
                       reads=((f"qpt{pq}",), "pswapT"), writes=("pb2",))
                    st("dve", (lambda e, q_=qt, tt=t0: e.tensor_mul(
                        ropet1[:], q_[:], copt[:, tt:tt + SL])),
                       reads=((f"qpt{pq}",), "copt"), writes=("ropet1",))
                    st("dve", (lambda e, tt=t0: e.tensor_mul(
                        ropet2[:], pbank[2][:, 0:SL], sipt[:, tt:tt + SL])),
                       reads=("pb2", "sipt"), writes=("ropet2",))
                    st("dve", lambda e: e.tensor_add(
                        ropeo[:], ropet1[:], ropet2[:]),
                       reads=("ropet1", "ropet2"), writes=("ropeo",))
                    dmac("gp", dst_d[oc * 128:(oc + 1) * 128, t0:t0 + SL],
                         ropeo[:], reads=("ropeo",), writes=((dkey, oc, s_),))
            for u in range(4):
                ti = s_ * 4 + u
                for half in range(2):
                    bank = 2 + half
                    for ic in range(8):
                        st("pe", (lambda e, ii=ic, hf=half, bb=bank, tt_=ti:
                                  e.matmul(pbank[bb][:, :],
                                           xnfm[:, ii * S + tt_ * 128:
                                                ii * S + tt_ * 128 + 128],
                                           vwTs[:, ii * D + hf * 512:
                                                ii * D + hf * 512 + 512],
                                           start=(ii == 0), stop=(ii == 7))),
                           reads=(("vwTs", ic),) + xn_reads,
                           writes=(f"pb{bank}",))
                    st("dve", (lambda e, hf=half, bb=2 + half: e.tensor_add(
                        vtokt[:, hf * 512:hf * 512 + 512], pbank[bb][:, :],
                        vbb[:, hf * 512:hf * 512 + 512])),
                       reads=(f"pb{2 + half}", "vbb"), writes=(("vtokt", half),))
                dmac("gp", vt_d[ti * 128:(ti + 1) * 128, :], vtokt[:],
                     reads=(("vtokt", 0), ("vtokt", 1)), writes=(("vt", ti),))

    def phase2attn(g):
        khd, qhd, vslab = g["khd"], g["qhd"], g["vslab"]
        ptl = g["ptl"]
        aot = g["aot"]
        zsb = g["zsb"]
        negm, mqcol, mkcol = g["negmh"], g["mqcol"], g["mkcol"]
        st("dve", lambda e: e.memset(khd[64:128, :], 0.0),
           reads=(), writes=("khd",))
        st("dve", lambda e: e.memset(qhd[64:128, :], 0.0),
           reads=(), writes=("qhd",))
        st("dve", lambda e: e.memset(vslab[:], 0.0),
           reads=(), writes=(("vslab", "ones"),))
        st("dve", lambda e: e.memset(
            vslab[:].rearrange("p (kt c) -> p kt c", c=128)[:, :, 64:65], 1.0),
           reads=(("vslab", "ones"),), writes=(("vslab", "ones"),))
        st("dve", lambda e: e.tensor_copy(maccqb[:], maccq[:]),
           reads=("maccq",), writes=("maccqb",))
        st("dve", lambda e: e.tensor_copy(macckb[:], macck[:]),
           reads=("macck",), writes=("macckb",))
        vt_view = vt_d.rearrange("(kt p) v -> p kt v", p=128)
        for h in range(AH):
            dmac("sp", khd[0:64, :], kr_d[h * 64:(h + 1) * 64, :],
                 reads=tuple(("kr", 2 * (h // 2) + hh, s2) for hh in range(2)
                             for s2 in range(NSL)) + ("khd",),
                 writes=("khd",))
            dmac("sp", qhd[0:64, :], qr_d[h * 64:(h + 1) * 64, :],
                 reads=tuple(("qr", 2 * (h // 2) + hh, s2) for hh in range(2)
                             for s2 in range(NSL)) + ("qhd",),
                 writes=("qhd",))
            for g4 in range(4):
                dmac("sp",
                     vslab[:].rearrange("p (kt c) -> p kt c", c=128)
                     [:, g4 * 8:(g4 + 1) * 8, 0:64],
                     vt_view[:, g4 * 8:(g4 + 1) * 8, h * 64:(h + 1) * 64],
                     reads=tuple(("vt", tti) for tti in range(NT)) +
                           (("vslab", "war"),),
                     writes=(("vslab", g4),))
            hm, hs_ = (esel0b, "esel0b") if h % 2 == 0 else (esel1b, "esel1b")
            st("pe", (lambda e, sel=hm, hh=h: e.matmul(
                pbank[5][:, 0:1], sel[:], maccqb[:, hh // 2:hh // 2 + 1],
                start=True, stop=True)),
               reads=(hs_, "maccqb"), writes=("pb5",))
            st("act", lambda e: e.activation(out=mqcol[:], in_=pbank[5][:, 0:1],
                                             func=AF.Ln),
               reads=("pb5",), writes=("mqcol",))
            st("act", lambda e: e.activation(out=mqcol[:], in_=mqcol[:],
                                             func=AF.Exp, scale=0.5),
               reads=("mqcol",), writes=("mqcol",))
            st("pe", (lambda e, sel=hm, hh=h: e.matmul(
                pbank[5][:, 0:1], sel[:], macckb[:, hh // 2:hh // 2 + 1],
                start=True, stop=True)),
               reads=(hs_, "macckb"), writes=("pb5",))
            st("act", lambda e: e.activation(out=mkcol[:], in_=pbank[5][:, 0:1],
                                             func=AF.Ln),
               reads=("pb5",), writes=("mkcol",))
            st("act", lambda e: e.activation(out=mkcol[:], in_=mkcol[:],
                                             func=AF.Exp, scale=0.5),
               reads=("mkcol",), writes=("mkcol",))
            st("dve", lambda e: e.tensor_mul(negm[:], mqcol[:], mkcol[:]),
               reads=("mqcol", "mkcol"), writes=("negm",))
            st("dve", lambda e: e.tensor_scalar(
                negm[:], negm[:], -0.125, None, ALU.mult),
               reads=("negm",), writes=("negm",))
            for qb in range(8):
                q0 = qb * SL
                NG = NT // 2

                def emit_st(g):
                    for half in range(2):
                        kt = 2 * g + half
                        bs = 2 * (g % 2) + half
                        st("pe", (lambda e, kk=kt, q_=q0, bb=bs: e.matmul(
                            pbank[bb][:, :], khd[:, kk * 128:(kk + 1) * 128],
                            qhd[:, q_:q_ + SL], start=True, stop=True)),
                           reads=("khd", "qhd"), writes=(f"pb{bs}",))

                emit_st(0)
                emit_st(1)
                for g in range(NG):
                    jp = g % 2
                    b0 = 2 * (g % 2)
                    st("act", (lambda e, j=jp: e.activation(
                        out=ptl[j][:, 0:1024], in_=pbw[j][:, 0:1024],
                        func=AF.Exp, scale=0.125, bias=negm[:])),
                       reads=(f"pb{b0}", f"pb{b0 + 1}", "negm"),
                       writes=((f"ptl{jp}",),))
                    if g + 2 < NG:
                        emit_st(g + 2)
                    ob = 4 + qb % 2
                    for half in range(2):
                        kt = 2 * g + half
                        st("pe", (lambda e, kk=kt, j=jp, hf=half, bb=ob: e.matmul(
                            pbank[bb][:, 0:SL],
                            vslab[:, kk * 128:(kk + 1) * 128],
                            ptl[j][:, hf * SL:hf * SL + SL],
                            start=(kk == 0), stop=(kk == 31))),
                           reads=((f"ptl{jp}",), ("vslab", kt // 8),
                                  ("vslab", "ones")),
                           writes=(f"pb{ob}",))
                ob = 4 + qb % 2
                st("dve", (lambda e, bb=ob, q_=qb: e.tensor_copy(
                    aot[0:64, 0:SL], pbank[bb][0:64, 0:SL])),
                   reads=(f"pb{ob}",), writes=("aot",))
                st("dve", (lambda e, bb=ob, q_=qb: e.tensor_copy(
                    zsb[0:1, q_ * SL:(q_ + 1) * SL], pbank[bb][64:65, 0:SL])),
                   reads=(f"pb{ob}",), writes=("zsb",))
                dmac("gp", ao_d[h * 64:(h + 1) * 64, q0:q0 + SL], aot[0:64, 0:SL],
                     reads=("aot",), writes=(("ao", h, qb),))
                if qb == 7:
                    dmac("gp", zs_d[h:h + 1, :], zsb[0:1, :],
                         reads=("zsb",), writes=(("zs", h),))

    def phase3(g):
        aofm, fwTs = g["aofm"], g["fwTs"]
        zrec = g["zrec"]
        expd = g["expd"]
        moeacc, moebf, moefm = g["moeacc"], g["moebf"], g["moefm"]
        baset, outt = g["baset"], g["outt"]
        gatesf, gtmp, gmsk = g["gatesf"], g["gtmp"], g["gmsk"]
        gcol, gcol2 = g["gcol"], g["gcol2"]
        wtok4, wfm4 = g["wtok4"], g["wfm4"]
        load_wchunks(fwTs, "fwTs", fwT_h, 8, D)
        dmac("sp", zrec[0:AH, :], zs_d,
             reads=tuple(("zs", h) for h in range(AH)), writes=("zrec",))
        st("dve", lambda e: e.reciprocal(zrec[0:AH, :], zrec[0:AH, :]),
           reads=("zrec",), writes=("zrec",))
        for ic in range(8):
            for half in range(2):
                st("pe", (lambda e, ii=ic, hf=half: e.matmul(
                    (pbank[2] if hf == 0 else pbank[3])[0:8, 0:512],
                    ebTs[:, ii * 8:(ii + 1) * 8],
                    fwTs[:, ii * D + hf * 512: ii * D + hf * 512 + 512],
                    start=(ii == 0), stop=(ii == 7))),
                   reads=(("fwTs", ic), "ebTs"),
                   writes=(("pb2" if half == 0 else "pb3"),))
        for half in range(2):
            st("dve", (lambda e, hf=half: e.tensor_copy(
                ebf[0:8, hf * 512:hf * 512 + 512],
                (pbank[2] if hf == 0 else pbank[3])[0:8, 0:512])),
               reads=(("pb2" if half == 0 else "pb3"),), writes=("ebf",))
        for tl in range(8):
            t0 = tl * SL
            for ic in range(8):
                dmac("sp", aofm[:, ic * SL:(ic + 1) * SL],
                     ao_d[ic * 128:(ic + 1) * 128, t0:t0 + SL],
                     reads=tuple(("ao", h, tl) for h in range(AH)) + ("aofm",),
                     writes=("aofm",))
            for ic in range(8):
                st("pe", (lambda e, ii=ic, tt=t0: e.matmul(
                    pbank[5][:, 0:SL], headmap[:, ii * 128:(ii + 1) * 128],
                    zrec[0:16, tt:tt + SL],
                    start=True, stop=True)),
                   reads=("zrec", "headmap"), writes=("pb5",))
                st("dve", (lambda e, ii=ic: e.tensor_mul(
                    aofm[:, ii * SL:(ii + 1) * SL],
                    aofm[:, ii * SL:(ii + 1) * SL], pbank[5][:, 0:SL])),
                   reads=("aofm", "pb5"), writes=("aofm",))
            for u in range(4):
                uo = u * 128
                for ic in range(8):
                    st("pe", (lambda e, ii=ic, oo=uo: e.matmul(
                        pbank[4][:, 0:8],
                        aofm[:, ii * SL + oo: ii * SL + oo + 128],
                        gwTs[:, ii * 8:(ii + 1) * 8],
                        start=(ii == 0), stop=(ii == 7))),
                       reads=("aofm", "gwTs"), writes=("pb4",))
                st("dve", lambda e: e.tensor_add(gatesf[:], pbank[4][:, 0:8],
                                                 gbb[:]),
                   reads=("pb4", "gbb"), writes=("gatesf",))
                st("dve", lambda e: e.tensor_reduce(
                    gcol[:], gatesf[:], AX.X, ALU.max, negate=True),
                   reads=("gatesf",), writes=("gcol",))
                st("act", lambda e: e.activation(
                    out=gtmp[:], in_=gatesf[:], func=AF.Exp, bias=gcol[:]),
                   reads=("gatesf", "gcol"), writes=("gtmp",))
                st("dve", lambda e: e.tensor_reduce(
                    gcol2[:], gtmp[:], AX.X, ALU.add),
                   reads=("gtmp",), writes=("gcol2",))
                st("dve", lambda e: e.reciprocal(gcol2[:], gcol2[:]),
                   reads=("gcol2",), writes=("gcol2",))
                st("dve", lambda e: e.tensor_scalar(
                    gatesf[:], gtmp[:], gcol2[:], None, ALU.mult),
                   reads=("gtmp", "gcol2"), writes=("gatesf",))
                st("dve", lambda e: e.tensor_reduce(
                    gcol[:], gatesf[:], AX.X, ALU.max),
                   reads=("gatesf",), writes=("gcol",))
                st("dve", lambda e: e.tensor_scalar(
                    gmsk[:], gatesf[:], gcol[:], None, ALU.is_ge),
                   reads=("gatesf", "gcol"), writes=("gmsk",))
                st("dve", lambda e: e.scalar_tensor_tensor(
                    gtmp[:], gmsk[:], -1e30, gatesf[:], ALU.mult, ALU.add),
                   reads=("gmsk", "gatesf"), writes=("gtmp",))
                st("dve", lambda e: e.tensor_reduce(
                    gcol2[:], gtmp[:], AX.X, ALU.max),
                   reads=("gtmp",), writes=("gcol2",))
                st("dve", lambda e: e.tensor_scalar(
                    gmsk[:], gatesf[:], gcol2[:], None, ALU.is_ge),
                   reads=("gatesf", "gcol2"), writes=("gmsk",))
                st("dve", (lambda e, uu=u: e.tensor_mul(
                    wtok4[:, uu * 8:(uu + 1) * 8], gmsk[:], gatesf[:])),
                   reads=("gmsk", "gatesf"), writes=("wtok4",))
                st("pe", (lambda e, uu=u: e.transpose(
                    pbank[4][0:8, 0:128], wtok4[:, uu * 8:(uu + 1) * 8],
                    identf[:])),
                   reads=("wtok4", "identf"), writes=("pb4",))
                st("dve", (lambda e, uu=u: e.tensor_copy(
                    wfm4[0:8, uu * 128:(uu + 1) * 128], pbank[4][0:8, 0:128])),
                   reads=("pb4",), writes=("wfm4",))
            for n in range(NE):
                j = n % 3
                for ic in range(8):
                    dmac("sp", expd[j][:, ic * D:(ic + 1) * D],
                         ewb_d[n, ic * 128:(ic + 1) * 128, :],
                         reads=(("ewb", n), f"expd{j}"),
                         writes=(f"expd{j}",))
                for u in range(4):
                    uo = u * 128
                    for half in range(2):
                        bank = (2 * u + half) % 4
                        for ic in range(8):
                            st("pe", (lambda e, ii=ic, jj=j, hf=half,
                                      bb=bank, oo=uo: e.matmul(
                                          pbank[bb][:, :],
                                          aofm[:, ii * SL + oo: ii * SL + oo + 128],
                                          expd[jj][:, ii * D + hf * 512:
                                                   ii * D + hf * 512 + 512],
                                          start=(ii == 0), stop=(ii == 7))),
                               reads=("aofm", f"expd{j}"),
                               writes=(f"pb{bank}",))
                        if n == 0:
                            st("dve", (lambda e, hf=half, uu=u, bb=bank:
                                       e.tensor_scalar(
                                           moeacc[uu][:, hf * 512:hf * 512 + 512],
                                           pbank[bb][:, :],
                                           wtok4[:, uu * 8:uu * 8 + 1],
                                           None, ALU.mult)),
                               reads=(f"pb{bank}", "wtok4"),
                               writes=((f"moeacc{u}",),))
                        else:
                            st("dve", (lambda e, nn=n, hf=half, uu=u, bb=bank:
                                       e.scalar_tensor_tensor(
                                           moeacc[uu][:, hf * 512:hf * 512 + 512],
                                           pbank[bb][:, :],
                                           wtok4[:, uu * 8 + nn:uu * 8 + nn + 1],
                                           moeacc[uu][:, hf * 512:hf * 512 + 512],
                                           ALU.mult, ALU.add)),
                               reads=(f"pb{bank}", "wtok4", (f"moeacc{u}",)),
                               writes=((f"moeacc{u}",),))
            for u in range(4):
                po = u % 2
                st("dve", (lambda e, uu=u, pp_=po: e.tensor_copy(
                    moebf[pp_][:], moeacc[uu][:])),
                   reads=((f"moeacc{u}",), (f"moebf{po}",)),
                   writes=((f"moebf{po}",),))
                for oc in range(8):
                    bank = oc % 2
                    st("pe", (lambda e, oo=oc, bb=bank, pp_=po: e.transpose(
                        pbtr[bb][:, 0:128], moebf[pp_][:, oo * 128:(oo + 1) * 128],
                        identb[:])),
                       reads=((f"moebf{po}",), "identb"), writes=(f"pt{bank}",))
                    st("dve", (lambda e, oo=oc, bb=bank, pp_=po: e.tensor_copy(
                        moefm[pp_][:, oo * 128:(oo + 1) * 128],
                        pbtr[bb][:, 0:128])),
                       reads=(f"pt{bank}",), writes=((f"moefm{po}",),))
                for half in range(2):
                    bank = 4 + half
                    for oc in range(8):
                        st("pe", (lambda e, oo=oc, hf=half, bb=bank, pp_=po:
                                  e.matmul(
                                      pbank[bb][:, :],
                                      moefm[pp_][:, oo * 128:(oo + 1) * 128],
                                      fwTs[:, oo * D + hf * 512:
                                           oo * D + hf * 512 + 512],
                                      start=(oo == 0), stop=False)),
                           reads=((f"moefm{po}",), ("fwTs", oc)),
                           writes=(f"pb{bank}",))
                    st("pe", (lambda e, hf=half, bb=bank, uu=u: e.matmul(
                        pbank[bb][:, :], wfm4[0:8, uu * 128:(uu + 1) * 128],
                        ebf[0:8, hf * 512:hf * 512 + 512],
                        start=False, stop=True)),
                       reads=("wfm4", "ebf"), writes=(f"pb{bank}",))
                ti = tl * 4 + u
                dmac("sp", baset[:], base_d[ti * 128:(ti + 1) * 128, :],
                     reads=(("based", ti), "baset"),
                     writes=("baset",))
                ot = outt[po]
                for half in range(2):
                    st("dve", (lambda e, hf=half, o_=ot: e.tensor_add(
                        o_[:, hf * 512:hf * 512 + 512], pbank[4 + hf][:, :],
                        fbb[:, hf * 512:hf * 512 + 512])),
                       reads=(f"pb{4 + half}", "fbb", (f"outt{po}",)),
                       writes=((f"outt{po}",),))
                    st("dve", (lambda e, hf=half, o_=ot: e.tensor_add(
                        o_[:, hf * 512:hf * 512 + 512],
                        o_[:, hf * 512:hf * 512 + 512],
                        baset[:, hf * 512:hf * 512 + 512])),
                       reads=((f"outt{po}",), "baset"),
                       writes=((f"outt{po}",),))
                dmac("gp", out_h[ti * 128:(ti + 1) * 128, :], ot[:],
                     reads=((f"outt{po}",),),
                     writes=((f"outt{po}",), ("outw", ti)))

    loads_init()
    for n_ in range(NE):
        dmac("gp", ewb_d[n_, :, :], ewT_h[n_, :, :],
             reads=(), writes=(("ewb", n_),))
    if "p0" in phases:
        phase0()
    P.reset_region("RB")
    P.reset_region("RC")
    g1a = {}
    g1a["winsb"] = palloc("winsb", (128, 8 * 3 * D), BF16, OFF["winsb"], "RB")
    g1a["wogsb"] = palloc("wogsb", (128, 8 * D), BF16, OFF["wogsb"], "RC")
    g1a["praw"] = [palloc(f"praw{i}", (128, SL + 4), BF16, OFF[f"praw{i}"], "RC")
                   for i in range(4)]
    convt = palloc("convt", (128, SL), BF16, OFF["convt"], "RC")
    convt2 = palloc("convt2", (128, SL), BF16, OFF["convt"] + 1, "RC")
    g1a["ogt"] = [palloc(f"ogt{i}", (128, SL), F32, OFF[f"ogt{i}"], "RC")
                  for i in range(2)]
    if "p1a" in phases:
        phase1a(g1a)
    P.reset_region("RB")
    P.reset_region("RC")
    g1b = {}
    g1b["qnsl"] = palloc("qnsl", (128, 8 * SL), BF16, OFF["qnsl"], "RB")
    g1b["knsl"] = palloc("knsl", (128, 8 * SL), BF16, OFF["knsl"], "RB")
    g1b["vnsl"] = palloc("vnsl", (128, 8 * SL), BF16, OFF["vnsl"], "RB")
    g1b["woutsb"] = palloc("woutsb", (128, 8 * D), BF16, OFF["woutsb"], "RB")
    g1b["ogsl"] = palloc("ogsl", (128, 8 * SL), F32, OFF["ogsl"], "RC")
    g1b["sqt"] = sqt_p
    g1b["rrow"] = rrow_p
    cht4 = chunk_tiles4()
    readn4_t, readfm4_t, dnbase_t = rd_tiles4()
    if "p1b" in phases:
        phase1b(g1b)
    P.reset_region("RB")
    P.reset_region("RC")
    g2p = {}
    g2p["qwTs"] = palloc("qwTs", (128, 8 * D), BF16, OFF["qwTs"], "RB")
    g2p["kwTs"] = palloc("kwTs", (128, 8 * D), BF16, OFF["kwTs"], "RB")
    g2p["vwTs"] = palloc("vwTs", (128, 8 * D), BF16, OFF["vwTs"], "RB")
    g2p["copt"] = palloc("copt", (128, S), F32, OFF["copt"], "RC")
    g2p["sipt"] = palloc("sipt", (128, S), F32, OFF["sipt"], "RC")
    g2p["qpt"] = [palloc(f"qpt{i}", (128, SL), BF16, OFF[f"qpt{i}"], "RC")
                  for i in range(2)]
    g2p["ropet1"] = palloc("ropet1", (128, SL), F32, OFF["ropet1"], "RC")
    g2p["ropet2"] = palloc("ropet2", (128, SL), F32, OFF["ropet2"], "RC")
    g2p["ropeo"] = palloc("ropeo", (128, SL), BF16, OFF["ropeo"], "RC")
    g2p["vtokt"] = palloc("vtokt", (128, D), BF16, OFF["vtokt"], "RC")
    g2p["sqa"] = palloc("sqa", (128, SL), BF16, OFF["sqa"], "RC")
    if "p2p" in phases:
        phase2proj(g2p)
    P.reset_region("RB")
    P.reset_region("RC")
    g2a = {}
    g2a["khd"] = palloc("khd", (128, S), BF16, OFF["khd"], "RB")
    g2a["qhd"] = palloc("qhd", (128, S), BF16, OFF["qhd"], "RB")
    g2a["vslab"] = palloc("vslab", (128, 32 * 128), BF16, OFF["vslab"], "RB")
    g2a["ptl"] = [palloc(f"ptl{i}", (128, 1024), BF16, OFF[f"ptl{i}"], "RB")
                  for i in range(2)]
    g2a["zsb"] = palloc("zsb", (1, S), F32, OFF["zbs"], "RB")
    g2a["aot"] = palloc("aot", (64, SL), BF16, OFF["aot"], "RB")
    g2a["negmh"] = palloc("negmh", (128, 1), F32, OFF["negmh"], "RB")
    g2a["mqcol"] = palloc("mqcol", (128, 1), F32, OFF["mqcol"], "RB")
    g2a["mkcol"] = palloc("mkcol", (128, 1), F32, OFF["mkcol"], "RB")
    if "p2a" in phases:
        phase2attn(g2a)
    P.reset_region("RA")
    P.reset_region("RB")
    P.reset_region("RC")
    g3 = {}
    g3["expd"] = [palloc("expd0", (128, 8 * D), BF16, OFF["expd0"], "RA"),
                  palloc("expd1", (128, 8 * D), BF16, OFF["expd1"], "RA"),
                  palloc("expd2", (128, 8 * D), BF16, 124, "RC")]
    g3["moeacc"] = [palloc(f"moeacc{i}", (128, D), F32, OFF["moeacc"] + 4 * i,
                           "RA") for i in range(4)]
    g3["moebf"] = [palloc(f"moebf{i}", (128, D), BF16, OFF["moebf"] + 2 * i,
                          "RA") for i in range(2)]
    g3["moefm"] = [palloc(f"moefm{i}", (128, D), BF16, OFF["moefm"] + 2 * i,
                          "RA") for i in range(2)]
    g3["baset"] = palloc("baset", (128, D), F32, OFF["baset"], "RA")
    g3["outt"] = [palloc("outt0", (128, D), F32, OFF["outt0"], "RA"),
                  palloc("outt1", (128, D), F32, OFF["outt1"], "RC")]
    g3["gatesf"] = palloc("gatesf", (128, 8), F32, OFF["gatesf"], "RC")
    g3["gtmp"] = palloc("gtmp", (128, 8), F32, OFF["gtmp"], "RC")
    g3["gmsk"] = palloc("gmsk", (128, 8), F32, OFF["gmsk"], "RC")
    g3["gcol"] = palloc("gcol", (128, 1), F32, OFF["gcol"], "RC")
    g3["gcol2"] = palloc("gcol2", (128, 1), F32, OFF["gcol2"], "RC")
    g3["wtok4"] = palloc("wtok4", (128, 32), F32, OFF["wtok4"], "RC")
    g3["wfm4"] = palloc("wfm4", (8, 512), BF16, OFF["wfm4"], "RC")
    g3["aofm"] = palloc("aofm", (128, 8 * SL), BF16, OFF["aofm"], "RB")
    g3["zrec"] = palloc("zrec", (16, S), F32, OFF["zrec"], "RB")
    g3["fwTs"] = palloc("fwTs", (128, 8 * D), BF16, OFF["fwTs"], "RB")
    if "p3" in phases:
        phase3(g3)

    if "p3" not in phases:
        for i in range(NT):
            p = i % 2
            dmac("gp", out_h[i * 128:(i + 1) * 128, :], xn32t[p][:],
                 reads=(f"xn32t{p}",), writes=(("outw", i),))

    for name, h in dbg_outs.items():
        src = {"xn32": xn32_d, "qc": qc_d, "kc": kc_d, "vc": vc_d,
               "oglog": oglog_d, "base": base_d, "qr": qr_d, "kr": kr_d,
               "vt": vt_d, "ao": ao_d, "zs": zs_d, "readn": readn_d}.get(name)
        if src is not None:
            reads = {"xn32": tuple(("xn32d", i) for i in range(NT)),
                     "base": tuple(("based", i) for i in range(NT)),
                     "qc": tuple(("qc", oc, s2) for oc in range(8) for s2 in range(NSL)),
                     "kc": tuple(("kc", oc, s2) for oc in range(8) for s2 in range(NSL)),
                     "vc": tuple(("vc", oc, s2) for oc in range(8) for s2 in range(NSL)),
                     "oglog": tuple(("oglog", oc, s2) for oc in range(8) for s2 in range(NSL)),
                     "qr": tuple(("qr", oc, s2) for oc in range(8) for s2 in range(NSL)),
                     "kr": tuple(("kr", oc, s2) for oc in range(8) for s2 in range(NSL)),
                     "vt": tuple(("vt", i) for i in range(NT)),
                     "ao": tuple(("ao", hh, qb) for hh in range(AH) for qb in range(8)),
                     "zs": tuple(("zs", hh) for hh in range(AH)),
                     "readn": tuple(("readnd", i) for i in range(NT)),
                     }[name]
            dmac("gp", h, src, reads=reads, writes=((name, "dbgw"),))
        elif name == "absb":
            dmac("gp", h, absb[:], reads=tuple(("absb", i) for i in range(NT)),
                 writes=(("absb", "dbgw"),))

    counters = P.finalize()
    per_engine = {e: [] for e in ENGINES}
    for s_ in P.steps:
        per_engine[s_["engine"]].append(s_)

    with (
        nc.semaphore("sem_pe") as sem_pe,
        nc.semaphore("sem_act") as sem_act,
        nc.semaphore("sem_dve") as sem_dve,
        nc.semaphore("sem_gp") as sem_gp,
        nc.semaphore("sem_sp") as sem_sp,
        nc.semaphore("sem_gp_dma") as sem_gp_dma,
        nc.semaphore("sem_sp_dma") as sem_sp_dma,
        nc.Block() as block,
    ):
        sems = {"pe": sem_pe, "act": sem_act, "dve": sem_dve, "gp": sem_gp,
                "sp": sem_sp, "gp_dma": sem_gp_dma, "sp_dma": sem_sp_dma}

        def run_engine(eng_obj, steps):
            for s_ in steps:
                for key, val in s_["waits"]:
                    eng_obj.wait_ge(sems[key], val)
                inst = s_["emit"](eng_obj)
                if s_["need_sem"]:
                    inst.then_inc(sems[s_["semkey"]], 16 if s_["dma"] else 1)

        @block.tensor
        def _(t):
            run_engine(t, per_engine["pe"])

        @block.scalar
        def _(a):
            run_engine(a, per_engine["act"])

        @block.vector
        def _(v):
            run_engine(v, per_engine["dve"])

        @block.gpsimd
        def _(gg):
            run_engine(gg, per_engine["gp"])
            for key in ("gp_dma", "sp_dma"):
                if counters.get(key):
                    gg.wait_ge(sems[key], counters[key])

        @block.sync
        def _(sy):
            run_engine(sy, per_engine["sp"])

    return nc, list(dbg_outs.keys())


def _prep_inputs(ins):
    f32 = np.float32

    def colchunk(v):
        return np.ascontiguousarray(np.asarray(v, f32).reshape(8, 128).T)

    base = {
        "win": np.ascontiguousarray(np.asarray(ins["L0_attn_W_in"], f32)),
        "wog": np.ascontiguousarray(np.asarray(ins["L0_attn_W_ctrl"], f32)[:, 8:]),
        "wab": np.ascontiguousarray(
            np.asarray(ins["L0_attn_W_ctrl"], f32)[:, :8]
            .reshape(8, 128, 8).transpose(1, 0, 2).reshape(128, 64)),
        "bog": colchunk(np.asarray(ins["L0_attn_b_ctrl"])[8:]),
        "bab": np.asarray(ins["L0_attn_b_ctrl"], f32)[:8].reshape(1, 8),
        "wconv": np.ascontiguousarray(np.stack([
            np.asarray(ins["L0_attn_w_q"], f32).reshape(8, 128, 4).transpose(1, 0, 2),
            np.asarray(ins["L0_attn_w_k"], f32).reshape(8, 128, 4).transpose(1, 0, 2),
            np.asarray(ins["L0_attn_w_v"], f32).reshape(8, 128, 4).transpose(1, 0, 2),
        ], axis=1).reshape(128, 96)),
        "wout": np.ascontiguousarray(np.asarray(ins["L0_attn_W_out"], f32)),
        "bout": colchunk(ins["L0_attn_b_out"]),
        "ramp": colchunk(ins["L0_attn_ramp_logit"]),
        "qwT": np.ascontiguousarray(np.asarray(ins["L0_ffn_q_w"], f32).T),
        "kwT": np.ascontiguousarray(np.asarray(ins["L0_ffn_k_w"], f32).T),
        "vwT": np.ascontiguousarray(np.asarray(ins["L0_ffn_v_w"], f32).T),
        "qb": colchunk(ins["L0_ffn_q_b"]),
        "kb": colchunk(ins["L0_ffn_k_b"]),
        "vbrow": np.asarray(ins["L0_ffn_v_b"], f32).reshape(1, D),
        "gwT": np.ascontiguousarray(
            np.asarray(ins["L0_ffn_gate_w"], f32).T
            .reshape(8, 128, 8).transpose(1, 0, 2).reshape(128, 64)),
        "gbrow": np.asarray(ins["L0_ffn_gate_b"], f32).reshape(1, 8),
        "ewT": np.ascontiguousarray(
            np.asarray(ins["L0_ffn_expert_w"], f32).transpose(0, 2, 1)),
        "ebT": np.ascontiguousarray(
            np.asarray(ins["L0_ffn_expert_b"], f32).T
            .reshape(8, 128, 8).transpose(1, 0, 2).reshape(128, 64)),
        "fwT": np.ascontiguousarray(np.asarray(ins["L0_ffn_ffn_w"], f32).T),
        "fbrow": np.asarray(ins["L0_ffn_ffn_b"], f32).reshape(1, D),
    }
    for k, v in CONSTS.items():
        base[k] = v
    x = np.asarray(ins["L0_attn_x"], np.float32)
    return [dict(base, x=np.ascontiguousarray(x[b])) for b in range(8)]


def kernel(**inputs):
    in_maps = _prep_inputs(inputs)
    nc, _dbg = build_program()
    res = run_bass_kernel_spmd(nc, in_maps, core_ids=list(range(8)))
    return np.stack([r["out"] for r in res.results], axis=0).astype(np.float32)


# version 15
# speedup vs baseline: 1.2711x; 1.0085x over previous
import numpy as np
import ml_dtypes
import concourse.bass as bass
import concourse.mybir as mybir
from concourse.bass_utils import run_bass_kernel_spmd

F32 = mybir.dt.float32
BF16 = mybir.dt.bfloat16
AF = mybir.ActivationFunctionType
ALU = mybir.AluOpType
AX = mybir.AxisListType
NPBF = ml_dtypes.bfloat16

S = 4096
D = 1024
NSL = 8
SL = 512
NT = 32
C = 128
HD = 256
DNH = 4
AH = 16
ADH = 64
NE = 8

ENGINES = ("pe", "act", "dve", "gp", "sp")


class Plan:

    def __init__(self):
        self.steps = []
        self.lastw = {}
        self.readers = {}
        self.region_of = {}
        self.reg_pending = {}
        self.reg_inited = {}

    def set_region(self, res, region):
        self.region_of[res] = region

    @staticmethod
    def _base(res):
        return res if isinstance(res, str) else res[0]

    def reset_region(self, region):
        pend = set()
        for res in list(self.lastw.keys()) + list(self.readers.keys()):
            if self.region_of.get(self._base(res)) != region:
                continue
            if res in self.lastw:
                pend.add(self.lastw.pop(res))
            pend |= self.readers.pop(res, set())
        self.reg_pending[region] = pend
        self.reg_inited[region] = set()

    def step(self, engine, emit, reads=(), writes=(), dma=False):
        idx = len(self.steps)
        deps = set()
        for r in reads:
            if r in self.lastw:
                deps.add(self.lastw[r])
        for w in writes:
            if w in self.lastw:
                deps.add(self.lastw[w])
            deps |= self.readers.get(w, set())
            reg = self.region_of.get(self._base(w))
            if reg is not None and self.reg_pending.get(reg):
                if w not in self.reg_inited[reg]:
                    deps |= self.reg_pending[reg]
                    self.reg_inited[reg].add(w)
        for r in reads:
            self.readers.setdefault(r, set()).add(idx)
        for w in writes:
            self.lastw[w] = idx
            self.readers[w] = {idx}
        deps.discard(idx)
        self.steps.append(dict(engine=engine, emit=emit, deps=deps, dma=dma,
                               need_sem=False))
        return idx

    SELF_SYNC = ("dve", "act", "gp")

    def finalize(self):
        steps = self.steps
        for s in steps:
            for d in s["deps"]:
                p = steps[d]
                if (p["engine"] != s["engine"] or p["dma"]
                        or p["engine"] in self.SELF_SYNC):
                    p["need_sem"] = True
        for s in steps:
            if s["dma"]:
                s["need_sem"] = True
        counters = {}
        for s in steps:
            if s["need_sem"]:
                key = s["engine"] + ("_dma" if s["dma"] else "")
                counters[key] = counters.get(key, 0) + (16 if s["dma"] else 1)
                s["semkey"] = key
                s["semval"] = counters[key]
        wm = {}
        for s in steps:
            need = {}
            for d in s["deps"]:
                p = steps[d]
                if (p["engine"] == s["engine"] and not p["dma"]
                        and p["engine"] not in self.SELF_SYNC):
                    continue
                need[p["semkey"]] = max(need.get(p["semkey"], 0), p["semval"])
            waits = []
            for key, val in sorted(need.items()):
                if wm.get((s["engine"], key), 0) >= val:
                    continue
                wm[(s["engine"], key)] = val
                waits.append((key, val))
            s["waits"] = waits
        return counters


def host_consts():
    c = {}
    c["identb"] = np.eye(128, dtype=NPBF)
    c["identf"] = np.eye(128, dtype=np.float32)
    c["onesf"] = np.ones((1, 128), np.float32)
    c["onescolb"] = np.ones((128, 1), NPBF)
    c["uincf"] = np.triu(np.ones((128, 128), np.float32))
    maskadd = np.where(np.arange(128)[:, None] <= np.arange(128)[None, :],
                       0.0, -1e30).astype(np.float32)
    c["maskaddTf"] = np.ascontiguousarray(maskadd.T)
    c["offdiagf"] = (1.0 - np.eye(128)).astype(np.float32)
    c["offdiag4f"] = np.tile(-(1.0 - np.eye(128)).astype(np.float32), (1, 4))
    e127 = np.zeros((128, 128), np.float32)
    e127[127, :] = 1.0
    c["e127f"] = e127
    es0 = np.zeros((2, 128), np.float32); es0[0, :] = 1.0
    es1 = np.zeros((2, 128), np.float32); es1[1, :] = 1.0
    c["esel0f"], c["esel1f"] = es0, es1
    c["esel0b"], c["esel1b"] = es0.astype(NPBF), es1.astype(NPBF)
    hs = np.zeros((128, 2), NPBF); hs[:64, 0] = 1; hs[64:, 1] = 1
    c["headsel"] = hs
    bsel = np.zeros((2, 128), np.float32); bsel[0, :64] = 1; bsel[1, 64:] = 1
    c["blockself"] = bsel
    hm = np.zeros((16, 8 * 128), np.float32)
    for ic in range(8):
        hm[2 * ic, ic * 128: ic * 128 + 64] = 1
        hm[2 * ic + 1, ic * 128 + 64: ic * 128 + 128] = 1
    c["headmapf"] = hm
    psw = np.zeros((128, 128), NPBF)
    for a in range(2):
        for dd in range(32):
            psw[a * 64 + dd + 32, a * 64 + dd] = -1.0
            psw[a * 64 + dd, a * 64 + dd + 32] = 1.0
    c["pswapT"] = psw
    inv = 1.0 / 10000.0 ** (np.arange(0, ADH, 2, dtype=np.float64) / ADH)
    ang = np.outer(inv, np.arange(S, dtype=np.float64))
    ang2 = np.concatenate([ang, ang], axis=0)
    cos = np.cos(ang2).astype(np.float32)
    sin = np.sin(ang2).astype(np.float32)
    c["copt"] = np.ascontiguousarray(np.concatenate([cos, cos], axis=0))
    c["sipt"] = np.ascontiguousarray(np.concatenate([sin, sin], axis=0))
    return c


CONSTS = host_consts()


def build_program(debug=(), phases=("p0","p1a","p1b","p2p","p2a","p3")):
    nc = bass.Bass("TRN2", target_bir_lowering=False, debug=False)
    P = Plan()

    def inp(name, shape, dt=F32):
        return nc.dram_tensor(name, list(shape), dt, kind="ExternalInput").ap()

    x_h = inp("x", (S, D))
    win_h = inp("win", (D, 3 * D))
    wog_h = inp("wog", (D, D))
    wab_h = inp("wab", (128, 64))
    bog_h = inp("bog", (128, 8))
    bab_h = inp("bab", (1, 8))
    wconv_h = inp("wconv", (128, 96))
    wout_h = inp("wout", (D, D))
    bout_h = inp("bout", (128, 8))
    ramp_h = inp("ramp", (128, 8))
    qwT_h = inp("qwT", (D, D))
    kwT_h = inp("kwT", (D, D))
    vwT_h = inp("vwT", (D, D))
    qb_h = inp("qb", (128, 8))
    kb_h = inp("kb", (128, 8))
    vb_h = inp("vbrow", (1, D))
    gwT_h = inp("gwT", (128, 64))
    gb_h = inp("gbrow", (1, 8))
    ewT_h = inp("ewT", (NE, D, D))
    ebT_h = inp("ebT", (128, 64))
    fwT_h = inp("fwT", (D, D))
    fb_h = inp("fbrow", (1, D))
    con_h = {k: inp(k, CONSTS[k].shape,
                    BF16 if CONSTS[k].dtype == NPBF else F32)
             for k in CONSTS}
    out_h = nc.dram_tensor("out", [S, D], F32, kind="ExternalOutput").ap()

    def scr(name, shape, dt):
        return nc.dram_tensor(name, list(shape), dt, kind="Internal").ap()

    xn32_d = scr("xn32", (S, D), F32)
    base_d = scr("base", (S, D), F32)
    oglog_d = scr("oglog", (D, S), F32)
    qc_d = scr("qc", (D, S), BF16)
    kc_d = scr("kc", (D, S), BF16)
    vc_d = scr("vc", (D, S), BF16)
    qr_d = scr("qr", (D, S), BF16)
    kr_d = scr("kr", (D, S), BF16)
    vt_d = scr("vt", (S, D), BF16)
    ao_d = scr("ao", (D, S), BF16)
    zs_d = scr("zs", (AH, S), F32)
    readn_d = scr("readn_d", (S, D), BF16)
    ewb_d = scr("ewb", (NE, D, D), BF16)

    dbg_outs = {}
    def dbgout(name, shape, dt):
        if name in debug:
            dbg_outs[name] = nc.dram_tensor("dbg_" + name, list(shape), dt,
                                            kind="ExternalOutput").ap()
    dbgout("xn32", (S, D), F32)
    dbgout("qc", (D, S), F32)
    dbgout("kc", (D, S), F32)
    dbgout("vc", (D, S), F32)
    dbgout("oglog", (D, S), F32)
    dbgout("absb", (128, NT * 8), F32)
    dbgout("base", (S, D), F32)
    dbgout("qr", (D, S), F32)
    dbgout("kr", (D, S), F32)
    dbgout("vt", (S, D), F32)
    dbgout("ao", (D, S), F32)
    dbgout("zs", (AH, S), F32)
    dbgout("readn", (S, D), F32)

    sb = {}

    def sbuf(name, shape, dt):
        t = nc.alloc_sbuf_tensor("s_" + name, list(shape), dt)
        sb[name] = t
        return t

    identb = sbuf("identb", (128, 128), BF16)
    identf = sbuf("identf", (128, 128), F32)
    onesf = sbuf("onesf", (1, 128), F32)
    onescolb = sbuf("onescolb", (128, 1), BF16)
    uincf = sbuf("uincf", (128, 128), F32)
    maskaddT = sbuf("maskaddT", (128, 128), F32)
    offdiag = sbuf("offdiag", (128, 128), F32)
    offdiag4 = sbuf("offdiag4", (128, 512), F32)
    e127 = sbuf("e127", (128, 128), F32)
    esel0 = sbuf("esel0", (2, 128), F32)
    esel1 = sbuf("esel1", (2, 128), F32)
    esel0b = sbuf("esel0b", (2, 128), BF16)
    esel1b = sbuf("esel1b", (2, 128), BF16)
    maccqb = sbuf("maccqb", (2, 8), BF16)
    macckb = sbuf("macckb", (2, 8), BF16)
    headsel = sbuf("headsel", (128, 2), BF16)
    blocksel = sbuf("blocksel", (2, 128), F32)
    headmap = sbuf("headmap", (16, 8 * 128), F32)
    pswapT = sbuf("pswapT", (128, 128), BF16)
    bogc = sbuf("bogc", (128, 8), F32)
    bogc2 = sbuf("bogc2", (128, 8), F32)
    babr = sbuf("babr", (1, 8), F32)
    boutc = sbuf("boutc", (128, 8), F32)
    rampc = sbuf("rampc", (128, 8), F32)
    rampt = sbuf("rampt", (128, 8), F32)
    qbc = sbuf("qbc", (128, 8), F32)
    kbc = sbuf("kbc", (128, 8), F32)
    vbrow = sbuf("vbrow", (1, D), F32)
    vbb = sbuf("vbb", (128, D), F32)
    gbrow = sbuf("gbrow", (1, 8), F32)
    gbb = sbuf("gbb", (128, 8), F32)
    babb = sbuf("babb", (128, 8), F32)
    fbrow = sbuf("fbrow", (1, D), F32)
    fbb = sbuf("fbb", (128, D), F32)
    gwTs = sbuf("gwTs", (128, 64), BF16)
    ebTs = sbuf("ebTs", (128, 64), BF16)
    wabsb = sbuf("wabsb", (128, 64), BF16)
    wconv = sbuf("wconv", (128, 96), F32)
    ebf = sbuf("ebf", (8, D), BF16)
    tails = sbuf("tails", (128, 24 * 3), BF16)
    st32 = sbuf("st32", (128, DNH * 2 * HD), F32)
    stbf = sbuf("stbf", (128, DNH * 2 * HD), BF16)
    absb = sbuf("absb", (128, NT * 8), F32)
    lnst = sbuf("lnst", (128, 8), F32)
    maccq = sbuf("maccq", (2, 8), F32)
    sqt_p = sbuf("sqt_p", (128, SL), BF16)
    rrow_p = sbuf("rrow_p", (1, SL), F32)
    cpe5 = sbuf("cpe5", (128, 1), F32)
    cpe6 = sbuf("cpe6", (128, 1), F32)
    nc.const_aps.register(F32, 1e-5, cpe5[:])
    nc.const_aps.register(F32, 1e-6, cpe6[:])
    macck = sbuf("macck", (2, 8), F32)
    mtmp = sbuf("mtmp", (2, 1), F32)

    pool_base = (nc.sbuf_base + 1023) // 1024 * 1024
    pool_avail = nc.sbuf_bytes_remaining - (pool_base - nc.sbuf_base)
    pool = {}

    avail_pp = pool_avail if pool_avail <= 256 * 1024 else pool_avail // 128

    def palloc(name, shape, dt, off_kb, region):
        nbytes = int(np.prod(shape[1:])) * (2 if dt == BF16 else 4)
        off = pool_base + off_kb * 1024
        assert off_kb * 1024 + nbytes <= avail_pp - 1024, \
            f"pool overflow {name}: off_kb={off_kb} nbytes={nbytes} avail={avail_pp}"
        t = nc.alloc_sbuf_tensor_at("s_" + name, list(shape), dt, offset=off)
        pool[name] = t
        P.set_region(name, region)
        return t

    xnfm = palloc("xnfm", (128, 8 * S), BF16, 0, "RA")
    xt = [palloc(f"xt{i}", (128, D), F32, 64 + 4 * i, "RB") for i in range(2)]
    xbf = [palloc(f"xbf{i}", (128, D), BF16, 72 + 2 * i, "RB") for i in range(2)]
    x2t = palloc("x2t", (128, D), F32, 76, "RB")
    xn32t = [palloc(f"xn32t{i}", (128, D), F32, 80 + 4 * i, "RB") for i in range(2)]
    OFF = dict(
        winsb=64,
        wogsb=112,
        praw0=128, praw1=130, praw2=132, praw3=134,
        convt=136, ogt0=138, ogt1=140,
        qnsl=64, knsl=72, vnsl=80,
        woutsb=88,
        sqt=157, rrow=158,
        ogsl=112,
        qwTs=64, kwTs=80, vwTs=96,
        copt=112, sipt=128,
        qpt0=144, qpt1=145,
        ropet1=146, ropet2=148,
        ropeo=150, vtokt=152, sqa=154,
        khd=64, qhd=72, vslab=80,
        ptl0=88, ptl1=90, aot=92, zbs=96,
        negmh=93, mqcol=94, mkcol=95,
        expd0=0, expd1=16,
        moeacc=32,
        moebf=48, moefm=52, baset=56,
        outt0=60, outt1=120,
        zrec=88,
        gatesf=112, gtmp=113, gmsk=114,
        gcol=115, gcol2=116, wtok4=117, wfm4=118,
        aofm=64, fwTs=72,
    )

    def chunk_tiles4():
        ofs = 112 * 1024
        d = {}

        def ca(name, shape, dt):
            nonlocal ofs
            nbytes = int(np.prod(shape[1:])) * (2 if dt == BF16 else 4)
            t = nc.alloc_sbuf_tensor_at("s_" + name, list(shape), dt,
                                        offset=pool_base + ofs)
            ofs = (ofs + nbytes + 31) // 32 * 32
            pool[name] = t
            P.set_region(name, "RC")
            assert ofs <= 164 * 1024, f"chunk4 overflow {name} at {ofs}"
            return t

        d["gsb"] = ca("gsb", (128, 4), F32)
        d["biasc"] = ca("biasc", (128, 4), F32)
        d["tiny4"] = ca("tiny4", (128, 12), F32)
        d["growall"] = ca("growall", (1, 512), F32)
        d["dstr4"] = ca("dstr4", (128, 512), F32)
        d["mc4"] = ca("mc4", (128, 1024), F32)
        d["readf4"] = ca("readf4", (128, 1024), F32)
        d["u04"] = ca("u04", (128, 1024), BF16)
        d["rr4"] = ca("rr4", (128, 1024), BF16)
        d["pp4a"] = ca("pp4a", (128, 512), BF16)
        d["pp4b"] = ca("pp4b", (128, 512), BF16)
        d["bp4a"] = ca("bp4a", (128, 512), BF16)
        d["bp4b"] = ca("bp4b", (128, 512), BF16)
        for par in range(2):
            d[f"ctc{par}"] = ca(f"ctc{par}", (128, 4), F32)
            d[f"clast4{par}"] = ca(f"clast4{par}", (128, 4), F32)
            d[f"dincl4{par}"] = ca(f"dincl4{par}", (128, 512), F32)
            d[f"at4{par}"] = ca(f"at4{par}", (128, 512), BF16)
            d[f"wt4{par}"] = ca(f"wt4{par}", (128, 512), BF16)
            d[f"an4{par}"] = ca(f"an4{par}", (128, 512), BF16)
            d[f"vt4{par}"] = ca(f"vt4{par}", (128, 1024), BF16)
            d[f"ktok4{par}"] = ca(f"ktok4{par}", (128, 1024), BF16)
            d[f"ogch{par}"] = ca(f"ogch{par}", (128, 1024), F32)
        return d

    def rd_tiles4():
        t1 = nc.alloc_sbuf_tensor_at("s_readn4", [128, D], BF16,
                                     offset=pool_base + 104 * 1024)
        t2 = nc.alloc_sbuf_tensor_at("s_readfm4", [128, D], BF16,
                                     offset=pool_base + 106 * 1024)
        t3 = nc.alloc_sbuf_tensor_at("s_dnbase", [128, D], F32,
                                     offset=pool_base + 108 * 1024)
        for t, n in ((t1, "readn4"), (t2, "readfm4"), (t3, "dnbase")):
            pool[n] = t
            P.set_region(n, "RB")
        return t1, t2, t3

    pbw = [nc.alloc_psum_tensor(f"pw{i}", [128, 1024], F32) for i in range(2)]

    class BankView:
        def __init__(self, t, coff):
            self.t, self.coff = t, coff

        def __getitem__(self, idx):
            r, c = idx
            cs = c.start or 0
            ce = c.stop if c.stop is not None else 512
            return self.t[r, self.coff + cs: self.coff + ce]

    pbank = [BankView(pbw[0], 0), BankView(pbw[0], 512),
             BankView(pbw[1], 0), BankView(pbw[1], 512),
             nc.alloc_psum_tensor("pb4", [128, 512], F32),
             nc.alloc_psum_tensor("pb5", [128, 512], F32)]
    pbtr = [nc.alloc_psum_tensor(f"pt{i}", [128, 1024], BF16) for i in range(2)]

    def st(engine, emit, reads=(), writes=(), dma=False):
        return P.step(engine, emit, reads=reads, writes=writes, dma=dma)

    def dmac(engine, dst, src, reads, writes):
        return P.step(engine, (lambda e, d_=dst, s_=src: e.dma_start(d_, s_)),
                      reads=reads, writes=writes, dma=True)

    def loads_init():
        for t, h, key in ((identb, con_h["identb"], "identb"),
                          (identf, con_h["identf"], "identf"),
                          (onesf, con_h["onesf"], "onesf"),
                          (onescolb, con_h["onescolb"], "onescolb"),
                          (uincf, con_h["uincf"], "uincf"),
                          (maskaddT, con_h["maskaddTf"], "maskaddT"),
                          (offdiag, con_h["offdiagf"], "offdiag"),
                          (offdiag4, con_h["offdiag4f"], "offdiag4"),
                          (e127, con_h["e127f"], "e127"),
                          (esel0, con_h["esel0f"], "esel0"),
                          (esel1, con_h["esel1f"], "esel1"),
                          (esel0b, con_h["esel0b"], "esel0b"),
                          (esel1b, con_h["esel1b"], "esel1b"),
                          (headsel, con_h["headsel"], "headsel"),
                          (blocksel, con_h["blockself"], "blocksel"),
                          (headmap, con_h["headmapf"], "headmap"),
                          (pswapT, con_h["pswapT"], "pswapT"),
                          (bogc, bog_h, "bogc"), (babr, bab_h, "babr"),
                          (boutc, bout_h, "boutc"),
                          (qbc, qb_h, "qbc"), (kbc, kb_h, "kbc"),
                          (vbrow, vb_h, "vbrow"), (gbrow, gb_h, "gbrow"),
                          (fbrow, fb_h, "fbrow"),
                          (rampt, ramp_h, "rampl")):
            dmac("sp", t[:], h, reads=(), writes=(key,))
        dmac("sp", wconv[:], wconv_h, reads=(), writes=("wconv",))
        dmac("gp", gwTs[:], gwT_h, reads=(), writes=("gwTs",))
        dmac("gp", ebTs[:], ebT_h, reads=(), writes=("ebTs",))
        dmac("gp", wabsb[:], wab_h, reads=(), writes=("wabsb",))
        st("act", lambda e: e.activation(out=rampc[:], in_=rampt[:], func=AF.Tanh,
                                         scale=0.5),
           reads=("rampl",), writes=("rampc",))
        st("dve", lambda e: e.tensor_scalar(rampc[:], rampc[:], 0.5, 0.5,
                                            ALU.mult, ALU.add),
           reads=("rampc",), writes=("rampc",))
        st("dve", lambda e: e.tensor_scalar(bogc2[:], bogc[:], 0.5, None, ALU.mult),
           reads=("bogc",), writes=("bogc2",))
        st("dve", lambda e: e.memset(cpe5[:], 1e-5), reads=(), writes=("constpool",))
        st("dve", lambda e: e.memset(cpe6[:], 1e-6), reads=(), writes=("constpool",))
        st("dve", lambda e: e.memset(maccq[:], 0.0), reads=(), writes=("maccq",))
        st("dve", lambda e: e.memset(macck[:], 0.0), reads=(), writes=("macck",))
        st("dve", lambda e: e.memset(tails[:], 0.0), reads=(), writes=("tails",))

        def bcast_row(dst, row, rowkey, dstkey, n, bank):
            for j in range(0, n, 512):
                w = min(512, n - j)
                st("pe", (lambda e, jj=j, ww=w, rr=row, bb=bank:
                          e.matmul(pbank[bb][:, 0:ww], onesf[:],
                                   rr[0:1, jj:jj + ww], start=True, stop=True)),
                   reads=("onesf", rowkey), writes=(f"pb{bank}",))
                st("dve", (lambda e, dd=dst, jj=j, ww=w, bb=bank:
                           e.tensor_copy(dd[:, jj:jj + ww], pbank[bb][:, 0:ww])),
                   reads=(f"pb{bank}",), writes=(dstkey,))
        bcast_row(vbb, vbrow, "vbrow", "vbb", D, 0)
        bcast_row(fbb, fbrow, "fbrow", "fbb", D, 1)
        bcast_row(gbb, gbrow, "gbrow", "gbb", 8, 2)
        bcast_row(babb, babr, "babr", "babb", 8, 3)
        st("dve", lambda e: e.memset(st32[:], 0.0), reads=(),
           writes=tuple(("st32", h, kc) for h in range(4) for kc in range(2)))
        st("dve", lambda e: e.memset(stbf[:], 0.0), reads=(),
           writes=tuple(("stbf", h, kc) for h in range(4) for kc in range(2)))

    def load_wchunks(dst, dstkey, src_h, n_ic, width):
        for ic in range(n_ic):
            dmac("gp", dst[:, ic * width:(ic + 1) * width],
                 src_h[ic * 128:(ic + 1) * 128, :],
                 reads=(), writes=((dstkey, ic),))

    def phase0():
        for i in range(NT):
            p = i % 2
            xtile, xbtile, xntile = xt[p], xbf[p], xn32t[p]
            kx, kxb, kxn = f"xt{p}", f"xbf{p}", f"xn32t{p}"
            dmac("sp", xtile[:], x_h[i * 128:(i + 1) * 128, :],
                 reads=(), writes=(kx,))
            st("dve", (lambda e, xx=xtile: e.tensor_reduce(
                lnst[:, 0:1], xx[:], AX.X, ALU.add)),
               reads=(kx,), writes=("lnst0",))
            st("dve", lambda e: e.tensor_scalar(
                lnst[:, 1:2], lnst[:, 0:1], -1.0 / D, None, ALU.mult),
               reads=("lnst0",), writes=("lnst1",))
            st("dve", (lambda e, xx=xtile: e.tensor_mul(x2t[:], xx[:], xx[:])),
               reads=(kx,), writes=("x2t",))
            st("dve", lambda e: e.tensor_reduce(
                lnst[:, 2:3], x2t[:], AX.X, ALU.add),
               reads=("x2t",), writes=("lnst2",))
            st("dve", lambda e: e.tensor_scalar(
                lnst[:, 3:4], lnst[:, 2:3], 1.0 / D, None, ALU.mult),
               reads=("lnst2",), writes=("lnst3",))
            st("dve", lambda e: e.tensor_mul(
                lnst[:, 4:5], lnst[:, 1:2], lnst[:, 1:2]),
               reads=("lnst1",), writes=("lnst4",))
            st("dve", lambda e: e.tensor_sub(
                lnst[:, 5:6], lnst[:, 3:4], lnst[:, 4:5]),
               reads=("lnst3", "lnst4"), writes=("lnst5",))
            st("act", lambda e: e.activation(
                out=lnst[:, 6:7], in_=lnst[:, 5:6], func=AF.Ln, bias=1e-5),
               reads=("lnst5", "constpool"), writes=("lnst6",))
            st("act", lambda e: e.activation(
                out=lnst[:, 7:8], in_=lnst[:, 6:7], func=AF.Exp, scale=-0.5),
               reads=("lnst6",), writes=("lnst7",))
            st("dve", (lambda e, xx=xtile, xo=xntile: e.tensor_scalar(
                xo[:], xx[:], lnst[:, 1:2], lnst[:, 7:8], ALU.add, ALU.mult)),
               reads=(kx, "lnst1", "lnst7"), writes=(kxn,))
            dmac("gp", xn32_d[i * 128:(i + 1) * 128, :], xntile[:],
                 reads=(kxn,), writes=(("xn32d", i),))
            st("dve", (lambda e, xx=xtile, xo=xntile: e.tensor_add(
                x2t[:], xx[:], xo[:])),
               reads=(kx, kxn), writes=("x2t",))
            dmac("gp", base_d[i * 128:(i + 1) * 128, :], x2t[:],
                 reads=("x2t",), writes=(("based", i),))
            st("dve", (lambda e, xo=xntile, xb=xbtile: e.tensor_copy(
                xb[:], xo[:])),
               reads=(kxn,), writes=(kxb,))
            for ic in range(8):
                bank = ic % 2
                st("pe", (lambda e, xb=xbtile, icc=ic, bb=bank: e.transpose(
                    pbtr[bb][:, 0:128], xb[:, icc * 128:(icc + 1) * 128],
                    identb[:])),
                   reads=(kxb, "identb"), writes=(f"pt{bank}",))
                st("dve", (lambda e, icc=ic, ii=i, bb=bank: e.tensor_copy(
                    xnfm[:, icc * S + ii * 128: icc * S + (ii + 1) * 128],
                    pbtr[bb][:, 0:128])),
                   reads=(f"pt{bank}",), writes=(("xnfm", ic, i),))

    def phase1a(g):
        winsb, wogsb, praw = g["winsb"], g["wogsb"], g["praw"]
        load_wchunks(winsb, "winsb", win_h, 8, 3 * D)
        load_wchunks(wogsb, "wogsb", wog_h, 8, D)
        for s_ in range(NSL):
            t0 = s_ * SL
            xn_reads = tuple(("xnfm", ic, t) for ic in range(8)
                             for t in range(s_ * 4, s_ * 4 + 4))
            for oc in range(24):
                qkv, occ = oc // 8, oc % 8
                bank = oc % 2
                for ic in range(8):
                    st("pe", (lambda e, oo=oc, ii=ic, bb=bank, tt=t0:
                              e.matmul(pbank[bb][:, :],
                                       winsb[:, ii * 3 * D + oo * 128:
                                             ii * 3 * D + oo * 128 + 128],
                                       xnfm[:, ii * S + tt: ii * S + tt + SL],
                                       start=(ii == 0), stop=(ii == 7))),
                       reads=(("winsb", ic),) + xn_reads,
                       writes=(f"pb{bank}",))
                pr = praw[oc % 4]
                kpr = f"praw{oc % 4}"
                st("dve", (lambda e, pr_=pr, oo=oc: e.tensor_copy(
                    pr_[:, 0:3], tails[:, oo * 3:oo * 3 + 3])),
                   reads=(("tails", oc),), writes=((kpr, "head"),))
                st("act", (lambda e, pr_=pr, bb=bank: e.activation(
                    out=pr_[:, 3:SL + 3], in_=pbank[bb][:, :],
                    func=AF.Identity)),
                   reads=(f"pb{bank}",), writes=(kpr,))
                st("dve", (lambda e, pr_=pr, oo=oc: e.tensor_copy(
                    tails[:, oo * 3:oo * 3 + 3], pr_[:, SL:SL + 3])),
                   reads=(kpr,), writes=(("tails", oc),))
                wc0 = qkv * 32 + occ * 4
                ceng = "dve"
                cvt = convt if oc % 2 == 0 else convt2
                kcv = "convt" if oc % 2 == 0 else "convt2"
                st(ceng, (lambda e, pr_=pr, w0=wc0, cv=cvt: e.tensor_scalar(
                    cv[:], pr_[:, 0:SL], wconv[:, w0:w0 + 1], None,
                    ALU.mult)),
                   reads=(kpr, (kpr, "head"), "wconv"), writes=(kcv,))
                for tap in range(1, 4):
                    st(ceng, (lambda e, pr_=pr, wc=wc0 + tap, tp=tap, cv=cvt:
                               e.scalar_tensor_tensor(
                                   cv[:], pr_[:, tp:SL + tp],
                                   wconv[:, wc:wc + 1], cv[:],
                                   ALU.mult, ALU.add)),
                       reads=(kpr, (kpr, "head"), "wconv", kcv),
                       writes=(kcv,))
                st("act", (lambda e, cv=cvt: e.activation(
                    out=cv[:], in_=cv[:], func=AF.Silu)),
                   reads=(kcv,), writes=(kcv,))
                dst_d = (qc_d, kc_d, vc_d)[qkv]
                dmac("gp", dst_d[occ * 128:(occ + 1) * 128, t0:t0 + SL],
                     cvt[:], reads=(kcv,),
                     writes=((("qc", "kc", "vc")[qkv], occ, s_),))
            for oc in range(8):
                bank = 2 + (oc % 2)
                for ic in range(8):
                    st("pe", (lambda e, oo=oc, ii=ic, bb=bank, tt=t0:
                              e.matmul(pbank[bb][:, :],
                                       wogsb[:, ii * D + oo * 128:
                                             ii * D + oo * 128 + 128],
                                       xnfm[:, ii * S + tt: ii * S + tt + SL],
                                       start=(ii == 0), stop=(ii == 7))),
                       reads=(("wogsb", ic),) + xn_reads,
                       writes=(f"pb{bank}",))
                p = oc % 2
                ogt = g["ogt"][p]
                st("act", (lambda e, oo=oc, bb=bank, og=ogt: e.activation(
                    out=og[:], in_=pbank[bb][:, :], func=AF.Tanh,
                    scale=0.5, bias=bogc2[:, oo:oo + 1])),
                   reads=(f"pb{bank}", "bogc2"), writes=(f"ogt{p}",))
                st("dve", (lambda e, og=ogt: e.tensor_scalar(
                    og[:], og[:], 0.5, 0.5, ALU.mult, ALU.add)),
                   reads=(f"ogt{p}",), writes=(f"ogt{p}",))
                dmac("gp", oglog_d[oc * 128:(oc + 1) * 128, t0:t0 + SL],
                     ogt[:], reads=(f"ogt{p}",), writes=(("oglog", oc, s_),))
            for tt in range(4):
                ti = s_ * 4 + tt
                bank = 4
                for ic in range(8):
                    st("pe", (lambda e, ii=ic, tt_=ti, bb=bank: e.matmul(
                        pbank[bb][:, 0:8],
                        xnfm[:, ii * S + tt_ * 128: ii * S + tt_ * 128 + 128],
                        wabsb[:, ii * 8:(ii + 1) * 8],
                        start=(ii == 0), stop=(ii == 7))),
                       reads=xn_reads + ("wabsb",), writes=(f"pb{bank}",))
                st("dve", (lambda e, tt_=ti, bb=bank: e.tensor_add(
                    absb[:, tt_ * 8:(tt_ + 1) * 8], pbank[bb][:, 0:8],
                    babb[:, 0:8])),
                   reads=(f"pb{bank}", "babb"), writes=(("absb", ti),))
                st("act", (lambda e, tt_=ti: e.activation(
                    out=absb[:, tt_ * 8:(tt_ + 1) * 8],
                    in_=absb[:, tt_ * 8:(tt_ + 1) * 8], func=AF.Tanh,
                    scale=0.5)),
                   reads=(("absb", ti),), writes=(("absb", ti),))
                st("dve", (lambda e, tt_=ti: e.tensor_scalar(
                    absb[:, tt_ * 8:tt_ * 8 + 4],
                    absb[:, tt_ * 8:tt_ * 8 + 4], 0.49, 0.5,
                    ALU.mult, ALU.add)),
                   reads=(("absb", ti),), writes=(("absb", ti),))
                st("dve", (lambda e, tt_=ti: e.tensor_scalar(
                    absb[:, tt_ * 8 + 4:tt_ * 8 + 8],
                    absb[:, tt_ * 8 + 4:tt_ * 8 + 8], 0.5, 0.5,
                    ALU.mult, ALU.add)),
                   reads=(("absb", ti),), writes=(("absb", ti),))

    def phase1b(g):
        woutsb = g["woutsb"]
        qnsl, knsl, vnsl = g["qnsl"], g["knsl"], g["vnsl"]
        sqtl, rrow = g["sqt"], g["rrow"]
        d = cht4
        rn, rfm, dnb = readn4_t, readfm4_t, dnbase_t
        load_wchunks(woutsb, "woutsb", wout_h, 8, D)
        PW0 = ("pb0", "pb1")
        PW1 = ("pb2", "pb3")
        kq_r = tuple(("qnsl", oc) for oc in range(8))
        kk_r = tuple(("knsl", oc) for oc in range(8))
        kv_r = tuple(("vnsl", oc) for oc in range(8))

        def qsl(tt, h, half):
            oc = 2 * h + half
            return qnsl[:, oc * SL + tt * 128: oc * SL + tt * 128 + 128]

        def ksl(tt, h, half):
            oc = 2 * h + half
            return knsl[:, oc * SL + tt * 128: oc * SL + tt * 128 + 128]

        def vsl(tt, h, half):
            oc = 2 * h + half
            return vnsl[:, oc * SL + tt * 128: oc * SL + tt * 128 + 128]

        def prep(s_, tt, par):
            ti = s_ * 4 + tt
            t0 = s_ * SL + tt * 128
            kog, kct, kcl = f"ogch{par}", f"ctc{par}", f"clast4{par}"
            kdi, kat, kwt = f"dincl4{par}", f"at4{par}", f"wt4{par}"
            kan, kvt, kkt = f"an4{par}", f"vt4{par}", f"ktok4{par}"
            for oc in range(8):
                dmac("sp", d[kog][:, oc * 128:(oc + 1) * 128],
                     oglog_d[oc * 128:(oc + 1) * 128, t0:t0 + 128],
                     reads=(("oglog", oc, s_), kog), writes=(kog,))
            st("act", (lambda e, tt_=ti: e.activation(
                out=d["gsb"][:, 0:4], in_=absb[:, tt_ * 8:tt_ * 8 + 4],
                func=AF.Ln)),
               reads=(("absb", ti),), writes=("gsb",))
            st("act", (lambda e, tt_=ti: e.activation(
                out=d["biasc"][:, 0:4],
                in_=absb[:, tt_ * 8 + 4:tt_ * 8 + 8], func=AF.Ln)),
               reads=(("absb", ti),), writes=("biasc",))
            st("pe", lambda e: e.matmul(
                pbank[5][:, 0:4], uincf[:], d["gsb"][:, 0:4],
                start=True, stop=True),
               reads=("gsb", "uincf"), writes=("pb5",))
            st("dve", lambda e: e.tensor_copy(d["gsb"][:, 0:4],
                                              pbank[5][:, 0:4]),
               reads=("pb5",), writes=("gsb",))
            st("dve", lambda e: e.tensor_sub(
                d["biasc"][:, 0:4], d["biasc"][:, 0:4], d["gsb"][:, 0:4]),
               reads=("biasc", "gsb"), writes=("biasc",))
            st("act", (lambda e, kc=kct: e.activation(
                out=d[kc][:, 0:4], in_=d["gsb"][:, 0:4], func=AF.Exp)),
               reads=("gsb",), writes=(kct,))
            st("pe", (lambda e, kc=kct: e.matmul(
                pbank[5][:, 4:8], e127[:], d[kc][:, 0:4],
                start=True, stop=True)),
               reads=(kct, "e127"), writes=("pb5",))
            st("dve", (lambda e, kc=kcl: e.tensor_copy(d[kc][:, 0:4],
                                                       pbank[5][:, 4:8])),
               reads=("pb5",), writes=(kcl,))
            for h in range(4):
                st("pe", (lambda e, hh=h: e.matmul(
                    pbank[5][0:1, hh * 128:(hh + 1) * 128],
                    d["gsb"][:, hh:hh + 1], identf[:],
                    start=True, stop=True)),
                   reads=("gsb", "identf"), writes=("pb5",))
            st("dve", lambda e: e.tensor_copy(d["growall"][0:1, 0:512],
                                              pbank[5][0:1, 0:512]),
               reads=("pb5",), writes=("growall",))
            for h in range(4):
                st("pe", (lambda e, hh=h: e.matmul(
                    pbank[4][:, hh * 128:(hh + 1) * 128], onesf[:],
                    d["growall"][0:1, hh * 128:(hh + 1) * 128],
                    start=True, stop=False)),
                   reads=("growall", "onesf"), writes=("pb4",))
                st("pe", (lambda e, hh=h: e.matmul(
                    pbank[4][:, hh * 128:(hh + 1) * 128], maskaddT[:],
                    identf[:], start=False, stop=True)),
                   reads=("maskaddT", "identf"), writes=("pb4",))
            for h in range(4):
                st("act", (lambda e, hh=h, kd=kdi: e.activation(
                    out=d[kd][:, hh * 128:(hh + 1) * 128],
                    in_=pbank[4][:, hh * 128:(hh + 1) * 128], func=AF.Exp,
                    bias=d["biasc"][:, hh:hh + 1])),
                   reads=("pb4", "biasc"), writes=(kdi,))
            st("dve", (lambda e, kd=kdi: e.tensor_mul(
                d["dstr4"][:], d[kd][:], offdiag4[:])),
               reads=(kdi, "offdiag4"), writes=("dstr4",))
            for h in range(4):
                for half in range(2):
                    st("pe", (lambda e, hh=h, hf=half, ka=ksl(tt, h, half):
                              e.matmul(
                                  pbw[0][:, hh * 128:(hh + 1) * 128],
                                  ka, ka,
                                  start=(hf == 0), stop=(hf == 1))),
                       reads=kk_r, writes=PW0)
            st("dve", (lambda e, ka=kat: e.tensor_mul(
                d[ka][:], pbw[0][:, 0:512], d["dstr4"][:])),
               reads=PW0 + ("dstr4",), writes=(kat,))
            for h in range(4):
                for half in range(2):
                    st("pe", (lambda e, hh=h, hf=half, ka=ksl(tt, h, half),
                              qa=qsl(tt, h, half): e.matmul(
                                  pbw[1][:, hh * 128:(hh + 1) * 128],
                                  ka, qa,
                                  start=(hf == 0), stop=(hf == 1))),
                       reads=kk_r + kq_r, writes=PW1)
            st("dve", (lambda e, kw=kwt, kd=kdi: e.tensor_mul(
                d[kw][:], pbw[1][:, 0:512], d[kd][:])),
               reads=PW1 + (kdi,), writes=(kwt,))
            for h in range(4):
                st("pe", (lambda e, hh=h, ka=kat: e.transpose(
                    pbtr[0][:, hh * 128:(hh + 1) * 128],
                    d[ka][:, hh * 128:(hh + 1) * 128], identb[:])),
                   reads=(kat, "identb"), writes=("pt0",))
            st("act", (lambda e, ka=kan: e.activation(
                out=d[ka][:], in_=pbtr[0][:, 0:512], func=AF.Identity)),
               reads=("pt0",), writes=(kan,))
            for h in range(4):
                for half in range(2):
                    st("pe", (lambda e, hh=h, hf=half, va=vsl(tt, h, half):
                              e.transpose(
                                  pbtr[1][:, (2 * hh + hf) * 128:
                                          (2 * hh + hf) * 128 + 128],
                                  va, identb[:])),
                       reads=kv_r + ("identb",), writes=("pt1",))
            st("act", (lambda e, kv=kvt: e.activation(
                out=d[kv][:], in_=pbtr[1][:, 0:1024], func=AF.Identity)),
               reads=("pt1",), writes=(kvt,))
            for h in range(4):
                for half in range(2):
                    st("pe", (lambda e, hh=h, hf=half, ka=ksl(tt, h, half):
                              e.transpose(
                                  pbtr[0][:, (2 * hh + hf) * 128:
                                          (2 * hh + hf) * 128 + 128],
                                  ka, identb[:])),
                       reads=kk_r + ("identb",), writes=("pt0",))
            for h in range(4):
                st("dve", (lambda e, hh=h, kk=kkt, kd=kdi: e.tensor_scalar(
                    d[kk][:, hh * 256: hh * 256 + 256],
                    pbtr[0][:, hh * 256: hh * 256 + 256],
                    d[kd][:, hh * 128 + 127: hh * 128 + 128],
                    None, ALU.mult)),
                   reads=("pt0", kdi), writes=(kkt,))

        def solve(s_, tt, par):
            ti = s_ * 4 + tt
            kct, kcl = f"ctc{par}", f"clast4{par}"
            kat, kwt = f"at4{par}", f"wt4{par}"
            kan, kvt, kkt = f"an4{par}", f"vt4{par}", f"ktok4{par}"
            for h in range(4):
                for half in range(2):
                    st("pe", (lambda e, hh=h, hf=half, ka=ksl(tt, h, half):
                              e.matmul(
                                  pbw[1][:, hh * 256: hh * 256 + 256],
                                  ka,
                                  stbf[:, (2 * hh + hf) * 256:
                                       (2 * hh + hf) * 256 + 256],
                                  start=(hf == 0), stop=(hf == 1))),
                       reads=kk_r + (("stbf", h, half),), writes=PW1)
            for h in range(4):
                st("act", (lambda e, hh=h, kc=kct: e.activation(
                    out=d["mc4"][:, hh * 256: hh * 256 + 256],
                    in_=pbw[1][:, hh * 256: hh * 256 + 256],
                    func=AF.Identity, scale=d[kc][:, hh:hh + 1])),
                   reads=PW1 + (kct,), writes=("mc4",))
            st("dve", (lambda e, kv=kvt: e.tensor_sub(
                d["rr4"][:], d[kv][:], d["mc4"][:])),
               reads=(kvt, "mc4"), writes=("rr4",))
            for h in range(4):
                st("pe", (lambda e, hh=h, ka=kat: e.matmul(
                    pbw[0][:, hh * 256: hh * 256 + 256],
                    d[ka][:, hh * 128:(hh + 1) * 128],
                    d["rr4"][:, hh * 256: hh * 256 + 256],
                    start=True, stop=False)),
                   reads=(kat, "rr4"), writes=PW0)
                st("pe", (lambda e, hh=h: e.matmul(
                    pbw[0][:, hh * 256: hh * 256 + 256],
                    identb[:],
                    d["rr4"][:, hh * 256: hh * 256 + 256],
                    start=False, stop=True)),
                   reads=("identb", "rr4"), writes=PW0)
            st("act", lambda e: e.activation(
                out=d["u04"][:], in_=pbw[0][:, 0:1024], func=AF.Identity),
               reads=PW0, writes=("u04",))
            cur_p, cur_b, cur_u = kat, kan, "u04"
            for lev, pwr in enumerate((2, 4, 8, 16, 32, 64)):
                new_p = ("pp4a", "pp4b")[lev % 2]
                new_b = ("bp4a", "bp4b")[lev % 2]
                for h in range(4):
                    if pwr < 64:
                        st("pe", (lambda e, hh=h, cp=cur_p, cb=cur_b:
                                  e.matmul(
                                      pbw[1][:, hh * 128:(hh + 1) * 128],
                                      d[cp][:, hh * 128:(hh + 1) * 128],
                                      d[cb][:, hh * 128:(hh + 1) * 128],
                                      start=True, stop=True)),
                           reads=(cur_p, cur_b), writes=PW1)
                    st("pe", (lambda e, hh=h, cp=cur_p, cb=cur_b: e.matmul(
                        pbw[1][:, 512 + hh * 128: 512 + (hh + 1) * 128],
                        d[cb][:, hh * 128:(hh + 1) * 128],
                        d[cp][:, hh * 128:(hh + 1) * 128],
                        start=True, stop=True)),
                       reads=(cur_p, cur_b), writes=PW1)
                if pwr < 64:
                    st("act", (lambda e, nb=new_b: e.activation(
                        out=d[nb][:], in_=pbw[1][:, 0:512],
                        func=AF.Identity)),
                       reads=PW1, writes=(new_b,))
                st("act", (lambda e, np_=new_p: e.activation(
                    out=d[np_][:], in_=pbw[1][:, 512:1024],
                    func=AF.Identity)),
                   reads=PW1, writes=(new_p,))
                for h in range(4):
                    st("pe", (lambda e, hh=h, np_=new_p, cu=cur_u: e.matmul(
                        pbw[0][:, hh * 256: hh * 256 + 256],
                        d[np_][:, hh * 128:(hh + 1) * 128],
                        d[cu][:, hh * 256: hh * 256 + 256],
                        start=True, stop=False)),
                       reads=(new_p, cur_u), writes=PW0)
                    st("pe", (lambda e, hh=h, cu=cur_u: e.matmul(
                        pbw[0][:, hh * 256: hh * 256 + 256],
                        identb[:],
                        d[cu][:, hh * 256: hh * 256 + 256],
                        start=False, stop=True)),
                       reads=("identb", cur_u), writes=PW0)
                nxt_u = "rr4" if cur_u == "u04" else "u04"
                st("act", (lambda e, nu=nxt_u: e.activation(
                    out=d[nu][:], in_=pbw[0][:, 0:1024], func=AF.Identity)),
                   reads=PW0, writes=(nxt_u,))
                cur_p, cur_b, cur_u = new_p, new_b, nxt_u
            for h in range(4):
                for half in range(2):
                    st("pe", (lambda e, hh=h, hf=half, qa=qsl(tt, h, half):
                              e.matmul(
                                  pbw[1][:, hh * 256: hh * 256 + 256],
                                  qa,
                                  stbf[:, (2 * hh + hf) * 256:
                                       (2 * hh + hf) * 256 + 256],
                                  start=(hf == 0), stop=(hf == 1))),
                       reads=kq_r + (("stbf", h, half),), writes=PW1)
            for h in range(4):
                st("act", (lambda e, hh=h, kc=kct: e.activation(
                    out=d["readf4"][:, hh * 256: hh * 256 + 256],
                    in_=pbw[1][:, hh * 256: hh * 256 + 256],
                    func=AF.Identity, scale=d[kc][:, hh:hh + 1])),
                   reads=PW1 + (kct,), writes=("readf4",))
            for h in range(4):
                st("pe", (lambda e, hh=h, cu=cur_u, kw=kwt: e.matmul(
                    pbw[0][:, hh * 256: hh * 256 + 256],
                    d[kw][:, hh * 128:(hh + 1) * 128],
                    d[cu][:, hh * 256: hh * 256 + 256],
                    start=True, stop=True)),
                   reads=(kwt, cur_u), writes=PW0)
            st("dve", lambda e: e.tensor_add(
                d["readf4"][:], d["readf4"][:], pbw[0][:, 0:1024]),
               reads=("readf4",) + PW0, writes=("readf4",))
            for h in range(4):
                st("act", (lambda e, hh=h: e.activation(
                    out=d["mc4"][:, hh * 256: hh * 256 + 256],
                    in_=d["readf4"][:, hh * 256: hh * 256 + 256],
                    func=AF.Square, accum_out=d["tiny4"][:, hh:hh + 1])),
                   reads=("readf4",), writes=("mc4", "tiny4"))
            st("act", lambda e: e.activation(
                out=d["tiny4"][:, 4:8], in_=d["tiny4"][:, 0:4], func=AF.Ln,
                scale=1.0 / 256, bias=1e-6),
               reads=("tiny4", "constpool"), writes=("tiny4",))
            st("act", lambda e: e.activation(
                out=d["tiny4"][:, 8:12], in_=d["tiny4"][:, 4:8],
                func=AF.Exp, scale=-0.5),
               reads=("tiny4",), writes=("tiny4",))
            for h in range(4):
                st("dve", (lambda e, hh=h: e.tensor_scalar(
                    rn[:, hh * 256:(hh + 1) * 256],
                    d["readf4"][:, hh * 256: hh * 256 + 256],
                    d["tiny4"][:, 8 + hh:9 + hh], None, ALU.mult)),
                   reads=("readf4", "tiny4"), writes=(("readn4", h),))
            for h in range(4):
                for kc in range(2):
                    dstp = (pbw[0] if h < 2 else pbw[1])
                    ko = ((2 * h + kc) % 4) * 256
                    st("pe", (lambda e, hh=h, cc=kc, cu=cur_u, dd=dstp,
                              oo=ko, kk=kkt: e.matmul(
                                  dd[:, oo: oo + 256],
                                  d[kk][:, (2 * hh + cc) * 128:
                                         (2 * hh + cc) * 128 + 128],
                                  d[cu][:, hh * 256: hh * 256 + 256],
                                  start=True, stop=True)),
                       reads=(kkt, cur_u),
                       writes=(PW0 if h < 2 else PW1))
            for h in range(4):
                for kc in range(2):
                    srcp = (pbw[0] if h < 2 else pbw[1])
                    ko = ((2 * h + kc) % 4) * 256
                    sl32 = st32[:, (2 * h + kc) * 256:(2 * h + kc) * 256 + 256]
                    st("dve", (lambda e, hh=h, s3=sl32, sp=srcp, oo=ko,
                               kc_=kcl: e.scalar_tensor_tensor(
                                   s3, s3, d[kc_][:, hh:hh + 1],
                                   sp[:, oo: oo + 256], ALU.mult, ALU.add)),
                       reads=(("st32", h, kc), kcl) +
                             (PW0 if h < 2 else PW1),
                       writes=(("st32", h, kc),))
            for half in range(2):
                st("act", (lambda e, hf=half: e.activation(
                    out=stbf[:, hf * 1024:(hf + 1) * 1024],
                    in_=st32[:, hf * 1024:(hf + 1) * 1024],
                    func=AF.Identity)),
                   reads=tuple(("st32", h2, kc2) for h2 in range(4)
                               for kc2 in range(2)),
                   writes=tuple(("stbf", h2, kc2)
                                for h2 in range(2 * half, 2 * half + 2)
                                for kc2 in range(2)))
            if "readn" in dbg_outs:
                dmac("gp", readn_d[ti * 128:(ti + 1) * 128, :], rn[:],
                     reads=tuple(("readn4", h2) for h2 in range(4)),
                     writes=(("readnd", ti),))
            _chunk_tail4(ti, tt, par, woutsb, rn, rfm, dnb, d)

        for s_ in range(NSL):
            t0 = s_ * SL
            for oc in range(8):
                dmac("sp", qnsl[:, oc * SL:(oc + 1) * SL],
                     qc_d[oc * 128:(oc + 1) * 128, t0:t0 + SL],
                     reads=(("qc", oc, s_), ("qnsl", oc)), writes=(("qnsl", oc),))
                dmac("sp", knsl[:, oc * SL:(oc + 1) * SL],
                     kc_d[oc * 128:(oc + 1) * 128, t0:t0 + SL],
                     reads=(("kc", oc, s_), ("knsl", oc)), writes=(("knsl", oc),))
                dmac("sp", vnsl[:, oc * SL:(oc + 1) * SL],
                     vc_d[oc * 128:(oc + 1) * 128, t0:t0 + SL],
                     reads=(("vc", oc, s_), ("vnsl", oc)), writes=(("vnsl", oc),))
            for which, tile in (("q", qnsl), ("k", knsl)):
                kk = "qnsl" if which == "q" else "knsl"
                for h in range(4):
                    bank = h
                    for half in range(2):
                        oc = 2 * h + half
                        st("dve", (lambda e, tl=tile, oo=oc: e.tensor_mul(
                            sqtl[:], tl[:, oo * SL:(oo + 1) * SL],
                            tl[:, oo * SL:(oo + 1) * SL])),
                           reads=((kk, oc),), writes=("sqt",))
                        st("pe", (lambda e, hf=half, bb=bank: e.matmul(
                            pbank[bb][0:1, 0:SL], onescolb[:], sqtl[:],
                            start=(hf == 0), stop=(hf == 1))),
                           reads=("sqt", "onescolb"), writes=(f"pb{bank}",))
                    st("dve", (lambda e, bb=bank: e.tensor_scalar(
                        rrow[0:1, 0:SL], pbank[bb][0:1, 0:SL],
                        1e-12, None, ALU.max)),
                       reads=(f"pb{bank}",), writes=("rrow",))
                    st("act", lambda e: e.activation(
                        out=rrow[0:1, 0:SL], in_=rrow[0:1, 0:SL], func=AF.Ln),
                       reads=("rrow",), writes=("rrow",))
                    st("act", lambda e: e.activation(
                        out=rrow[0:1, 0:SL], in_=rrow[0:1, 0:SL],
                        func=AF.Exp, scale=-0.5),
                       reads=("rrow",), writes=("rrow",))
                    st("dve", lambda e: e.tensor_scalar(
                        rrow[0:1, 0:SL], rrow[0:1, 0:SL], 1e6, None, ALU.min),
                       reads=("rrow",), writes=("rrow",))
                    bank2 = 4
                    st("pe", (lambda e, bb=bank2: e.matmul(
                        pbank[bb][:, 0:SL], onesf[:], rrow[0:1, 0:SL],
                        start=True, stop=True)),
                       reads=("rrow", "onesf"), writes=(f"pb{bank2}",))
                    for half in range(2):
                        oc = 2 * h + half
                        st("dve", (lambda e, tl=tile, oo=oc, bb=bank2:
                                   e.tensor_mul(
                                       tl[:, oo * SL:(oo + 1) * SL],
                                       tl[:, oo * SL:(oo + 1) * SL],
                                       pbank[bb][:, 0:SL])),
                           reads=((kk, oc), f"pb{bank2}"), writes=((kk, oc),))
            prep(s_, 0, 0)
            prep(s_, 1, 1)
            solve(s_, 0, 0)
            prep(s_, 2, 0)
            solve(s_, 1, 1)
            prep(s_, 3, 1)
            solve(s_, 2, 0)
            solve(s_, 3, 1)

    def _chunk_tail4(ti, tt, par, woutsb, rn, rfm, dnb, d):
        kog = f"ogch{par}"
        krn = tuple(("readn4", h) for h in range(4))
        for vc in range(8):
            bank = vc % 2
            st("pe", (lambda e, v=vc, bb=bank: e.transpose(
                pbtr[bb][:, (v // 2) * 128:(v // 2) * 128 + 128],
                rn[:, v * 128:(v + 1) * 128], identb[:])),
               reads=krn + ("identb",), writes=(f"pt{bank}",))
        for bank in range(2):
            st("dve", (lambda e, bb=bank: e.tensor_copy(
                rfm[:, bb * 512:(bb + 1) * 512], pbtr[bb][:, 0:512])),
               reads=(f"pt{bank}",), writes=(("readfm4", bank),))
        dmac("sp", dnb[:], base_d[ti * 128:(ti + 1) * 128, :],
             reads=(("based", ti), ("dnbase", "out")),
             writes=(("dnbase", "sum"),))
        for oc in range(8):
            bank = 4
            for vc in range(8):
                blk = (vc % 2) * 512 + (vc // 2) * 128
                st("pe", (lambda e, oo=oc, v=vc, bb=bank, bl=blk: e.matmul(
                    pbank[bb][:, 0:128],
                    woutsb[:, v * D + oo * 128: v * D + oo * 128 + 128],
                    rfm[:, bl: bl + 128],
                    start=(v == 0), stop=(v == 7))),
                   reads=(("woutsb", vc), ("readfm4", vc % 2)),
                   writes=(f"pb{bank}",))
            ogslice = d[kog][:, oc * 128:(oc + 1) * 128]
            st("dve", (lambda e, oo=oc, bb=bank, og=ogslice:
                       e.scalar_tensor_tensor(
                           d["mc4"][:, 0:128], pbank[bb][:, 0:128],
                           boutc[:, oo:oo + 1], og, ALU.add, ALU.mult)),
               reads=(f"pb{bank}", "boutc", kog, "mc4"),
               writes=("mc4",))
            st("dve", (lambda e, oo=oc: e.tensor_scalar(
                d["mc4"][:, 128:256], d["mc4"][:, 0:128],
                rampc[:, oo:oo + 1], None, ALU.mult)),
               reads=("mc4", "rampc"), writes=("mc4",))
            st("dve", lambda e: e.tensor_copy(d["rr4"][:, 0:128],
                                              d["mc4"][:, 128:256]),
               reads=("mc4", "rr4"), writes=("rr4",))
            bank2 = oc % 2
            st("pe", (lambda e, bb=bank2: e.transpose(
                pbtr[bb][:, 512:640], d["rr4"][:, 0:128], identb[:])),
               reads=("rr4", "identb"), writes=(f"pt{bank2}",))
            st("dve", (lambda e, oo=oc, bb=bank2, b_=dnb: e.tensor_add(
                b_[:, oo * 128:(oo + 1) * 128], b_[:, oo * 128:(oo + 1) * 128],
                pbtr[bb][:, 512:640])),
               reads=(f"pt{bank2}", ("dnbase", "sum")),
               writes=(("dnbase", "sum"),))
        dmac("gp", base_d[ti * 128:(ti + 1) * 128, :], dnb[:],
             reads=(("dnbase", "sum"),),
             writes=(("based", ti), ("dnbase", "out")))

    def phase2proj(g):
        qwTs, kwTs, vwTs = g["qwTs"], g["kwTs"], g["vwTs"]
        copt, sipt = g["copt"], g["sipt"]
        qpt, ropet1, ropet2, ropeo = g["qpt"], g["ropet1"], g["ropet2"], g["ropeo"]
        vtokt, sqa = g["vtokt"], g["sqa"]
        load_wchunks(qwTs, "qwTs", qwT_h, 8, D)
        load_wchunks(kwTs, "kwTs", kwT_h, 8, D)
        load_wchunks(vwTs, "vwTs", vwT_h, 8, D)
        dmac("sp", copt[:], con_h["copt"], reads=(), writes=("copt",))
        dmac("sp", sipt[:], con_h["sipt"], reads=(), writes=("sipt",))
        for s_ in range(NSL):
            t0 = s_ * SL
            xn_reads = tuple(("xnfm", ic, t) for ic in range(8)
                             for t in range(s_ * 4, s_ * 4 + 4))
            for which, wts, wkey, bias, bkey, dst_d, dkey, macc, mkey in (
                    ("q", qwTs, "qwTs", qbc, "qbc", qr_d, "qr", maccq, "maccq"),
                    ("k", kwTs, "kwTs", kbc, "kbc", kr_d, "kr", macck, "macck")):
                for oc in range(8):
                    bank = oc % 2
                    for ic in range(8):
                        st("pe", (lambda e, oo=oc, ii=ic, bb=bank, tt=t0, w=wts:
                                  e.matmul(pbank[bb][:, :],
                                           w[:, ii * D + oo * 128:
                                             ii * D + oo * 128 + 128],
                                           xnfm[:, ii * S + tt: ii * S + tt + SL],
                                           start=(ii == 0), stop=(ii == 7))),
                           reads=((wkey, ic),) + xn_reads, writes=(f"pb{bank}",))
                    pq = oc % 2
                    qt = qpt[pq]
                    st("act", (lambda e, oo=oc, bb=bank, q_=qt, b_=bias:
                               e.activation(out=q_[:], in_=pbank[bb][:, :],
                                            func=AF.Identity,
                                            bias=b_[:, oo:oo + 1])),
                       reads=(f"pb{bank}", bkey), writes=((f"qpt{pq}",),))
                    st("dve", (lambda e, q_=qt: e.tensor_mul(
                        sqa[:], q_[:], q_[:])),
                       reads=((f"qpt{pq}",),), writes=("sqa",))
                    st("pe", (lambda e: e.matmul(
                        pbank[2][0:2, 0:SL], headsel[:], sqa[:],
                        start=True, stop=True)),
                       reads=("sqa", "headsel"), writes=("pb2",))
                    st("dve", lambda e: e.tensor_reduce(
                        mtmp[:], pbank[2][0:2, 0:SL], AX.X, ALU.max),
                       reads=("pb2",), writes=("mtmp",))
                    st("dve", (lambda e, m_=macc, oo=oc: e.tensor_max(
                        m_[:, oo:oo + 1], m_[:, oo:oo + 1], mtmp[:])),
                       reads=("mtmp", mkey), writes=(mkey,))
                    st("pe", (lambda e, q_=qt: e.matmul(
                        pbank[2][:, 0:SL], pswapT[:], q_[:],
                        start=True, stop=True)),
                       reads=((f"qpt{pq}",), "pswapT"), writes=("pb2",))
                    st("dve", (lambda e, q_=qt, tt=t0: e.tensor_mul(
                        ropet1[:], q_[:], copt[:, tt:tt + SL])),
                       reads=((f"qpt{pq}",), "copt"), writes=("ropet1",))
                    st("dve", (lambda e, tt=t0: e.tensor_mul(
                        ropet2[:], pbank[2][:, 0:SL], sipt[:, tt:tt + SL])),
                       reads=("pb2", "sipt"), writes=("ropet2",))
                    st("dve", lambda e: e.tensor_add(
                        ropeo[:], ropet1[:], ropet2[:]),
                       reads=("ropet1", "ropet2"), writes=("ropeo",))
                    dmac("gp", dst_d[oc * 128:(oc + 1) * 128, t0:t0 + SL],
                         ropeo[:], reads=("ropeo",), writes=((dkey, oc, s_),))
            for u in range(4):
                ti = s_ * 4 + u
                for half in range(2):
                    bank = 2 + half
                    for ic in range(8):
                        st("pe", (lambda e, ii=ic, hf=half, bb=bank, tt_=ti:
                                  e.matmul(pbank[bb][:, :],
                                           xnfm[:, ii * S + tt_ * 128:
                                                ii * S + tt_ * 128 + 128],
                                           vwTs[:, ii * D + hf * 512:
                                                ii * D + hf * 512 + 512],
                                           start=(ii == 0), stop=(ii == 7))),
                           reads=(("vwTs", ic),) + xn_reads,
                           writes=(f"pb{bank}",))
                    st("dve", (lambda e, hf=half, bb=2 + half: e.tensor_add(
                        vtokt[:, hf * 512:hf * 512 + 512], pbank[bb][:, :],
                        vbb[:, hf * 512:hf * 512 + 512])),
                       reads=(f"pb{2 + half}", "vbb"), writes=(("vtokt", half),))
                dmac("gp", vt_d[ti * 128:(ti + 1) * 128, :], vtokt[:],
                     reads=(("vtokt", 0), ("vtokt", 1)), writes=(("vt", ti),))

    def phase2attn(g):
        khd, qhd, vslab = g["khd"], g["qhd"], g["vslab"]
        ptl = g["ptl"]
        aot = g["aot"]
        zsb = g["zsb"]
        negm, mqcol, mkcol = g["negmh"], g["mqcol"], g["mkcol"]
        st("dve", lambda e: e.memset(khd[64:128, :], 0.0),
           reads=(), writes=("khd",))
        st("dve", lambda e: e.memset(qhd[64:128, :], 0.0),
           reads=(), writes=("qhd",))
        st("dve", lambda e: e.memset(vslab[:], 0.0),
           reads=(), writes=(("vslab", "ones"),))
        st("dve", lambda e: e.memset(
            vslab[:].rearrange("p (kt c) -> p kt c", c=128)[:, :, 64:65], 1.0),
           reads=(("vslab", "ones"),), writes=(("vslab", "ones"),))
        st("dve", lambda e: e.tensor_copy(maccqb[:], maccq[:]),
           reads=("maccq",), writes=("maccqb",))
        st("dve", lambda e: e.tensor_copy(macckb[:], macck[:]),
           reads=("macck",), writes=("macckb",))
        vt_view = vt_d.rearrange("(kt p) v -> p kt v", p=128)
        for h in range(AH):
            dmac("sp", khd[0:64, :], kr_d[h * 64:(h + 1) * 64, :],
                 reads=tuple(("kr", 2 * (h // 2) + hh, s2) for hh in range(2)
                             for s2 in range(NSL)) + ("khd",),
                 writes=("khd",))
            dmac("sp", qhd[0:64, :], qr_d[h * 64:(h + 1) * 64, :],
                 reads=tuple(("qr", 2 * (h // 2) + hh, s2) for hh in range(2)
                             for s2 in range(NSL)) + ("qhd",),
                 writes=("qhd",))
            for g4 in range(4):
                dmac("sp",
                     vslab[:].rearrange("p (kt c) -> p kt c", c=128)
                     [:, g4 * 8:(g4 + 1) * 8, 0:64],
                     vt_view[:, g4 * 8:(g4 + 1) * 8, h * 64:(h + 1) * 64],
                     reads=tuple(("vt", tti) for tti in range(NT)) +
                           (("vslab", "war"),),
                     writes=(("vslab", g4),))
            hm, hs_ = (esel0b, "esel0b") if h % 2 == 0 else (esel1b, "esel1b")
            st("pe", (lambda e, sel=hm, hh=h: e.matmul(
                pbank[5][:, 0:1], sel[:], maccqb[:, hh // 2:hh // 2 + 1],
                start=True, stop=True)),
               reads=(hs_, "maccqb"), writes=("pb5",))
            st("act", lambda e: e.activation(out=mqcol[:], in_=pbank[5][:, 0:1],
                                             func=AF.Ln),
               reads=("pb5",), writes=("mqcol",))
            st("act", lambda e: e.activation(out=mqcol[:], in_=mqcol[:],
                                             func=AF.Exp, scale=0.5),
               reads=("mqcol",), writes=("mqcol",))
            st("pe", (lambda e, sel=hm, hh=h: e.matmul(
                pbank[5][:, 0:1], sel[:], macckb[:, hh // 2:hh // 2 + 1],
                start=True, stop=True)),
               reads=(hs_, "macckb"), writes=("pb5",))
            st("act", lambda e: e.activation(out=mkcol[:], in_=pbank[5][:, 0:1],
                                             func=AF.Ln),
               reads=("pb5",), writes=("mkcol",))
            st("act", lambda e: e.activation(out=mkcol[:], in_=mkcol[:],
                                             func=AF.Exp, scale=0.5),
               reads=("mkcol",), writes=("mkcol",))
            st("dve", lambda e: e.tensor_mul(negm[:], mqcol[:], mkcol[:]),
               reads=("mqcol", "mkcol"), writes=("negm",))
            st("dve", lambda e: e.tensor_scalar(
                negm[:], negm[:], -0.125, None, ALU.mult),
               reads=("negm",), writes=("negm",))
            for qb in range(8):
                q0 = qb * SL
                NG = NT // 2

                def emit_st(g):
                    for half in range(2):
                        kt = 2 * g + half
                        bs = 2 * (g % 2) + half
                        st("pe", (lambda e, kk=kt, q_=q0, bb=bs: e.matmul(
                            pbank[bb][:, :], khd[:, kk * 128:(kk + 1) * 128],
                            qhd[:, q_:q_ + SL], start=True, stop=True)),
                           reads=("khd", "qhd"), writes=(f"pb{bs}",))

                emit_st(0)
                emit_st(1)
                for g in range(NG):
                    jp = g % 2
                    b0 = 2 * (g % 2)
                    st("act", (lambda e, j=jp: e.activation(
                        out=ptl[j][:, 0:1024], in_=pbw[j][:, 0:1024],
                        func=AF.Exp, scale=0.125, bias=negm[:])),
                       reads=(f"pb{b0}", f"pb{b0 + 1}", "negm"),
                       writes=((f"ptl{jp}",),))
                    if g + 2 < NG:
                        emit_st(g + 2)
                    ob = 4 + qb % 2
                    for half in range(2):
                        kt = 2 * g + half
                        st("pe", (lambda e, kk=kt, j=jp, hf=half, bb=ob: e.matmul(
                            pbank[bb][:, 0:SL],
                            vslab[:, kk * 128:(kk + 1) * 128],
                            ptl[j][:, hf * SL:hf * SL + SL],
                            start=(kk == 0), stop=(kk == 31))),
                           reads=((f"ptl{jp}",), ("vslab", kt // 8),
                                  ("vslab", "ones")),
                           writes=(f"pb{ob}",))
                ob = 4 + qb % 2
                st("dve", (lambda e, bb=ob, q_=qb: e.tensor_copy(
                    aot[0:64, 0:SL], pbank[bb][0:64, 0:SL])),
                   reads=(f"pb{ob}",), writes=("aot",))
                st("dve", (lambda e, bb=ob, q_=qb: e.tensor_copy(
                    zsb[0:1, q_ * SL:(q_ + 1) * SL], pbank[bb][64:65, 0:SL])),
                   reads=(f"pb{ob}",), writes=("zsb",))
                dmac("gp", ao_d[h * 64:(h + 1) * 64, q0:q0 + SL], aot[0:64, 0:SL],
                     reads=("aot",), writes=(("ao", h, qb),))
                if qb == 7:
                    dmac("gp", zs_d[h:h + 1, :], zsb[0:1, :],
                         reads=("zsb",), writes=(("zs", h),))

    def phase3(g):
        aofm, fwTs = g["aofm"], g["fwTs"]
        zrec = g["zrec"]
        expd = g["expd"]
        moeacc, moebf, moefm = g["moeacc"], g["moebf"], g["moefm"]
        baset, outt = g["baset"], g["outt"]
        gatesf, gtmp, gmsk = g["gatesf"], g["gtmp"], g["gmsk"]
        gcol, gcol2 = g["gcol"], g["gcol2"]
        wtok4, wfm4 = g["wtok4"], g["wfm4"]
        load_wchunks(fwTs, "fwTs", fwT_h, 8, D)
        dmac("sp", zrec[0:AH, :], zs_d,
             reads=tuple(("zs", h) for h in range(AH)), writes=("zrec",))
        st("dve", lambda e: e.reciprocal(zrec[0:AH, :], zrec[0:AH, :]),
           reads=("zrec",), writes=("zrec",))
        for ic in range(8):
            for half in range(2):
                st("pe", (lambda e, ii=ic, hf=half: e.matmul(
                    (pbank[2] if hf == 0 else pbank[3])[0:8, 0:512],
                    ebTs[:, ii * 8:(ii + 1) * 8],
                    fwTs[:, ii * D + hf * 512: ii * D + hf * 512 + 512],
                    start=(ii == 0), stop=(ii == 7))),
                   reads=(("fwTs", ic), "ebTs"),
                   writes=(("pb2" if half == 0 else "pb3"),))
        for half in range(2):
            st("dve", (lambda e, hf=half: e.tensor_copy(
                ebf[0:8, hf * 512:hf * 512 + 512],
                (pbank[2] if hf == 0 else pbank[3])[0:8, 0:512])),
               reads=(("pb2" if half == 0 else "pb3"),), writes=("ebf",))
        for tl in range(8):
            t0 = tl * SL
            for ic in range(8):
                dmac("sp", aofm[:, ic * SL:(ic + 1) * SL],
                     ao_d[ic * 128:(ic + 1) * 128, t0:t0 + SL],
                     reads=tuple(("ao", h, tl) for h in range(AH)) + ("aofm",),
                     writes=("aofm",))
            for ic in range(8):
                st("pe", (lambda e, ii=ic, tt=t0: e.matmul(
                    pbank[5][:, 0:SL], headmap[:, ii * 128:(ii + 1) * 128],
                    zrec[0:16, tt:tt + SL],
                    start=True, stop=True)),
                   reads=("zrec", "headmap"), writes=("pb5",))
                st("dve", (lambda e, ii=ic: e.tensor_mul(
                    aofm[:, ii * SL:(ii + 1) * SL],
                    aofm[:, ii * SL:(ii + 1) * SL], pbank[5][:, 0:SL])),
                   reads=("aofm", "pb5"), writes=("aofm",))
            for u in range(4):
                uo = u * 128
                for ic in range(8):
                    st("pe", (lambda e, ii=ic, oo=uo: e.matmul(
                        pbank[4][:, 0:8],
                        aofm[:, ii * SL + oo: ii * SL + oo + 128],
                        gwTs[:, ii * 8:(ii + 1) * 8],
                        start=(ii == 0), stop=(ii == 7))),
                       reads=("aofm", "gwTs"), writes=("pb4",))
                st("dve", lambda e: e.tensor_add(gatesf[:], pbank[4][:, 0:8],
                                                 gbb[:]),
                   reads=("pb4", "gbb"), writes=("gatesf",))
                st("dve", lambda e: e.tensor_reduce(
                    gcol[:], gatesf[:], AX.X, ALU.max, negate=True),
                   reads=("gatesf",), writes=("gcol",))
                st("act", lambda e: e.activation(
                    out=gtmp[:], in_=gatesf[:], func=AF.Exp, bias=gcol[:]),
                   reads=("gatesf", "gcol"), writes=("gtmp",))
                st("dve", lambda e: e.tensor_reduce(
                    gcol2[:], gtmp[:], AX.X, ALU.add),
                   reads=("gtmp",), writes=("gcol2",))
                st("dve", lambda e: e.reciprocal(gcol2[:], gcol2[:]),
                   reads=("gcol2",), writes=("gcol2",))
                st("dve", lambda e: e.tensor_scalar(
                    gatesf[:], gtmp[:], gcol2[:], None, ALU.mult),
                   reads=("gtmp", "gcol2"), writes=("gatesf",))
                st("dve", lambda e: e.tensor_reduce(
                    gcol[:], gatesf[:], AX.X, ALU.max),
                   reads=("gatesf",), writes=("gcol",))
                st("dve", lambda e: e.tensor_scalar(
                    gmsk[:], gatesf[:], gcol[:], None, ALU.is_ge),
                   reads=("gatesf", "gcol"), writes=("gmsk",))
                st("dve", lambda e: e.scalar_tensor_tensor(
                    gtmp[:], gmsk[:], -1e30, gatesf[:], ALU.mult, ALU.add),
                   reads=("gmsk", "gatesf"), writes=("gtmp",))
                st("dve", lambda e: e.tensor_reduce(
                    gcol2[:], gtmp[:], AX.X, ALU.max),
                   reads=("gtmp",), writes=("gcol2",))
                st("dve", lambda e: e.tensor_scalar(
                    gmsk[:], gatesf[:], gcol2[:], None, ALU.is_ge),
                   reads=("gatesf", "gcol2"), writes=("gmsk",))
                st("dve", (lambda e, uu=u: e.tensor_mul(
                    wtok4[:, uu * 8:(uu + 1) * 8], gmsk[:], gatesf[:])),
                   reads=("gmsk", "gatesf"), writes=("wtok4",))
                st("pe", (lambda e, uu=u: e.transpose(
                    pbank[4][0:8, 0:128], wtok4[:, uu * 8:(uu + 1) * 8],
                    identf[:])),
                   reads=("wtok4", "identf"), writes=("pb4",))
                st("dve", (lambda e, uu=u: e.tensor_copy(
                    wfm4[0:8, uu * 128:(uu + 1) * 128], pbank[4][0:8, 0:128])),
                   reads=("pb4",), writes=("wfm4",))
            for n in range(NE):
                j = n % 3
                for ic in range(8):
                    dmac("sp", expd[j][:, ic * D:(ic + 1) * D],
                         ewb_d[n, ic * 128:(ic + 1) * 128, :],
                         reads=(("ewb", n), f"expd{j}"),
                         writes=(f"expd{j}",))
                for u in range(4):
                    uo = u * 128
                    for half in range(2):
                        bank = (2 * u + half) % 4
                        for ic in range(8):
                            st("pe", (lambda e, ii=ic, jj=j, hf=half,
                                      bb=bank, oo=uo: e.matmul(
                                          pbank[bb][:, :],
                                          aofm[:, ii * SL + oo: ii * SL + oo + 128],
                                          expd[jj][:, ii * D + hf * 512:
                                                   ii * D + hf * 512 + 512],
                                          start=(ii == 0), stop=(ii == 7))),
                               reads=("aofm", f"expd{j}"),
                               writes=(f"pb{bank}",))
                        if n == 0:
                            st("dve", (lambda e, hf=half, uu=u, bb=bank:
                                       e.tensor_scalar(
                                           moeacc[uu][:, hf * 512:hf * 512 + 512],
                                           pbank[bb][:, :],
                                           wtok4[:, uu * 8:uu * 8 + 1],
                                           None, ALU.mult)),
                               reads=(f"pb{bank}", "wtok4"),
                               writes=((f"moeacc{u}",),))
                        else:
                            st("dve", (lambda e, nn=n, hf=half, uu=u, bb=bank:
                                       e.scalar_tensor_tensor(
                                           moeacc[uu][:, hf * 512:hf * 512 + 512],
                                           pbank[bb][:, :],
                                           wtok4[:, uu * 8 + nn:uu * 8 + nn + 1],
                                           moeacc[uu][:, hf * 512:hf * 512 + 512],
                                           ALU.mult, ALU.add)),
                               reads=(f"pb{bank}", "wtok4", (f"moeacc{u}",)),
                               writes=((f"moeacc{u}",),))
            for u in range(4):
                po = u % 2
                st("dve", (lambda e, uu=u, pp_=po: e.tensor_copy(
                    moebf[pp_][:], moeacc[uu][:])),
                   reads=((f"moeacc{u}",), (f"moebf{po}",)),
                   writes=((f"moebf{po}",),))
                for oc in range(8):
                    bank = oc % 2
                    st("pe", (lambda e, oo=oc, bb=bank, pp_=po: e.transpose(
                        pbtr[bb][:, 0:128], moebf[pp_][:, oo * 128:(oo + 1) * 128],
                        identb[:])),
                       reads=((f"moebf{po}",), "identb"), writes=(f"pt{bank}",))
                    st("dve", (lambda e, oo=oc, bb=bank, pp_=po: e.tensor_copy(
                        moefm[pp_][:, oo * 128:(oo + 1) * 128],
                        pbtr[bb][:, 0:128])),
                       reads=(f"pt{bank}",), writes=((f"moefm{po}",),))
                for half in range(2):
                    bank = 4 + half
                    for oc in range(8):
                        st("pe", (lambda e, oo=oc, hf=half, bb=bank, pp_=po:
                                  e.matmul(
                                      pbank[bb][:, :],
                                      moefm[pp_][:, oo * 128:(oo + 1) * 128],
                                      fwTs[:, oo * D + hf * 512:
                                           oo * D + hf * 512 + 512],
                                      start=(oo == 0), stop=False)),
                           reads=((f"moefm{po}",), ("fwTs", oc)),
                           writes=(f"pb{bank}",))
                    st("pe", (lambda e, hf=half, bb=bank, uu=u: e.matmul(
                        pbank[bb][:, :], wfm4[0:8, uu * 128:(uu + 1) * 128],
                        ebf[0:8, hf * 512:hf * 512 + 512],
                        start=False, stop=True)),
                       reads=("wfm4", "ebf"), writes=(f"pb{bank}",))
                ti = tl * 4 + u
                dmac("sp", baset[:], base_d[ti * 128:(ti + 1) * 128, :],
                     reads=(("based", ti), "baset"),
                     writes=("baset",))
                ot = outt[po]
                for half in range(2):
                    st("dve", (lambda e, hf=half, o_=ot: e.tensor_add(
                        o_[:, hf * 512:hf * 512 + 512], pbank[4 + hf][:, :],
                        fbb[:, hf * 512:hf * 512 + 512])),
                       reads=(f"pb{4 + half}", "fbb", (f"outt{po}",)),
                       writes=((f"outt{po}",),))
                    st("dve", (lambda e, hf=half, o_=ot: e.tensor_add(
                        o_[:, hf * 512:hf * 512 + 512],
                        o_[:, hf * 512:hf * 512 + 512],
                        baset[:, hf * 512:hf * 512 + 512])),
                       reads=((f"outt{po}",), "baset"),
                       writes=((f"outt{po}",),))
                dmac("gp", out_h[ti * 128:(ti + 1) * 128, :], ot[:],
                     reads=((f"outt{po}",),),
                     writes=((f"outt{po}",), ("outw", ti)))

    loads_init()
    for n_ in range(NE):
        dmac("gp", ewb_d[n_, :, :], ewT_h[n_, :, :],
             reads=(), writes=(("ewb", n_),))
    if "p0" in phases:
        phase0()
    P.reset_region("RB")
    P.reset_region("RC")
    g1a = {}
    g1a["winsb"] = palloc("winsb", (128, 8 * 3 * D), BF16, OFF["winsb"], "RB")
    g1a["wogsb"] = palloc("wogsb", (128, 8 * D), BF16, OFF["wogsb"], "RC")
    g1a["praw"] = [palloc(f"praw{i}", (128, SL + 4), BF16, OFF[f"praw{i}"], "RC")
                   for i in range(4)]
    convt = palloc("convt", (128, SL), BF16, OFF["convt"], "RC")
    convt2 = palloc("convt2", (128, SL), BF16, OFF["convt"] + 1, "RC")
    g1a["ogt"] = [palloc(f"ogt{i}", (128, SL), F32, OFF[f"ogt{i}"], "RC")
                  for i in range(2)]
    if "p1a" in phases:
        phase1a(g1a)
    P.reset_region("RB")
    P.reset_region("RC")
    g1b = {}
    g1b["qnsl"] = palloc("qnsl", (128, 8 * SL), BF16, OFF["qnsl"], "RB")
    g1b["knsl"] = palloc("knsl", (128, 8 * SL), BF16, OFF["knsl"], "RB")
    g1b["vnsl"] = palloc("vnsl", (128, 8 * SL), BF16, OFF["vnsl"], "RB")
    g1b["woutsb"] = palloc("woutsb", (128, 8 * D), BF16, OFF["woutsb"], "RB")
    g1b["sqt"] = sqt_p
    g1b["rrow"] = rrow_p
    cht4 = chunk_tiles4()
    readn4_t, readfm4_t, dnbase_t = rd_tiles4()
    if "p1b" in phases:
        phase1b(g1b)
    P.reset_region("RB")
    P.reset_region("RC")
    g2p = {}
    g2p["qwTs"] = palloc("qwTs", (128, 8 * D), BF16, OFF["qwTs"], "RB")
    g2p["kwTs"] = palloc("kwTs", (128, 8 * D), BF16, OFF["kwTs"], "RB")
    g2p["vwTs"] = palloc("vwTs", (128, 8 * D), BF16, OFF["vwTs"], "RB")
    g2p["copt"] = palloc("copt", (128, S), F32, OFF["copt"], "RC")
    g2p["sipt"] = palloc("sipt", (128, S), F32, OFF["sipt"], "RC")
    g2p["qpt"] = [palloc(f"qpt{i}", (128, SL), BF16, OFF[f"qpt{i}"], "RC")
                  for i in range(2)]
    g2p["ropet1"] = palloc("ropet1", (128, SL), F32, OFF["ropet1"], "RC")
    g2p["ropet2"] = palloc("ropet2", (128, SL), F32, OFF["ropet2"], "RC")
    g2p["ropeo"] = palloc("ropeo", (128, SL), BF16, OFF["ropeo"], "RC")
    g2p["vtokt"] = palloc("vtokt", (128, D), BF16, OFF["vtokt"], "RC")
    g2p["sqa"] = palloc("sqa", (128, SL), BF16, OFF["sqa"], "RC")
    if "p2p" in phases:
        phase2proj(g2p)
    P.reset_region("RB")
    P.reset_region("RC")
    g2a = {}
    g2a["khd"] = palloc("khd", (128, S), BF16, OFF["khd"], "RB")
    g2a["qhd"] = palloc("qhd", (128, S), BF16, OFF["qhd"], "RB")
    g2a["vslab"] = palloc("vslab", (128, 32 * 128), BF16, OFF["vslab"], "RB")
    g2a["ptl"] = [palloc(f"ptl{i}", (128, 1024), BF16, OFF[f"ptl{i}"], "RB")
                  for i in range(2)]
    g2a["zsb"] = palloc("zsb", (1, S), F32, OFF["zbs"], "RB")
    g2a["aot"] = palloc("aot", (64, SL), BF16, OFF["aot"], "RB")
    g2a["negmh"] = palloc("negmh", (128, 1), F32, OFF["negmh"], "RB")
    g2a["mqcol"] = palloc("mqcol", (128, 1), F32, OFF["mqcol"], "RB")
    g2a["mkcol"] = palloc("mkcol", (128, 1), F32, OFF["mkcol"], "RB")
    if "p2a" in phases:
        phase2attn(g2a)
    P.reset_region("RA")
    P.reset_region("RB")
    P.reset_region("RC")
    g3 = {}
    g3["expd"] = [palloc("expd0", (128, 8 * D), BF16, OFF["expd0"], "RA"),
                  palloc("expd1", (128, 8 * D), BF16, OFF["expd1"], "RA"),
                  palloc("expd2", (128, 8 * D), BF16, 124, "RC")]
    g3["moeacc"] = [palloc(f"moeacc{i}", (128, D), F32, OFF["moeacc"] + 4 * i,
                           "RA") for i in range(4)]
    g3["moebf"] = [palloc(f"moebf{i}", (128, D), BF16, OFF["moebf"] + 2 * i,
                          "RA") for i in range(2)]
    g3["moefm"] = [palloc(f"moefm{i}", (128, D), BF16, OFF["moefm"] + 2 * i,
                          "RA") for i in range(2)]
    g3["baset"] = palloc("baset", (128, D), F32, OFF["baset"], "RA")
    g3["outt"] = [palloc("outt0", (128, D), F32, OFF["outt0"], "RA"),
                  palloc("outt1", (128, D), F32, OFF["outt1"], "RC")]
    g3["gatesf"] = palloc("gatesf", (128, 8), F32, OFF["gatesf"], "RC")
    g3["gtmp"] = palloc("gtmp", (128, 8), F32, OFF["gtmp"], "RC")
    g3["gmsk"] = palloc("gmsk", (128, 8), F32, OFF["gmsk"], "RC")
    g3["gcol"] = palloc("gcol", (128, 1), F32, OFF["gcol"], "RC")
    g3["gcol2"] = palloc("gcol2", (128, 1), F32, OFF["gcol2"], "RC")
    g3["wtok4"] = palloc("wtok4", (128, 32), F32, OFF["wtok4"], "RC")
    g3["wfm4"] = palloc("wfm4", (8, 512), BF16, OFF["wfm4"], "RC")
    g3["aofm"] = palloc("aofm", (128, 8 * SL), BF16, OFF["aofm"], "RB")
    g3["zrec"] = palloc("zrec", (16, S), F32, OFF["zrec"], "RB")
    g3["fwTs"] = palloc("fwTs", (128, 8 * D), BF16, OFF["fwTs"], "RB")
    if "p3" in phases:
        phase3(g3)

    if "p3" not in phases:
        for i in range(NT):
            p = i % 2
            dmac("gp", out_h[i * 128:(i + 1) * 128, :], xn32t[p][:],
                 reads=(f"xn32t{p}",), writes=(("outw", i),))

    for name, h in dbg_outs.items():
        src = {"xn32": xn32_d, "qc": qc_d, "kc": kc_d, "vc": vc_d,
               "oglog": oglog_d, "base": base_d, "qr": qr_d, "kr": kr_d,
               "vt": vt_d, "ao": ao_d, "zs": zs_d, "readn": readn_d}.get(name)
        if src is not None:
            reads = {"xn32": tuple(("xn32d", i) for i in range(NT)),
                     "base": tuple(("based", i) for i in range(NT)),
                     "qc": tuple(("qc", oc, s2) for oc in range(8) for s2 in range(NSL)),
                     "kc": tuple(("kc", oc, s2) for oc in range(8) for s2 in range(NSL)),
                     "vc": tuple(("vc", oc, s2) for oc in range(8) for s2 in range(NSL)),
                     "oglog": tuple(("oglog", oc, s2) for oc in range(8) for s2 in range(NSL)),
                     "qr": tuple(("qr", oc, s2) for oc in range(8) for s2 in range(NSL)),
                     "kr": tuple(("kr", oc, s2) for oc in range(8) for s2 in range(NSL)),
                     "vt": tuple(("vt", i) for i in range(NT)),
                     "ao": tuple(("ao", hh, qb) for hh in range(AH) for qb in range(8)),
                     "zs": tuple(("zs", hh) for hh in range(AH)),
                     "readn": tuple(("readnd", i) for i in range(NT)),
                     }[name]
            dmac("gp", h, src, reads=reads, writes=((name, "dbgw"),))
        elif name == "absb":
            dmac("gp", h, absb[:], reads=tuple(("absb", i) for i in range(NT)),
                 writes=(("absb", "dbgw"),))

    counters = P.finalize()
    per_engine = {e: [] for e in ENGINES}
    for s_ in P.steps:
        per_engine[s_["engine"]].append(s_)

    with (
        nc.semaphore("sem_pe") as sem_pe,
        nc.semaphore("sem_act") as sem_act,
        nc.semaphore("sem_dve") as sem_dve,
        nc.semaphore("sem_gp") as sem_gp,
        nc.semaphore("sem_sp") as sem_sp,
        nc.semaphore("sem_gp_dma") as sem_gp_dma,
        nc.semaphore("sem_sp_dma") as sem_sp_dma,
        nc.Block() as block,
    ):
        sems = {"pe": sem_pe, "act": sem_act, "dve": sem_dve, "gp": sem_gp,
                "sp": sem_sp, "gp_dma": sem_gp_dma, "sp_dma": sem_sp_dma}

        def run_engine(eng_obj, steps):
            for s_ in steps:
                for key, val in s_["waits"]:
                    eng_obj.wait_ge(sems[key], val)
                inst = s_["emit"](eng_obj)
                if s_["need_sem"]:
                    inst.then_inc(sems[s_["semkey"]], 16 if s_["dma"] else 1)

        @block.tensor
        def _(t):
            run_engine(t, per_engine["pe"])

        @block.scalar
        def _(a):
            run_engine(a, per_engine["act"])

        @block.vector
        def _(v):
            run_engine(v, per_engine["dve"])

        @block.gpsimd
        def _(gg):
            run_engine(gg, per_engine["gp"])
            for key in ("gp_dma", "sp_dma"):
                if counters.get(key):
                    gg.wait_ge(sems[key], counters[key])

        @block.sync
        def _(sy):
            run_engine(sy, per_engine["sp"])

    return nc, list(dbg_outs.keys())


def _prep_inputs(ins):
    f32 = np.float32

    def colchunk(v):
        return np.ascontiguousarray(np.asarray(v, f32).reshape(8, 128).T)

    base = {
        "win": np.ascontiguousarray(np.asarray(ins["L0_attn_W_in"], f32)),
        "wog": np.ascontiguousarray(np.asarray(ins["L0_attn_W_ctrl"], f32)[:, 8:]),
        "wab": np.ascontiguousarray(
            np.asarray(ins["L0_attn_W_ctrl"], f32)[:, :8]
            .reshape(8, 128, 8).transpose(1, 0, 2).reshape(128, 64)),
        "bog": colchunk(np.asarray(ins["L0_attn_b_ctrl"])[8:]),
        "bab": np.asarray(ins["L0_attn_b_ctrl"], f32)[:8].reshape(1, 8),
        "wconv": np.ascontiguousarray(np.stack([
            np.asarray(ins["L0_attn_w_q"], f32).reshape(8, 128, 4).transpose(1, 0, 2),
            np.asarray(ins["L0_attn_w_k"], f32).reshape(8, 128, 4).transpose(1, 0, 2),
            np.asarray(ins["L0_attn_w_v"], f32).reshape(8, 128, 4).transpose(1, 0, 2),
        ], axis=1).reshape(128, 96)),
        "wout": np.ascontiguousarray(np.asarray(ins["L0_attn_W_out"], f32)),
        "bout": colchunk(ins["L0_attn_b_out"]),
        "ramp": colchunk(ins["L0_attn_ramp_logit"]),
        "qwT": np.ascontiguousarray(np.asarray(ins["L0_ffn_q_w"], f32).T),
        "kwT": np.ascontiguousarray(np.asarray(ins["L0_ffn_k_w"], f32).T),
        "vwT": np.ascontiguousarray(np.asarray(ins["L0_ffn_v_w"], f32).T),
        "qb": colchunk(ins["L0_ffn_q_b"]),
        "kb": colchunk(ins["L0_ffn_k_b"]),
        "vbrow": np.asarray(ins["L0_ffn_v_b"], f32).reshape(1, D),
        "gwT": np.ascontiguousarray(
            np.asarray(ins["L0_ffn_gate_w"], f32).T
            .reshape(8, 128, 8).transpose(1, 0, 2).reshape(128, 64)),
        "gbrow": np.asarray(ins["L0_ffn_gate_b"], f32).reshape(1, 8),
        "ewT": np.ascontiguousarray(
            np.asarray(ins["L0_ffn_expert_w"], f32).transpose(0, 2, 1)),
        "ebT": np.ascontiguousarray(
            np.asarray(ins["L0_ffn_expert_b"], f32).T
            .reshape(8, 128, 8).transpose(1, 0, 2).reshape(128, 64)),
        "fwT": np.ascontiguousarray(np.asarray(ins["L0_ffn_ffn_w"], f32).T),
        "fbrow": np.asarray(ins["L0_ffn_ffn_b"], f32).reshape(1, D),
    }
    for k, v in CONSTS.items():
        base[k] = v
    x = np.asarray(ins["L0_attn_x"], np.float32)
    return [dict(base, x=np.ascontiguousarray(x[b])) for b in range(8)]


def kernel(**inputs):
    in_maps = _prep_inputs(inputs)
    nc, _dbg = build_program()
    res = run_bass_kernel_spmd(nc, in_maps, core_ids=list(range(8)))
    return np.stack([r["out"] for r in res.results], axis=0).astype(np.float32)
